# version 26
# baseline (speedup 1.0000x reference)
import contextlib

import numpy as np
import ml_dtypes

import concourse.bass as bass
import concourse.mybir as mybir
from concourse.bass_utils import run_bass_kernel_spmd

F32 = mybir.dt.float32
BF16 = mybir.dt.bfloat16
AF = mybir.ActivationFunctionType
ALU = mybir.AluOpType
AX = mybir.AxisListType

D = 2048
NH = 16
NKV = 4
HD = 128
FF = 8192
TQ = 512
NC16 = D // 128
SQRT_D = 45.254833995939045

C_M40 = -40.0
C_EPS5 = 1e-5
C_EPS6 = 1e-6
C_NHALF = -0.5
C_SQD = SQRT_D
C_M1 = -1.0
C_P1 = 1.0
REG_CONSTS = [C_M40, C_EPS5, C_EPS6, C_NHALF, C_SQD, C_M1, C_P1]

L_R1, L_SUM, L_SQ, L_MU, L_VAR, L_RSTD, L_NMR, L_TMP = 0, 4, 5, 6, 7, 8, 9, 10

TC_TH = 0
TC_T = 16
TC_RSTD = 48
TC_NMR = 49
TC_SUM = 52
TC_SQ = 53
TC_MU = 54
TC_VAR = 55
TC_RS = 56
TC_NM = 57
TC_TMP = 58


class Sem:
    __slots__ = ("h", "n", "name")

    def __init__(self, stack, nc, name):
        self.h = stack.enter_context(nc.semaphore(name))
        self.n = 0
        self.name = name


class Prog:
    ENGS = ("tensor", "vector", "scalar", "sync", "gpsimd")

    def __init__(self):
        self.q = {e: [] for e in self.ENGS}
        self.hwm = {e: {} for e in self.ENGS}

    def emit(self, eng, fn):
        self.q[eng].append(fn)

    def wait(self, eng, sem, val):
        if val <= 0:
            return
        if self.hwm[eng].get(sem.name, 0) >= val:
            return
        self.hwm[eng][sem.name] = val
        h = sem.h
        self.q[eng].append(lambda e: e.wait_ge(h, val))


def build(T, debug=False):
    assert T % TQ == 0
    NT = T // TQ
    NSUB = T // 128

    nc = bass.Bass("TRN2", target_bir_lowering=False,
                   detect_race_conditions=False)

    def inp(name, shape, dt=F32):
        return nc.dram_tensor(name, list(shape), dt, kind="ExternalInput")

    x_h = inp("x", [T, D])
    wqkv_h = inp("wqkv", [D, 3072])
    bqkv_h = inp("bqkv", [1, 3072])
    wo_h = inp("wo", [D, D])
    bo_h = inp("bo", [1, D])
    gw_h = inp("gate_w", [D, D])
    gb_h = inp("gate_b", [1, D])
    aggw_h = inp("agg_w", [D, D])
    bcw_h = inp("bc_w", [D, D])
    d1s_h = inp("d1_slice", [8, 128, D])
    d2s_h = inp("d2_slice", [FF // 8, D])
    d1bT_h = inp("d1_bT", [128, 8])
    bsel_h = inp("bsel", [8])
    aggbT_h = inp("agg_bT", [128, NC16])
    d2bT_h = inp("d2_bT", [128, NC16])
    bcbT_h = inp("bc_bT", [128, NC16])
    ingT_h = inp("in_gT", [128, NC16])
    inbT_h = inp("in_bT", [128, NC16])
    pngT_h = inp("pn_gT", [128, NC16])
    pnbT_h = inp("pn_bT", [128, NC16])
    outg_h = inp("out_g", [D])
    outb_h = inp("out_b", [D])
    cos_h = inp("cos_t", [T, 64])
    sin_h = inp("sin_t", [T, 64])
    trimask_h = inp("trimask", [128, 128], BF16)
    ones1_h = inp("ones1", [1, 128], BF16)
    ones2d_h = inp("ones2d", [128, 128], BF16)
    onescol_h = inp("onescol", [128, 1])
    onesrow_h = inp("onesrow", [1, 128])
    ident_h = inp("ident_bf", [128, 128], BF16)
    identf_h = inp("ident_f32", [128, 128])
    ones2df_h = inp("ones2df", [128, 128])
    consts_h = inp("consts", [len(REG_CONSTS)])

    out_h = nc.dram_tensor("out", [T, D], F32, kind="ExternalOutput")

    def scratch(name, shape, dt):
        kind = "ExternalOutput" if debug else "Internal"
        return nc.dram_tensor(name, list(shape), dt, kind=kind)

    agin_h = nc.dram_tensor("cc_agin", [1, D], F32)
    agout_h = nc.dram_tensor("cc_agout", [8, D], F32)
    dbg_th_h = nc.dram_tensor("dbg_th", [8, 128, 128], F32,
                              kind="ExternalOutput")
    arin_h = [nc.dram_tensor(f"cc_arin{k}", [128, 128], F32) for k in range(4)]
    arout_h = [nc.dram_tensor(f"cc_arout{k}", [128, 128], F32) for k in range(4)]
    q_h = scratch("q_s", [T, D], BF16)
    k_h = scratch("k_s", [T, NKV * HD], BF16)
    v_h = scratch("v_s", [NKV, T // 128, 128, 128], BF16)
    x1_h = scratch("x1_s", [T, D], F32)
    x1T_h = scratch("x1T_s", [T // 128, 128, D], BF16)
    bcrow_h = scratch("bc_row", [D], F32)

    P = Prog()
    stk = contextlib.ExitStack()
    nc._keepalive_stack = stk

    def S(name):
        return Sem(stk, nc, name)

    d_csA = S("d_csA")
    d_csB = S("d_csB")
    d_csC = S("d_csC")
    d_csD = S("d_csD")
    d_xf = S("d_xf")
    e_XT = S("e_XT")
    e_XP = S("e_XP")
    d_wq = S("d_wq")
    d_arb = S("d_arb")
    e_CC = S("e_CC")
    d_st = S("d_st")
    d_wo = S("d_wo")
    d_qt = [S(f"d_qt{i}") for i in range(NH)]
    d_kv = [S("d_kv0"), S("d_kv1")]
    d_x = S("d_x")
    d_x1F = S("d_x1F")
    d_x1R = S("d_x1R")
    d_x1s = S("d_x1s")
    e_X1T = S("e_X1T")
    e_X1E = S("e_X1E")
    d_thr = [S(f"d_thr{i}") for i in range(4)]
    d_bias = S("d_bias")
    d_x1row = S("d_x1row")
    d_bcr = S("d_bcr")
    d_bcb = S("d_bcb")
    d_gw = S("d_gw")
    d_x1t = [S("d_x1t0"), S("d_x1t1")]
    d_x1l = [S("d_x1l0"), S("d_x1l1")]
    d_out = S("d_out")

    e_p1 = S("e_p1")
    e_evA = S("e_evA")
    e_evB = S("e_evB")
    e_S = S("e_S")
    e_E = S("e_E")
    e_M = S("e_M")
    e_AV = S("e_AV")
    e_SM = S("e_SM")
    e_ACv = S("e_ACv")
    e_ACg = S("e_ACg")
    e_QH = S("e_QH")
    e_KV = S("e_KV")
    e_OE = S("e_OE")
    e_SE = S("e_SE")
    e_OP = S("e_OP")
    e_OPE = S("e_OPE")
    e_LN = S("e_LN")
    e_TH = S("e_TH")
    e_G = S("e_G")
    e_GE = S("e_GE")
    e_Fp = [S("e_F0"), S("e_F1")]

    consts_sb = nc.alloc_sbuf_tensor("consts_sb", [128, 8], F32)
    cM40 = consts_sb[:, 0:1]
    cEPS5 = consts_sb[:, 1:2]
    cEPS6 = consts_sb[:, 2:3]
    cNH = consts_sb[:, 3:4]
    cSQD = consts_sb[:, 4:5]
    cM1s = consts_sb[:, 5:6]
    cP1 = consts_sb[:, 6:7]
    cM40r = consts_sb[0:1, 0:1]
    cEPS5r = consts_sb[0:1, 1:2]
    cEPS6r = consts_sb[0:1, 2:3]
    cNHr = consts_sb[0:1, 3:4]
    bo_sb = nc.alloc_sbuf_tensor("bo_sb", [1, D], BF16)
    ones1 = nc.alloc_sbuf_tensor("ones1_sb", [1, 128], BF16)
    ones2d = nc.alloc_sbuf_tensor("ones2d_sb", [128, 128], BF16)
    onescol = nc.alloc_sbuf_tensor("onescol_sb", [128, 1], F32)
    onesrow = nc.alloc_sbuf_tensor("onesrow_sb", [1, 128], F32)
    trimask = nc.alloc_sbuf_tensor("trimask_sb", [128, 128], BF16)
    ident_bf = nc.alloc_sbuf_tensor("ident_bf_sb", [128, 128], BF16)
    ident_f32 = nc.alloc_sbuf_tensor("ident_f32_sb", [128, 128], F32)
    ones2df = nc.alloc_sbuf_tensor("ones2df_sb", [128, 128], F32)
    lnv = nc.alloc_sbuf_tensor("lnv", [128, 32], F32)

    def ld(dst_ap, src_ap, sem=d_csA, q="sync"):
        sem.n += 16
        h = sem.h
        P.emit(q, lambda e, d=dst_ap, s=src_ap: e.dma_start(d, s).then_inc(h, 16))

    ld(consts_sb[:, 0 : len(REG_CONSTS)],
       consts_h.ap().unsqueeze(0).partition_broadcast(128))
    ld(ones1[:], ones1_h.ap())
    ld(ones2d[:], ones2d_h.ap())
    ld(onescol[:], onescol_h.ap())
    ld(onesrow[:], onesrow_h.ap())
    ld(trimask[:], trimask_h.ap())
    ld(ident_bf[:], ident_h.ap())
    ld(ident_f32[:], identf_h.ap())
    ld(ones2df[:], ones2df_h.ap())
    ld(bo_sb[:], bo_h.ap(), sem=d_bias, q="gpsimd")
    CONST_LDS = d_csA.n
    for eng in ("scalar", "vector", "tensor"):
        P.wait(eng, d_csA, CONST_LDS)

    p1_stack = contextlib.ExitStack()
    wqkv_sb = p1_stack.enter_context(
        nc.sbuf_tensor("wqkv_sb", [128, NC16 * 3072], BF16))
    xt_ring = p1_stack.enter_context(
        nc.sbuf_tensor("xt_ring", [128, 2 * NC16 * TQ], BF16))
    xf_ring = p1_stack.enter_context(nc.sbuf_tensor("xf_ring", [128, 2 * D], F32))
    qkv_out = p1_stack.enter_context(
        nc.sbuf_tensor("qkv_out", [128, 2 * 3072], BF16))
    cos_sb = p1_stack.enter_context(nc.sbuf_tensor("cos_sb", [128, NSUB * 64], F32))
    sin_sb = p1_stack.enter_context(nc.sbuf_tensor("sin_sb", [128, NSUB * 64], F32))
    rope_t = p1_stack.enter_context(nc.sbuf_tensor("rope_t", [128, 4 * 16 * 64], F32))
    psA = p1_stack.enter_context(nc.psum_tensor("psA", [128, 1536], F32))
    psB = p1_stack.enter_context(nc.psum_tensor("psB", [128, 1536], F32))
    xp_ps = p1_stack.enter_context(nc.psum_tensor("xp_ps", [128, 1024], F32))
    bqkv_sb = p1_stack.enter_context(nc.sbuf_tensor("bqkv_sb", [1, 3072], BF16))
    ld(bqkv_sb[:], bqkv_h.ap(), sem=d_bias, q="gpsimd")
    P.wait("tensor", d_bias, d_bias.n)

    ld(cos_sb[:], cos_h.ap().rearrange("(g p) j -> p g j", p=128), sem=d_csB)
    ld(sin_sb[:], sin_h.ap().rearrange("(g p) j -> p g j", p=128), sem=d_csB)
    P.wait("vector", d_csB, d_csB.n)

    wq_src = wqkv_h.ap().rearrange("(c p) f -> p c f", p=128)
    wq_dst = wqkv_sb.ap().rearrange("p (c f) -> p c f", f=3072)
    for c in range(NC16):
        d_wq.n += 16
        h = d_wq.h
        P.emit("gpsimd", lambda e, c=c, h=h: e.dma_start(
            wq_dst[:, c, :], wq_src[:, c, :]).then_inc(h, 16))

    def xt_slot(t, c):
        off = ((t % 2) * NC16 + c) * TQ
        return xt_ring[:, off : off + TQ]

    def xf_slot(g):
        return xf_ring[:, (g % 2) * D : (g % 2) * D + D]

    def emit_xload(g):
        if g >= 2:
            P.wait("sync", e_XT, 4 * (g - 1))
        d_xf.n += 16
        h = d_xf.h
        P.emit("sync", lambda e, g=g, h=h: e.dma_start(
            xf_slot(g), x_h.ap()[g * 128 : (g + 1) * 128, :]).then_inc(h, 16))

    def emit_xpose(t, s):
        g = t * 4 + s
        P.wait("tensor", d_xf, 16 * (g + 1))
        if t >= 2:
            P.wait("tensor", e_p1, (t - 1) * 8)
        for cg in range(4):
            gi4 = g * 4 + cg
            par = gi4 % 2
            if gi4 >= 2:
                P.wait("tensor", e_XP, gi4 - 1)
            hXT = e_XT.h
            for j in range(4):
                def tp(e, c=cg * 4 + j, par=par, j=j, g=g, last=(j == 3)):
                    ins = e.transpose(
                        xp_ps.ap()[:, par * 512 + j * 128 : par * 512 + (j + 1) * 128],
                        xf_slot(g)[:, c * 128 : (c + 1) * 128],
                        ident_f32[:])
                    if last:
                        ins.then_inc(hXT, 1)
                P.emit("tensor", tp)
            e_XT.n += 1
            P.wait("scalar", e_XT, e_XT.n)
            if t >= 2:
                P.wait("scalar", e_p1, (t - 1) * 8)
            slot0 = (t % 2) * NC16 + cg * 4
            dst = xt_ring.ap().rearrange("p (sl q) -> p sl q", q=TQ)[
                :, slot0 : slot0 + 4, s * 128 : (s + 1) * 128]
            e_XP.n += 1
            hXP = e_XP.h
            P.emit("scalar", lambda e, dst=dst, par=par: e.copy(
                dst, xp_ps.ap()[:, par * 512 : par * 512 + 512].rearrange(
                    "p (c f) -> p c f", f=128)).then_inc(hXP, 1))

    emit_xload(0)
    emit_xload(1)

    for t in range(NT):
        for s in range(4):
            if t * 4 + s + 2 < NSUB:
                emit_xload(t * 4 + s + 2)
            emit_xpose(t, s)
        for s in range(4):
            g = t * 4 + s
            par = g % 2
            qo = qkv_out[:, par * 3072 : par * 3072 + 3072]
            P.wait("tensor", e_XP, 4 * (g + 1))
            for half, ps, e_ev in ((0, psA, e_evA), (1, psB, e_evB)):
                if half == 0:
                    P.wait("tensor", e_evA, g)
                else:
                    P.wait("tensor", e_evB, 2 * g)
                for c in range(NC16):
                    lhs = xt_slot(t, c)[:, s * 128 : (s + 1) * 128]
                    P.wait("tensor", d_wq, 16 * (c + 1))
                    for sl in range(3):
                        col0 = half * 1536 + sl * 512
                        P.emit("tensor", lambda e, lhs=lhs, ps=ps, sl=sl, c=c, col0=col0:
                               e.matmul(ps.ap()[:, sl * 512 : sl * 512 + 512],
                                        lhs,
                                        wqkv_sb[:, c * 3072 + col0 : c * 3072 + col0 + 512],
                                        start=(c == 0), stop=False))
                e_p1.n += 1
                hp1 = e_p1.h
                for sl in range(3):
                    col0 = half * 1536 + sl * 512
                    last = sl == 2
                    def mmb(e, ps=ps, sl=sl, col0=col0, last=last):
                        ins = e.matmul(ps.ap()[:, sl * 512 : sl * 512 + 512],
                                       ones1[:],
                                       bqkv_sb[:, col0 : col0 + 512],
                                       start=False, stop=True)
                        if last:
                            ins.then_inc(hp1, 1)
                    P.emit("tensor", mmb)

                P.wait("vector", e_p1, e_p1.n)
                if g >= 2:
                    P.wait("vector", d_st, 96 * (g - 1))
                cos_ap = cos_sb[:, g * 64 : g * 64 + 64]
                sin_ap = sin_sb[:, g * 64 : g * 64 + 64]

                def rope(nheads, ps_off, out_off, inc_sem, ps=ps, qo=qo,
                         cos_ap=cos_ap, sin_ap=sin_ap):
                    pr = ps.ap()[:, ps_off : ps_off + nheads * 128].rearrange(
                        "p (h c) -> p h c", c=128)
                    xr = pr[:, :, 0:64]
                    xi = pr[:, :, 64:128]
                    cb = cos_ap.unsqueeze(1).broadcast_to([128, nheads, 64])
                    sb_ = sin_ap.unsqueeze(1).broadcast_to([128, nheads, 64])
                    t1 = rope_t.ap()[:, 0 : nheads * 64].rearrange(
                        "p (h c) -> p h c", c=64)
                    t2 = rope_t.ap()[:, 16 * 64 : 16 * 64 + nheads * 64].rearrange(
                        "p (h c) -> p h c", c=64)
                    t3 = rope_t.ap()[:, 32 * 64 : 32 * 64 + nheads * 64].rearrange(
                        "p (h c) -> p h c", c=64)
                    t4 = rope_t.ap()[:, 48 * 64 : 48 * 64 + nheads * 64].rearrange(
                        "p (h c) -> p h c", c=64)
                    ov = qo[:, out_off : out_off + nheads * 128].rearrange(
                        "p (h c) -> p h c", c=128)
                    P.emit("vector", lambda e: e.tensor_tensor(t1, xr, cb, ALU.mult))
                    P.emit("vector", lambda e: e.tensor_tensor(t2, xi, sb_, ALU.mult))
                    P.emit("vector", lambda e: e.tensor_tensor(t3, xr, sb_, ALU.mult))
                    P.emit("vector", lambda e: e.tensor_tensor(t4, xi, cb, ALU.mult))
                    P.emit("vector", lambda e: e.tensor_tensor(
                        ov[:, :, 0:64], t1, t2, ALU.subtract))
                    if inc_sem is None:
                        P.emit("vector", lambda e: e.tensor_tensor(
                            ov[:, :, 64:128], t3, t4, ALU.add))
                    else:
                        hh = inc_sem
                        P.emit("vector", lambda e: e.tensor_tensor(
                            ov[:, :, 64:128], t3, t4, ALU.add).then_inc(hh, 1))

                if half == 0:
                    e_evA.n += 1
                    rope(12, 0, 0, e_evA.h)
                else:
                    rope(4, 0, 1536, None)
                    e_evB.n += 1
                    rope(4, 512, 2048, e_evB.h)
                    P.wait("scalar", e_p1, e_p1.n)
                    if g >= 2:
                        P.wait("scalar", d_st, 96 * (g - 1))
                    e_evB.n += 1
                    hevB = e_evB.h
                    P.emit("scalar", lambda e, ps=ps, qo=qo: e.copy(
                        qo[:, 2560:3072], ps.ap()[:, 1024:1536]).then_inc(hevB, 1))

            P.wait("sync", e_evA, g + 1)
            P.wait("sync", e_evB, 2 * (g + 1))
            hst = d_st.h
            d_st.n += 16
            P.emit("sync", lambda e, g=g, qo=qo: e.dma_start(
                q_h.ap()[g * 128 : (g + 1) * 128, :], qo[:, 0:2048]).then_inc(hst, 16))
            d_st.n += 16
            P.emit("sync", lambda e, g=g, qo=qo: e.dma_start(
                k_h.ap()[g * 128 : (g + 1) * 128, :], qo[:, 2048:2560]).then_inc(hst, 16))
            for kv in range(NKV):
                d_st.n += 16
                P.emit("sync", lambda e, g=g, kv=kv, qo=qo: e.dma_start(
                    v_h.ap()[kv, g],
                    qo[:, 2560 + kv * 128 : 2560 + (kv + 1) * 128]).then_inc(hst, 16))

    P1_MM = e_p1.n
    P1_ST = d_st.n
    P1_EVA = e_evA.n
    P1_EVB = e_evB.n
    p1_stack.close()

    p2_stack = contextlib.ExitStack()
    wo_sb = p2_stack.enter_context(nc.sbuf_tensor("wo_sb", [128, NC16 * D], BF16))
    qt_ring = p2_stack.enter_context(nc.sbuf_tensor("qt_ring", [128, NH * TQ], BF16))
    kt_buf = p2_stack.enter_context(nc.sbuf_tensor("kt_buf", [128, 2 * T], BF16))
    v_buf = p2_stack.enter_context(nc.sbuf_tensor("v_buf", [128, 2 * T], BF16))
    pt_buf = p2_stack.enter_context(nc.sbuf_tensor("pt_buf", [128, 2 * 1024], BF16))
    ot_sb = p2_stack.enter_context(nc.sbuf_tensor("ot_sb", [128, NH * TQ], BF16))
    x_sb = p2_stack.enter_context(nc.sbuf_tensor("x_sb", [128, D], F32))
    x1_sb = p2_stack.enter_context(nc.sbuf_tensor("x1_sb", [128, D], F32))
    recip_sb = p2_stack.enter_context(nc.sbuf_tensor("recip_sb", [128, TQ], F32))
    x1tb = p2_stack.enter_context(nc.sbuf_tensor("x1tb", [128, D], BF16))
    thw_ring = p2_stack.enter_context(nc.sbuf_tensor("thw_ring", [128, 2 * D], BF16))
    thstg = p2_stack.enter_context(nc.sbuf_tensor("thstg", [128, D], F32))
    th_cols = p2_stack.enter_context(nc.sbuf_tensor("th_cols", [128, 96], F32))
    th_prm = p2_stack.enter_context(nc.sbuf_tensor("th_prm", [128, 7 * NC16], F32))
    th_d1bT = p2_stack.enter_context(nc.sbuf_tensor("th_d1bT", [128, 8], F32))
    th_pad = p2_stack.enter_context(nc.sbuf_tensor("th_pad", [128, 128], F32))
    th_sb128 = p2_stack.enter_context(nc.sbuf_tensor("th_sb128", [128, 128], F32))
    th_sdump = p2_stack.enter_context(nc.sbuf_tensor("th_sdump", [128, 16], F32))
    tb_sb = p2_stack.enter_context(nc.sbuf_tensor("tb_sb", [128, 128], F32))
    tn_sb = p2_stack.enter_context(nc.sbuf_tensor("tn_sb", [128, 128], F32))
    tnb_sb = p2_stack.enter_context(nc.sbuf_tensor("tnb_sb", [128, 128], BF16))
    hb_sb = p2_stack.enter_context(nc.sbuf_tensor("hb_sb", [128, 64], BF16))
    xraw_sb = p2_stack.enter_context(nc.sbuf_tensor("xraw_sb", [128, 128], F32))
    xnb_sb = p2_stack.enter_context(nc.sbuf_tensor("xnb_sb", [128, 128], BF16))
    arb_sb = p2_stack.enter_context(nc.sbuf_tensor("arb_sb", [128, 128], F32))
    bsel_sb = p2_stack.enter_context(nc.sbuf_tensor("bsel_sb", [128, 8], F32))

    sp0 = p2_stack.enter_context(nc.psum_tensor("sp0", [128, 1024], F32))
    sp1 = p2_stack.enter_context(nc.psum_tensor("sp1", [128, 1024], F32))
    o_ps = p2_stack.enter_context(nc.psum_tensor("o_ps", [128, TQ], F32))
    sm_ps = p2_stack.enter_context(nc.psum_tensor("sm_ps", [128, TQ], F32))
    op_ps = p2_stack.enter_context(nc.psum_tensor("op_ps", [128, TQ], F32))
    th_ps = p2_stack.enter_context(nc.psum_tensor("th_ps", [128, 384], F32))

    P.wait("sync", e_p1, P1_MM)
    P.wait("sync", e_evA, P1_EVA)
    P.wait("sync", e_evB, P1_EVB)
    ld(th_prm[:, 0:16], ingT_h.ap(), sem=d_csC)
    ld(th_prm[:, 16:32], inbT_h.ap(), sem=d_csC)
    ld(th_prm[:, 32:48], pngT_h.ap(), sem=d_csC)
    ld(th_prm[:, 48:64], pnbT_h.ap(), sem=d_csC)
    ld(th_prm[:, 64:80], aggbT_h.ap(), sem=d_csC)
    ld(th_prm[:, 80:96], d2bT_h.ap(), sem=d_csC)
    ld(th_prm[:, 96:112], bcbT_h.ap(), sem=d_csC)
    ld(th_d1bT[:], d1bT_h.ap(), sem=d_csC)
    ld(bsel_sb[:], bsel_h.ap().unsqueeze(0).partition_broadcast(128), sem=d_csC)
    TH_PRM_LDS = d_csC.n

    P.wait("gpsimd", e_p1, P1_MM)
    P.wait("gpsimd", e_evA, P1_EVA)
    P.wait("gpsimd", e_evB, P1_EVB)
    wo_src = wo_h.ap().rearrange("(c p) f -> p c f", p=128)
    wo_dst = wo_sb.ap().rearrange("p (c f) -> p c f", f=D)
    for c in range(NC16):
        d_wo.n += 16
        h = d_wo.h
        P.emit("gpsimd", lambda e, c=c, h=h: e.dma_start(
            wo_dst[:, c, :], wo_src[:, c, :]).then_inc(h, 16))

    if debug:
        dbg_ot = nc.dram_tensor("dbg_ot", [NH, 128, TQ], BF16, kind="ExternalOutput")
        dbg_sums = nc.dram_tensor("dbg_sums", [NH, 128, TQ], F32, kind="ExternalOutput")
        dbg_recip = nc.dram_tensor("dbg_recip", [NH, 128, TQ], F32, kind="ExternalOutput")
        dbg_r1 = nc.dram_tensor("dbg_r1", [NSUB, 128, D], F32, kind="ExternalOutput")
        dbg_lnv = nc.dram_tensor("dbg_lnv", [NSUB, 128, 32], F32, kind="ExternalOutput")
        d_dbg = S("d_dbg")

    tile_order = [0, NT - 1] + list(range(1, NT - 1))

    NB = 8
    C_SU, C_SQ2, C_RS2, C_NM2 = 0, 8, 16, 24
    R_MU, R_VAR, R_RS, R_NM, R_TMP = 40, 48, 56, 64, 72

    def th_step(eng, fn, waits=()):
        for sem, val in waits:
            P.wait(eng, sem, val)
        e_TH.n += 1
        h = e_TH.h
        if eng == "tensor":
            P.emit(eng, lambda e, fn=fn, h=h: fn(e).then_inc(h, 1))
        else:
            def run(e, fn=fn, h=h):
                fn(e)
                e.drain().then_inc(h, 1)
            P.emit(eng, run)

    def th_ln(src, dst_bf, g_off, b_off, eps_c):
        src3 = src.ap().rearrange("p (c b) -> p c b", b=NB)
        tn3 = tn_sb.ap().rearrange("p (c b) -> p c b", b=NB)
        for b in range(NB):
            th_step("vector", lambda e, b=b: e.reduce_sum(
                th_cols[:, C_SU + b : C_SU + b + 1], src3[:, :, b], axis=AX.X)
                and e.drain(),
                waits=([(e_TH, e_TH.n)] if b == 0 else ()))
        for b in range(NB):
            th_step("scalar", lambda e, b=b: e.activation(
                th_sdump[:], src3[:, :, b], AF.Square, bias=0.0, scale=1.0,
                accum_out=th_cols[:, C_SQ2 + b : C_SQ2 + b + 1]) and e.drain(),
                waits=([(e_TH, e_TH.n)] if b == 0 else ()))
        th_step("tensor", lambda e: e.matmul(
            th_ps.ap()[0:1, 256:272], onescol[:], th_cols[:, 0:16],
            start=True, stop=True), waits=[(e_TH, e_TH.n)])
        th_step("vector", lambda e: e.tensor_scalar(
            th_cols[0:1, R_MU : R_MU + 16], th_ps.ap()[0:1, 256:272],
            1.0 / D, None, ALU.mult), waits=[(e_TH, e_TH.n)])
        th_step("vector", lambda e: e.tensor_tensor(
            th_cols[0:1, R_TMP : R_TMP + 8], th_cols[0:1, R_MU : R_MU + 8],
            th_cols[0:1, R_MU : R_MU + 8], ALU.mult))
        th_step("vector", lambda e: e.tensor_tensor(
            th_cols[0:1, R_VAR : R_VAR + 8], th_cols[0:1, R_VAR : R_VAR + 8],
            th_cols[0:1, R_TMP : R_TMP + 8], ALU.subtract))
        th_step("scalar", lambda e, eps_c=eps_c: e.activation(
            th_cols[0:1, R_RS : R_RS + 8], th_cols[0:1, R_VAR : R_VAR + 8],
            AF.Ln, bias=eps_c, scale=1.0), waits=[(e_TH, e_TH.n)])
        th_step("scalar", lambda e: e.activation(
            th_cols[0:1, R_RS : R_RS + 8], th_cols[0:1, R_RS : R_RS + 8],
            AF.Exp, bias=0.0, scale=cNHr))
        th_step("vector", lambda e: e.scalar_tensor_tensor(
            th_cols[0:1, R_NM : R_NM + 8], th_cols[0:1, R_MU : R_MU + 8],
            C_M1, th_cols[0:1, R_RS : R_RS + 8], ALU.mult, ALU.mult),
            waits=[(e_TH, e_TH.n)])
        th_step("tensor", lambda e: e.matmul(
            th_ps.ap()[:, 272:288], onesrow[:], th_cols[0:1, R_RS : R_RS + 16],
            start=True, stop=True), waits=[(e_TH, e_TH.n)])
        th_step("vector", lambda e: e.tensor_copy(
            th_cols[:, C_RS2 : C_RS2 + 16], th_ps.ap()[:, 272:288]),
            waits=[(e_TH, e_TH.n)])
        for b in range(NB):
            th_step("scalar", lambda e, b=b: e.activation(
                tn3[:, :, b], src3[:, :, b], AF.Identity,
                bias=th_cols[:, C_NM2 + b : C_NM2 + b + 1],
                scale=th_cols[:, C_RS2 + b : C_RS2 + b + 1]),
                waits=([(e_TH, e_TH.n)] if b == 0 else ()))
        for ci in range(NC16):
            th_step("scalar", lambda e, ci=ci, dst_bf=dst_bf: e.activation(
                dst_bf[:, ci * 8 : ci * 8 + 8], tn_sb[:, ci * 8 : ci * 8 + 8],
                AF.Identity, bias=th_prm[:, b_off + ci : b_off + ci + 1],
                scale=th_prm[:, g_off + ci : g_off + ci + 1]),
                waits=([(e_TH, e_TH.n)] if ci == 0 else ()))

    slab_i = [0]
    slab_war = {0: 0, 1: 0}

    def th_slab(src_fn, mm_fn, width=D, dst3d=None):
        slot = slab_i[0] % 2
        slab_i[0] += 1
        sem = d_thr[slot]
        P.wait("gpsimd", e_TH, slab_war[slot])
        sem.n += 16
        h = sem.h
        dst = thw_ring[:, slot * D : slot * D + width]
        dd = dst if dst3d is None else dst.rearrange("p (c f) -> p c f",
                                                     f=dst3d)
        P.emit("gpsimd", lambda e, dd=dd, src=src_fn(), h=h:
               e.dma_start(dd, src).then_inc(h, 16))
        P.wait("tensor", sem, sem.n)
        P.wait("tensor", e_TH, e_TH.n)
        mm_fn(dst)
        slab_war[slot] = e_TH.n

    def f_ag():
        for eng in ("vector", "scalar", "tensor"):
            P.wait(eng, d_csC, TH_PRM_LDS)
        P.wait("gpsimd", d_x1F, 16 * 4)
        P.wait("gpsimd", d_x1R, 16 * 4)
        d_arb.n += 16
        ha0 = d_arb.h
        P.emit("gpsimd", lambda e, ha0=ha0: e.dma_start(
            agin_h.ap(), x1_h.ap()[T - 1 : T, :]).then_inc(ha0, 16))
        P.wait("gpsimd", d_arb, d_arb.n)
        e_CC.n += 1
        hc = e_CC.h
        P.emit("gpsimd", lambda e, hc=hc: e.collective_compute(
            "AllGather", ALU.bypass, replica_groups=[list(range(8))],
            ins=[agin_h.ap().opt()],
            outs=[agout_h.ap().opt()]).then_inc(hc, 1))
        P.wait("gpsimd", e_CC, e_CC.n)
        d_arb.n += 16
        ha = d_arb.h
        P.emit("gpsimd", lambda e, ha=ha: e.dma_start(
            thstg[0:8, :], agout_h.ap()).then_inc(ha, 16))

    def f_tx():
        for ci in range(NC16):
            th_step("tensor", lambda e, ci=ci: e.transpose(
                th_ps.ap()[:, 128:256], thstg[:, ci * 128 : (ci + 1) * 128],
                ident_f32[:]), waits=[(d_arb, d_arb.n), (e_TH, e_TH.n)])
            th_step("vector", lambda e, ci=ci: e.tensor_copy(
                xraw_sb[:, ci * 8 : ci * 8 + 8], th_ps.ap()[:, 128:136]),
                waits=[(e_TH, e_TH.n)])

    def f_agg(ocs, w_h, mov_sb):
        for oc in ocs:
            def mm(slab, oc=oc):
                hTH = e_TH.h
                for ci in range(NC16):
                    def m(e, oc=oc, ci=ci, slab=slab,
                          last=(ci == NC16 - 1)):
                        ins = e.matmul(
                            th_ps.ap()[:, 128 + oc * 8 : 128 + oc * 8 + 8],
                            slab[:, ci * 128 : (ci + 1) * 128],
                            mov_sb[:, ci * 8 : ci * 8 + 8],
                            start=(ci == 0), stop=(ci == NC16 - 1))
                        if last:
                            ins.then_inc(hTH, 1)
                    P.emit("tensor", m)
                e_TH.n += 1
            th_slab(lambda oc=oc: w_h.ap()[:, oc * 128 : (oc + 1) * 128]
                    .rearrange("(c p) f -> p c f", p=128), mm, dst3d=128)

    def f_agg_ev():
        for ci in range(NC16):
            th_step("scalar", lambda e, ci=ci: e.activation(
                tb_sb[:, ci * 8 : ci * 8 + 8],
                th_ps.ap()[:, 128 + ci * 8 : 128 + ci * 8 + 8],
                AF.Identity, bias=th_prm[:, 64 + ci : 64 + ci + 1],
                scale=1.0), waits=([(e_TH, e_TH.n)] if ci == 0 else ()))

    def f_d1(js):
        for hjl in js:
            def mm(slab, hjl=hjl):
                hTH = e_TH.h
                for ci in range(NC16):
                    def m(e, ci=ci, hjl=hjl, slab=slab,
                          last=(ci == NC16 - 1)):
                        ins = e.matmul(
                            th_ps.ap()[:, hjl * 8 : hjl * 8 + 8],
                            slab[:, ci * 128 : (ci + 1) * 128],
                            tnb_sb[:, ci * 8 : ci * 8 + 8],
                            start=(ci == 0), stop=(ci == NC16 - 1))
                        if last:
                            ins.then_inc(hTH, 1)
                    P.emit("tensor", m)
                e_TH.n += 1
            th_slab(lambda hjl=hjl: d1s_h.ap()[hjl], mm)
            th_step("scalar", lambda e, hjl=hjl: e.activation(
                hb_sb[:, hjl * 8 : hjl * 8 + 8],
                th_ps.ap()[:, hjl * 8 : hjl * 8 + 8], AF.Gelu,
                bias=th_d1bT[:, hjl : hjl + 1], scale=1.0),
                waits=[(e_TH, e_TH.n)])

    def f_d2(ocs):
        for oc in ocs:
            def mm(slab, oc=oc):
                hTH = e_TH.h
                for hjl in range(8):
                    def m(e, oc=oc, hjl=hjl, slab=slab, last=(hjl == 7)):
                        ins = e.matmul(
                            th_ps.ap()[:, 128 + oc * 8 : 128 + oc * 8 + 8],
                            slab[:, hjl * 128 : (hjl + 1) * 128],
                            hb_sb[:, hjl * 8 : hjl * 8 + 8],
                            start=(hjl == 0), stop=(hjl == 7))
                        if last:
                            ins.then_inc(hTH, 1)
                    P.emit("tensor", m)
                e_TH.n += 1
            th_slab(lambda oc=oc: d2s_h.ap()[:, oc * 128 : (oc + 1) * 128]
                    .rearrange("(h p) f -> p h f", p=128), mm,
                    width=1024, dst3d=128)

    def f_ar(k):
        th_step("vector", lambda e: e.tensor_copy(
            arb_sb[:], th_ps.ap()[:, 128:256]), waits=[(e_TH, e_TH.n)])
        P.wait("gpsimd", e_TH, e_TH.n)
        d_arb.n += 16
        ha = d_arb.h
        P.emit("gpsimd", lambda e, k=k, ha=ha: e.dma_start(
            arin_h[k].ap(), arb_sb[:]).then_inc(ha, 16))
        P.wait("gpsimd", d_arb, d_arb.n)
        e_CC.n += 1
        hc = e_CC.h
        P.emit("gpsimd", lambda e, k=k, hc=hc: e.collective_compute(
            "AllReduce", ALU.add, replica_groups=[list(range(8))],
            ins=[arin_h[k].ap().opt()],
            outs=[arout_h[k].ap().opt()]).then_inc(hc, 1))
        P.wait("gpsimd", e_CC, e_CC.n)
        d_arb.n += 16
        P.emit("gpsimd", lambda e, k=k, ha=ha: e.dma_start(
            arb_sb[:], arout_h[k].ap()).then_inc(ha, 16))

    def f_add():
        for ci in range(NC16):
            th_step("scalar", lambda e, ci=ci: e.activation(
                arb_sb[:, ci * 8 : ci * 8 + 8],
                arb_sb[:, ci * 8 : ci * 8 + 8],
                AF.Identity, bias=th_prm[:, 80 + ci : 80 + ci + 1],
                scale=1.0),
                waits=([(d_arb, d_arb.n), (e_TH, e_TH.n)] if ci == 0 else ()))
        th_step("vector", lambda e: e.tensor_tensor(
            tb_sb[:], tb_sb[:], arb_sb[:, 0:128], ALU.add),
            waits=[(e_TH, e_TH.n)])

    def f_thb():
        th_step("scalar", lambda e: e.copy(tnb_sb[:], tb_sb[:]),
                waits=[(e_TH, e_TH.n)])

    def f_bsel():
        arb3 = arb_sb.ap().rearrange("p (c b) -> p c b", b=NB)
        for ci in range(NC16):
            th_step("scalar", lambda e, ci=ci: e.activation(
                arb_sb[:, ci * 8 : ci * 8 + 8],
                th_ps.ap()[:, 128 + ci * 8 : 128 + ci * 8 + 8],
                AF.Identity, bias=th_prm[:, 96 + ci : 96 + ci + 1],
                scale=1.0), waits=([(e_TH, e_TH.n)] if ci == 0 else ()))
        th_step("vector", lambda e, arb3=arb3: e.tensor_tensor(
            arb3, arb3,
            bsel_sb.ap().unsqueeze(1).broadcast_to([128, NC16, NB]),
            ALU.mult), waits=[(e_TH, e_TH.n)])
        th_step("vector", lambda e, arb3=arb3: e.tensor_reduce(
            th_cols[:, 0:16], arb3, AX.X, ALU.add))
        th_step("vector", lambda e: e.memset(th_pad[:], 0.0))
        th_step("vector", lambda e: e.tensor_copy(
            th_pad[:, 0:16], th_cols[:, 0:16]))
        th_step("tensor", lambda e: e.transpose(
            th_ps.ap()[:, 128:256], th_pad[:], ident_f32[:]),
            waits=[(e_TH, e_TH.n)])
        th_step("vector", lambda e: e.tensor_copy(
            th_sb128[:], th_ps.ap()[:, 128:256]), waits=[(e_TH, e_TH.n)])
        P.wait("gpsimd", e_TH, e_TH.n)
        d_bcr.n += 16
        hbc = d_bcr.h
        P.emit("gpsimd", lambda e: e.dma_start(
            bcrow_h.ap().rearrange("(c f) -> c f", f=128),
            th_sb128[0:16, :]).then_inc(hbc, 16))

    fragments = []

    def piece(fn):
        fragments.append(fn)

    d_dmp = S("d_dmp")

    def th_dump(slot, src):
        def fr():
            P.wait("gpsimd", e_TH, e_TH.n)
            d_dmp.n += 16
            hd = d_dmp.h
            P.emit("gpsimd", lambda e, slot=slot, src=src, hd=hd: e.dma_start(
                dbg_th_h.ap()[slot], src).then_inc(hd, 16))
        return fr

    piece(f_ag)
    piece(f_tx)
    piece(th_dump(0, xraw_sb[:]))
    piece(lambda: th_ln(xraw_sb, xnb_sb, 0, 16, cEPS6r))
    piece(th_dump(1, xnb_sb[:]))
    for g_ in range(8):
        piece(lambda g_=g_: f_agg(range(2 * g_, 2 * g_ + 2), aggw_h, xnb_sb))
    piece(f_agg_ev)
    piece(th_dump(2, tb_sb[:]))
    for _k in range(4):
        piece(lambda: th_ln(tb_sb, tnb_sb, 32, 48, cEPS6r))
        for g_ in range(4):
            piece(lambda g_=g_: f_d1(range(2 * g_, 2 * g_ + 2)))
        for g_ in range(4):
            piece(lambda g_=g_: f_d2(range(4 * g_, 4 * g_ + 4)))
        piece(lambda _k=_k: f_ar(_k))
        piece(f_add)
        piece(th_dump(3 + _k, tb_sb[:]))
    piece(f_thb)
    for g_ in range(8):
        piece(lambda g_=g_: f_agg(range(2 * g_, 2 * g_ + 2), bcw_h, tnb_sb))
    piece(f_bsel)
    piece(th_dump(7, th_pad[:]))

    pair_i = [0]

    def emit_qt_loads(ti, tile):
        tstart = tile * TQ
        P.wait("sync", d_st, 96 * 4 * (tile + 1))
        for qh in range(NH):
            if ti >= 1:
                P.wait("sync", e_QH, (ti - 1) * NH + qh + 1)
            d_qt[qh].n += 16
            h = d_qt[qh].h
            P.emit("sync", lambda e, qh=qh, tstart=tstart, h=h: e.dma_start(
                qt_ring[:, qh * TQ : (qh + 1) * TQ],
                q_h.ap()[tstart : tstart + TQ, qh * 128 : (qh + 1) * 128],
                transpose=True).then_inc(h, 16))

    def emit_kv_loads(ti, tile):
        tstart = tile * TQ
        keys = tstart + TQ
        nblk = keys // 128
        for kvh in range(NKV):
            slot = ti * NKV + kvh
            par = slot % 2
            if slot >= 2:
                P.wait("sync", e_KV, slot - 1)
            d_kv[par].n += 32
            h = d_kv[par].h
            P.emit("sync", lambda e, kvh=kvh, par=par, keys=keys, h=h: e.dma_start(
                kt_buf[:, par * T : par * T + keys],
                k_h.ap()[0:keys, kvh * 128 : (kvh + 1) * 128],
                transpose=True).then_inc(h, 16))
            P.emit("sync", lambda e, kvh=kvh, par=par, nblk=nblk, h=h: e.dma_start(
                v_buf.ap()[:, par * T : par * T + nblk * 128].rearrange(
                    "p (b f) -> p b f", f=128),
                v_h.ap()[kvh, 0:nblk].rearrange("b p f -> p b f")).then_inc(h, 16))

    emit_qt_loads(0, tile_order[0])
    emit_kv_loads(0, tile_order[0])

    for ti, tile in enumerate(tile_order):
        tstart = tile * TQ
        nch = (tstart + TQ) // 128
        if ti + 1 < NT:
            emit_qt_loads(ti + 1, tile_order[ti + 1])
            emit_kv_loads(ti + 1, tile_order[ti + 1])

        for kvh in range(NKV):
            kvslot = ti * NKV + kvh
            par_kv = kvslot % 2
            P.wait("tensor", d_kv[kvslot % 2], 32 * (kvslot // 2 + 1))
            for qh4 in range(4):
                qh = kvh * 4 + qh4
                qidx = ti * NH + qh
                P.wait("tensor", d_qt[qh], 16 * (ti + 1))
                npairs = nch // 2
                pid0 = pair_i[0]

                def emit_S(p):
                    c0, c1 = 2 * p, 2 * p + 1
                    a0 = max(0, c0 * 128 - tstart)
                    pid = pid0 + p
                    par = pid % 2
                    sp = sp0 if par == 0 else sp1
                    P.wait("tensor", e_E, pid - 1)
                    hS = e_S.h
                    for hh, c in ((0, c0), (1, c1)):
                        def mmS(e, sp=sp, hh=hh, c=c, a0=a0, par_kv=par_kv,
                                qh=qh, last=(hh == 1)):
                            ins = e.matmul(
                                sp.ap()[:, hh * 512 + a0 : hh * 512 + 512],
                                kt_buf[:, par_kv * T + c * 128 : par_kv * T + (c + 1) * 128],
                                qt_ring[:, qh * TQ + a0 : (qh + 1) * TQ],
                                start=True, stop=True)
                            if last:
                                ins.then_inc(hS, 1)
                        P.emit("tensor", mmS)
                    e_S.n += 1
                    ptb = pt_buf[:, par * 1024 : par * 1024 + 1024]
                    P.wait("scalar", e_S, e_S.n)
                    P.wait("scalar", e_AV, pid - 1)
                    e_E.n += 1
                    hE = e_E.h
                    P.emit("scalar", lambda e, sp=sp, ptb=ptb, a0=a0:
                           e.activation(
                               ptb.rearrange("p (h f) -> p h f", f=512)[:, :, a0:512],
                               sp.ap().rearrange("p (h f) -> p h f", f=512)[:, :, a0:512],
                               AF.Exp, bias=cM40, scale=1.0).then_inc(hE, 1))
                    diag = [(hh, c) for hh, c in ((0, c0), (1, c1))
                            if 0 <= c * 128 - tstart < TQ]
                    if diag:
                        P.wait("vector", e_E, e_E.n)
                        e_M.n += 1
                        hM = e_M.h
                        for idx, (hh, c) in enumerate(diag):
                            doff = c * 128 - tstart
                            def mk(e, hh=hh, doff=doff, ptb=ptb,
                                   last=(idx == len(diag) - 1)):
                                ins = e.tensor_tensor(
                                    ptb[:, hh * 512 + doff : hh * 512 + doff + 128],
                                    ptb[:, hh * 512 + doff : hh * 512 + doff + 128],
                                    trimask[:], ALU.mult)
                                if last:
                                    ins.then_inc(hM, 1)
                            P.emit("vector", mk)
                        return ("M", e_M.n)
                    return ("E", e_E.n)

                def emit_AV(p, gate):
                    c0, c1 = 2 * p, 2 * p + 1
                    a0 = max(0, c0 * 128 - tstart)
                    a1 = max(0, c1 * 128 - tstart)
                    pid = pid0 + p
                    par = pid % 2
                    ptb = pt_buf[:, par * 1024 : par * 1024 + 1024]
                    if gate[0] == "M":
                        P.wait("tensor", e_M, gate[1])
                    else:
                        P.wait("tensor", e_E, gate[1])
                    if p == 0:
                        P.wait("tensor", e_OE, qidx)
                        P.wait("tensor", e_SE, qidx)
                    for hh, c, ac in ((0, c0, a0), (1, c1, a1)):
                        P.emit("tensor", lambda e, c=c, ac=ac, hh=hh, ptb=ptb,
                               par_kv=par_kv, nch=nch:
                               e.matmul(o_ps.ap()[:, ac:512],
                                        v_buf[:, par_kv * T + c * 128 : par_kv * T + (c + 1) * 128],
                                        ptb[:, hh * 512 + ac : hh * 512 + 512],
                                        start=(c == 0), stop=(c == nch - 1)))
                    e_AV.n += 1
                    hAV = e_AV.h
                    is_last_pair = p == npairs - 1
                    for hh, c, ac in ((0, c0, a0), (1, c1, a1)):
                        def mmO(e, c=c, ac=ac, hh=hh, ptb=ptb, last=(hh == 1),
                                ilp=is_last_pair, q4=qh4, nch=nch):
                            ins = e.matmul(sm_ps.ap()[:, ac:512],
                                           ones2d[:],
                                           ptb[:, hh * 512 + ac : hh * 512 + 512],
                                           start=(c == 0), stop=(c == nch - 1))
                            if last:
                                ins.then_inc(hAV, 1)
                                if ilp:
                                    e.nop().then_inc(e_QH.h, 1)
                                    if q4 == 3:
                                        e.nop().then_inc(e_KV.h, 1)
                        P.emit("tensor", mmO)
                    if is_last_pair:
                        e_QH.n += 1
                        if qh4 == 3:
                            e_KV.n += 1

                gates = {}
                gates[0] = emit_S(0)
                for p in range(npairs):
                    if p + 1 < npairs:
                        gates[p + 1] = emit_S(p + 1)
                    emit_AV(p, gates.pop(p))
                pair_i[0] += npairs
                P.wait("vector", e_AV, e_AV.n)
                if ti >= 1:
                    P.wait("vector", e_OP, ti * 16)
                e_SE.n += 1
                hSE = e_SE.h
                P.emit("vector", lambda e: e.reciprocal(
                    recip_sb[:], sm_ps.ap()[:]).then_inc(hSE, 1))
                e_OE.n += 1
                hOE = e_OE.h
                P.emit("vector", lambda e, qh=qh: e.scalar_tensor_tensor(
                    ot_sb[:, qh * TQ : (qh + 1) * TQ], o_ps.ap()[:],
                    0.02209708691207961, recip_sb[:],
                    ALU.mult, ALU.mult).then_inc(hOE, 1))
                if ti >= 2 and fragments:
                    fragments.pop(0)()
                if debug and ti == 0:
                    P.wait("sync", e_OE, e_OE.n)
                    d_dbg.n += 16
                    hdbg = d_dbg.h
                    P.emit("sync", lambda e, qh=qh: e.dma_start(
                        dbg_ot.ap()[qh], ot_sb[:, qh * TQ : (qh + 1) * TQ]).then_inc(hdbg, 16))

        P.wait("tensor", d_wo, 16 * NC16)
        _ = None
        for s in range(4):
            g = tile * 4 + s
            gi = ti * 4 + s
            par_x = gi % 2
            P.wait("scalar", e_OPE, 4 * gi)
            d_x.n += 16
            hx = d_x.h
            P.emit("scalar", lambda e, g=g, hx=hx: e.dma_start(
                x_sb[:],
                x_h.ap()[g * 128 : (g + 1) * 128, :]).then_inc(hx, 16))
            for oc in range(4):
                qtr = gi * 4 + oc
                P.wait("tensor", e_OE, (ti + 1) * NH)
                P.wait("tensor", e_OPE, qtr)
                P.wait("tensor", e_X1E, 4 * gi)
                for dc in range(NC16):
                    P.emit("tensor", lambda e, dc=dc, s=s, oc=oc:
                           e.matmul(op_ps.ap()[:],
                                    ot_sb[:, dc * TQ + s * 128 : dc * TQ + (s + 1) * 128],
                                    wo_sb[:, dc * D + oc * 512 : dc * D + (oc + 1) * 512],
                                    start=(dc == 0), stop=False))
                e_OP.n += 1
                hOP = e_OP.h
                P.emit("tensor", lambda e, oc=oc: e.matmul(
                    op_ps.ap()[:], ones1[:], bo_sb[:, oc * 512 : (oc + 1) * 512],
                    start=False, stop=True).then_inc(hOP, 1))
                P.wait("vector", e_OP, e_OP.n)
                P.wait("vector", d_x, 16 * (gi + 1))
                P.wait("vector", d_x1F, 16 * min(gi, 4))
                P.wait("vector", d_x1R, 16 * max(0, gi - 4))
                P.wait("vector", e_X1T, 4 * gi)
                e_OPE.n += 1
                hOPE = e_OPE.h
                P.emit("vector", lambda e, oc=oc: e.scalar_tensor_tensor(
                    x1_sb[:, oc * 512 : (oc + 1) * 512], op_ps.ap()[:], 1.0,
                    x_sb[:, oc * 512 : (oc + 1) * 512],
                    ALU.mult, ALU.add,
                    accum_out=lnv[:, L_R1 + oc : L_R1 + oc + 1]).then_inc(hOPE, 1))
            if debug:
                P.wait("scalar", e_OPE, e_OPE.n)
                d_dbg.n += 16
                hdbg2 = d_dbg.h
                P.emit("scalar", lambda e, g=g, hdbg2=hdbg2: e.dma_start(
                    dbg_r1.ap()[g], x1_sb[:]).then_inc(hdbg2, 16))
                P.wait("scalar", d_dbg, d_dbg.n)
            P.wait("scalar", e_OPE, e_OPE.n)
            e_LN.n += 1
            hLN = e_LN.h
            P.emit("scalar", lambda e: e.activation(
                x_sb[:], x1_sb[:], AF.Square, bias=0.0, scale=1.0,
                accum_out=lnv[:, L_SQ : L_SQ + 1]))
            P.emit("scalar", lambda e: e.drain().then_inc(hLN, 1))
            P.wait("vector", e_LN, e_LN.n)
            e_LN.n += 1
            def _ln1v(e):
                e.drain()
                e.tensor_reduce(lnv[:, L_SUM : L_SUM + 1],
                                lnv[:, L_R1 : L_R1 + 4], AX.X, ALU.add)
                e.drain()
                e.tensor_scalar(lnv[:, L_MU : L_MU + 1],
                                lnv[:, L_SUM : L_SUM + 1], 1.0 / D, None, ALU.mult)
                e.tensor_scalar(lnv[:, L_VAR : L_VAR + 1],
                                lnv[:, L_SQ : L_SQ + 1], 1.0 / D, None, ALU.mult)
                e.drain()
                e.tensor_tensor(lnv[:, L_TMP : L_TMP + 1],
                                lnv[:, L_MU : L_MU + 1],
                                lnv[:, L_MU : L_MU + 1], ALU.mult)
                e.drain()
                e.tensor_tensor(lnv[:, L_VAR : L_VAR + 1],
                                lnv[:, L_VAR : L_VAR + 1],
                                lnv[:, L_TMP : L_TMP + 1], ALU.subtract)
                e.drain().then_inc(hLN, 1)
            P.emit("vector", _ln1v)
            P.wait("scalar", e_LN, e_LN.n)
            e_LN.n += 1
            def _ln1s(e):
                e.activation(lnv[:, L_RSTD : L_RSTD + 1],
                             lnv[:, L_VAR : L_VAR + 1], AF.Ln, bias=cEPS5, scale=1.0)
                e.drain()
                e.activation(lnv[:, L_RSTD : L_RSTD + 1],
                             lnv[:, L_RSTD : L_RSTD + 1], AF.Exp, bias=0.0, scale=cNH)
                e.drain().then_inc(hLN, 1)
            P.emit("scalar", _ln1s)
            P.wait("vector", e_LN, e_LN.n)
            e_LN.n += 1
            def _ln1n(e):
                e.scalar_tensor_tensor(lnv[:, L_NMR : L_NMR + 1],
                                       lnv[:, L_MU : L_MU + 1], C_M1,
                                       lnv[:, L_RSTD : L_RSTD + 1], ALU.mult, ALU.mult)
                e.drain().then_inc(hLN, 1)
            P.emit("vector", _ln1n)
            P.wait("scalar", e_LN, e_LN.n)
            if debug:
                d_dbg.n += 16
                hdbg3 = d_dbg.h
                P.emit("scalar", lambda e, g=g, hdbg3=hdbg3: e.dma_start(
                    dbg_lnv.ap()[g], lnv[:]).then_inc(hdbg3, 16))
                P.wait("scalar", d_dbg, d_dbg.n)
            e_LN.n += 1
            P.emit("scalar", lambda e: e.activation(
                x1_sb[:], x1_sb[:], AF.Identity, bias=lnv[:, L_NMR : L_NMR + 1],
                scale=lnv[:, L_RSTD : L_RSTD + 1]).then_inc(hLN, 1))
            P.wait("scalar", e_LN, e_LN.n)
            sx1 = d_x1F if gi < 4 else d_x1R
            sx1.n += 16
            hx1 = sx1.h
            P.emit("scalar", lambda e, g=g, hx1=hx1: e.dma_start(
                x1_h.ap()[g * 128 : (g + 1) * 128, :], x1_sb[:]).then_inc(hx1, 16))
            P.wait("tensor", e_LN, e_LN.n)
            for cg in range(4):
                if cg > 0:
                    P.wait("tensor", e_X1E, 4 * gi + cg)
                hXT1 = e_X1T.h
                for j in range(4):
                    def tp1(e, c=cg * 4 + j, j=j, last=(j == 3)):
                        ins = e.transpose(
                            op_ps.ap()[:, j * 128 : (j + 1) * 128],
                            x1_sb[:, c * 128 : (c + 1) * 128], ident_f32[:])
                        if last:
                            ins.then_inc(hXT1, 1)
                    P.emit("tensor", tp1)
                e_X1T.n += 1
                P.wait("scalar", e_X1T, e_X1T.n)
                P.wait("scalar", d_x1s, 16 * gi)
                e_X1E.n += 1
                hXE1 = e_X1E.h
                P.emit("scalar", lambda e, cg=cg, hXE1=hXE1: e.copy(
                    x1tb[:, cg * 512 : (cg + 1) * 512],
                    op_ps.ap()[:]).then_inc(hXE1, 1))
            P.wait("gpsimd", e_X1E, e_X1E.n)
            d_x1s.n += 16
            hx1s = d_x1s.h
            P.emit("gpsimd", lambda e, g=g, hx1s=hx1s: e.dma_start(
                x1T_h.ap()[g], x1tb[:]).then_inc(hx1s, 16))

    while fragments:
        fragments.pop(0)()
    P.wait("gpsimd", d_dmp, d_dmp.n)

    P2_OP = e_OP.n
    P2_OPE = e_OPE.n
    P2_AV = e_AV.n
    P2_OE = e_OE.n
    P2_SE = e_SE.n
    P2_LN = e_LN.n
    TH_ALL = e_TH.n
    p2_stack.close()

    p3_stack = contextlib.ExitStack()
    gw_sb = p3_stack.enter_context(nc.sbuf_tensor("gw_sb", [128, NC16 * D], BF16))
    x1t_ring = p3_stack.enter_context(
        nc.sbuf_tensor("x1t_ring", [128, 2 * NC16 * TQ], BF16))
    bc_bc = p3_stack.enter_context(nc.sbuf_tensor("bc_bc", [128, D], F32))
    outg_bc = p3_stack.enter_context(nc.sbuf_tensor("outg_bc", [128, D], F32))
    outb_bc = p3_stack.enter_context(nc.sbuf_tensor("outb_bc", [128, D], F32))
    x1l_sb = p3_stack.enter_context(nc.sbuf_tensor("x1l_sb", [128, 2 * D], F32))
    g_sb2 = p3_stack.enter_context(nc.sbuf_tensor("g_sb2", [128, 2 * D], F32))
    o3_sb2 = p3_stack.enter_context(nc.sbuf_tensor("o3_sb2", [128, 2 * D], F32))
    st3 = p3_stack.enter_context(nc.sbuf_tensor("st3", [128, 64], F32))
    g0_ps = p3_stack.enter_context(nc.psum_tensor("g0_ps", [128, TQ], F32))
    g1_ps = p3_stack.enter_context(nc.psum_tensor("g1_ps", [128, TQ], F32))
    gb_sb = p3_stack.enter_context(nc.sbuf_tensor("gb_sb", [1, D], BF16))

    for eng in ("sync", "gpsimd", "scalar"):
        P.wait(eng, e_OP, P2_OP)
        P.wait(eng, e_OPE, P2_OPE)
        P.wait(eng, e_AV, P2_AV)
        P.wait(eng, e_OE, P2_OE)
        P.wait(eng, e_SE, P2_SE)
        P.wait(eng, e_SM, e_SM.n)
        P.wait(eng, e_ACv, e_ACv.n)
        P.wait(eng, e_ACg, e_ACg.n)
        P.wait(eng, e_LN, P2_LN)
        P.wait(eng, e_TH, TH_ALL)
        for s_ in d_thr:
            P.wait(eng, s_, s_.n)
        for s_ in d_qt:
            P.wait(eng, s_, s_.n)
        P.wait(eng, d_kv[0], d_kv[0].n)
        P.wait(eng, d_kv[1], d_kv[1].n)
        P.wait(eng, d_x, d_x.n)
        P.wait(eng, d_x1F, 16 * min(NSUB, 4))
        P.wait(eng, d_x1R, 16 * max(0, NSUB - 4))
        P.wait(eng, d_x1s, 16 * NSUB)
        P.wait(eng, e_X1T, 4 * NSUB)
        P.wait(eng, e_X1E, 4 * NSUB)

    ld(outg_bc[:], outg_h.ap().unsqueeze(0).partition_broadcast(128), sem=d_csD)
    ld(outb_bc[:], outb_h.ap().unsqueeze(0).partition_broadcast(128), sem=d_csD)
    P3_CS = d_csD.n
    P.wait("sync", d_bcr, 16)
    d_bcb.n += 16
    hbc2 = d_bcb.h
    P.emit("sync", lambda e: e.dma_start(
        bc_bc[:], bcrow_h.ap().unsqueeze(0).partition_broadcast(128)
    ).then_inc(hbc2, 16))
    BC_READY = d_bcb.n

    ld(gb_sb[:], gb_h.ap(), sem=d_bias, q="gpsimd")
    GB_LD = d_bias.n
    P.wait("tensor", d_bias, GB_LD)
    gw_src = gw_h.ap().rearrange("(c p) f -> p c f", p=128)
    gw_dst = gw_sb.ap().rearrange("p (c f) -> p c f", f=D)
    for c in range(NC16):
        d_gw.n += 16
        h = d_gw.h
        P.emit("gpsimd", lambda e, c=c, h=h: e.dma_start(
            gw_dst[:, c, :], gw_src[:, c, :]).then_inc(h, 16))

    def x1t_slot(t, c):
        off = ((t % 2) * NC16 + c) * TQ
        return x1t_ring[:, off : off + TQ]

    F_PER_SUB = 9
    pending = []

    def flush_pending(k=None):
        n = len(pending) if k is None else min(k, len(pending))
        for _ in range(n):
            eng, waits, fns = pending.pop(0)
            for sem_, val_ in waits:
                P.wait(eng, sem_, val_)
            for fn in fns:
                P.emit(eng, fn)

    def build_finals(g):
        par = g % 2
        PB = 32 * par
        f_ = e_Fp[par]
        hF = f_.h
        base = f_.n
        gv = g_sb2[:, par * D : par * D + D]
        ov = o3_sb2[:, par * D : par * D + D]
        x1v = x1l_sb[:, par * D : par * D + D]
        bn = st3[:, PB : PB + 24]
        mu_c = st3[:, PB + 24 : PB + 25]
        var_c = st3[:, PB + 25 : PB + 26]
        mv_c = st3[:, PB + 24 : PB + 26]
        rstd_c = st3[:, PB + 26 : PB + 27]
        nmr_c = st3[:, PB + 27 : PB + 28]

        def bn_block(e, src):
            for q in range(4):
                e.bn_stats(bn[:, 6 * q : 6 * q + 6],
                           src[:, 512 * q : 512 * q + 512])
            e.drain()
            e.bn_aggr(mv_c, bn)

        def _b1(e):
            e.scalar_tensor_tensor(ov, gv, 1.0, x1v, ALU.mult, ALU.add)
            bn_block(e, ov)
            e.drain().then_inc(hF, 1)
        pending.append(("vector",
                        [(e_GE, e_GE.n), (d_csD, P3_CS),
                         (d_x1l[par], 16 * (g // 2 + 1)), (d_out, 16 * (g - 1))],
                        [_b1]))

        def _b2(e):
            e.activation(rstd_c, var_c, AF.Ln, bias=cEPS6, scale=1.0)
            e.drain()
            e.activation(rstd_c, rstd_c, AF.Exp, bias=0.0, scale=cNH)
            e.drain().then_inc(hF, 1)
        pending.append(("scalar", [(f_, base + 1)], [_b2]))

        def _b3(e):
            e.scalar_tensor_tensor(nmr_c, mu_c, C_M1, rstd_c, ALU.mult, ALU.mult)
            e.drain().then_inc(hF, 1)
        pending.append(("vector", [(f_, base + 2)], [_b3]))

        def _b4(e):
            e.activation(gv, ov, AF.Identity, bias=nmr_c,
                         scale=rstd_c).then_inc(hF, 1)
        pending.append(("scalar", [(f_, base + 3)], [_b4]))

        def _b5a(e):
            e.tensor_tensor(gv, gv, outg_bc[:], ALU.mult)
            e.tensor_tensor(gv, gv, outb_bc[:], ALU.add)
            e.drain().then_inc(hF, 1)
        pending.append(("gpsimd", [(f_, base + 4), (d_csD, P3_CS)], [_b5a]))

        def _b5b(e):
            e.scalar_tensor_tensor(ov, gv, 1.0, x1v, ALU.mult, ALU.add)
            bn_block(e, ov)
            e.drain().then_inc(hF, 1)
        pending.append(("vector", [(f_, base + 5)], [_b5b]))

        def _b6(e):
            e.activation(rstd_c, var_c, AF.Ln, bias=cEPS5, scale=1.0)
            e.drain()
            e.activation(rstd_c, rstd_c, AF.Exp, bias=0.0, scale=cNH)
            e.drain().then_inc(hF, 1)
        pending.append(("scalar", [(f_, base + 6)], [_b6]))

        def _b7(e):
            e.scalar_tensor_tensor(nmr_c, mu_c, C_M1, rstd_c, ALU.mult, ALU.mult)
            e.drain().then_inc(hF, 1)
        pending.append(("vector", [(f_, base + 7)], [_b7]))

        def _b8(e):
            e.activation(ov, ov, AF.Identity, bias=nmr_c,
                         scale=rstd_c).then_inc(hF, 1)
        pending.append(("scalar", [(f_, base + 8)], [_b8]))

        d_out.n += 16
        hout = d_out.h

        def _b9(e):
            e.dma_start(out_h.ap()[g * 128 : (g + 1) * 128, :],
                        ov).then_inc(hout, 16)
        pending.append(("scalar", [(f_, base + 9)], [_b9]))
        f_.n += F_PER_SUB

    for t in range(NT):
        P.wait("sync", d_x1s, 16 * NSUB)
        if t >= 2:
            P.wait("sync", e_G, (t - 1) * 16)
        for c in range(NC16):
            d_x1t[t % 2].n += 16
            h = d_x1t[t % 2].h
            P.emit("sync", lambda e, t=t, c=c, h=h: e.dma_start(
                x1t_slot(t, c).rearrange("p (g f) -> p g f", f=128),
                x1T_h.ap()[4 * t : 4 * t + 4, :, c * 128 : (c + 1) * 128]
                .rearrange("g p f -> p g f")).then_inc(h, 16))

        for s in range(4):
            g = t * 4 + s
            par_x = g % 2
            if g == 0:
                d_x1l[0].n += 16
                hxl = d_x1l[0].h
                P.emit("scalar", lambda e, hxl=hxl: e.dma_start(
                    x1l_sb[:, 0:D],
                    x1_h.ap()[0:128, :]).then_inc(hxl, 16))
            for oc in range(4):
                qtr = g * 4 + oc
                gps = g0_ps if qtr % 2 == 0 else g1_ps
                P.wait("tensor", d_x1t[t % 2], 16 * NC16 * (t // 2 + 1))
                P.wait("tensor", d_gw, 16 * NC16)
                if qtr >= 2:
                    P.wait("tensor", e_GE, 2 * (qtr - 2) + 1)
                for dc in range(NC16):
                    P.emit("tensor", lambda e, dc=dc, s=s, oc=oc, t=t, gps=gps:
                           e.matmul(gps.ap()[:],
                                    x1t_slot(t, dc)[:, s * 128 : (s + 1) * 128],
                                    gw_sb[:, dc * D + oc * 512 : dc * D + (oc + 1) * 512],
                                    start=(dc == 0), stop=False))
                e_G.n += 1
                hG = e_G.h
                P.emit("tensor", lambda e, oc=oc, gps=gps: e.matmul(
                    gps.ap()[:], ones1[:], gb_sb[:, oc * 512 : (oc + 1) * 512],
                    start=False, stop=True).then_inc(hG, 1))
                P.wait("scalar", e_G, e_G.n)
                if g >= 2:
                    P.wait("scalar", e_Fp[g % 2], F_PER_SUB * (g // 2 - 1) + 6)
                gv = g_sb2[:, par_x * D : par_x * D + D]
                e_GE.n += 1
                hGE = e_GE.h
                P.emit("scalar", lambda e, oc=oc, gps=gps, gv=gv, hGE=hGE:
                       e.activation(
                    gv[:, oc * 512 : (oc + 1) * 512], gps.ap()[:],
                    AF.Sigmoid, bias=0.0, scale=1.0).then_inc(hGE, 1))
                P.wait("gpsimd", d_bcb, BC_READY)
                P.wait("gpsimd", e_GE, e_GE.n)
                e_GE.n += 1

                def _bcq(e, oc=oc, gv=gv, hGE=hGE):
                    e.tensor_tensor(gv[:, oc * 512 : (oc + 1) * 512],
                                    gv[:, oc * 512 : (oc + 1) * 512],
                                    bc_bc[:, oc * 512 : (oc + 1) * 512],
                                    ALU.mult)
                    e.drain().then_inc(hGE, 1)
                P.emit("gpsimd", _bcq)
                flush_pending(3 if oc < 3 else 1)
            gl = g + 1
            if gl < NSUB:
                if gl >= 2:
                    P.wait("scalar", e_Fp[gl % 2], F_PER_SUB * (gl // 2 - 1) + 6)
                d_x1l[gl % 2].n += 16
                hxl = d_x1l[gl % 2].h
                P.emit("scalar", lambda e, gl=gl, par=gl % 2, hxl=hxl: e.dma_start(
                    x1l_sb[:, par * D : par * D + D],
                    x1_h.ap()[gl * 128 : (gl + 1) * 128, :]).then_inc(hxl, 16))
            build_finals(g)

    flush_pending()
    P.wait("scalar", d_out, 16 * NSUB)
    p3_stack.close()

    with nc.Block() as block:
        @block.tensor
        def _(e):
            for fn in P.q["tensor"]:
                fn(e)

        @block.vector
        def _(e):
            for fn in P.q["vector"]:
                fn(e)

        @block.scalar
        def _(e):
            for fn in P.q["scalar"]:
                fn(e)

        @block.sync
        def _(e):
            for fn in P.q["sync"]:
                fn(e)

        @block.gpsimd
        def _(e):
            for fn in P.q["gpsimd"]:
                fn(e)

    return nc


def _perm_for_heads(nheads):
    perm = []
    for h in range(nheads):
        base = h * HD
        perm.extend(base + 2 * np.arange(64))
        perm.extend(base + 2 * np.arange(64) + 1)
    return np.array(perm)


def _colform(v):
    return np.ascontiguousarray(np.asarray(v, np.float32).reshape(NC16, 128).T)


def pack_inputs(inputs, T):
    f32 = lambda a: np.ascontiguousarray(np.asarray(a, np.float32))
    x = f32(inputs["L0_attn_x"])
    B = x.shape[0]
    wq = f32(inputs["L0_attn_Wq"])
    bq = f32(inputs["L0_attn_bq"])
    wk = f32(inputs["L0_attn_Wk"])
    bk = f32(inputs["L0_attn_bk"])
    wv = f32(inputs["L0_attn_Wv"])
    bv = f32(inputs["L0_attn_bv"])
    qperm = _perm_for_heads(NH)
    kperm = _perm_for_heads(NKV)
    wqkv = np.ascontiguousarray(
        np.concatenate([wq[:, qperm], wk[:, kperm], wv], axis=1))
    bqkv = np.ascontiguousarray(
        np.concatenate([bq[qperm], bk[kperm], bv])[None, :])

    d1 = f32(inputs["L0_ffn_d1_w"])
    d1r = np.ascontiguousarray(
        d1.reshape(NC16, 128, FF // 128, 128).transpose(2, 1, 0, 3).reshape(
            FF // 128, 128, D))

    j = np.arange(0, D, 2, dtype=np.float32) / np.float32(D)
    freqs = (np.float32(10000.0) ** (-j))[:64]
    t = np.arange(T, dtype=np.float32)
    ang = np.outer(t, freqs).astype(np.float32)

    bf = ml_dtypes.bfloat16
    common = {
        "wqkv": wqkv,
        "bqkv": bqkv,
        "wo": f32(inputs["L0_attn_Wo"]),
        "bo": f32(inputs["L0_attn_bo"]).reshape(1, D),
        "gate_w": f32(inputs["L0_ffn_gate_w"]),
        "gate_b": f32(inputs["L0_ffn_gate_b"]).reshape(1, D),
        "agg_w": f32(inputs["L0_ffn_agg_w"]),
        "bc_w": f32(inputs["L0_ffn_bc_w"]),
        "agg_bT": _colform(inputs["L0_ffn_agg_b"]),
        "d2_bT": _colform(inputs["L0_ffn_d2_b"]),
        "bc_bT": _colform(inputs["L0_ffn_bc_b"]),
        "in_gT": _colform(inputs["L0_ffn_in_gamma"]),
        "in_bT": _colform(inputs["L0_ffn_in_beta"]),
        "pn_gT": _colform(inputs["L0_ffn_pn_gamma"]),
        "pn_bT": _colform(inputs["L0_ffn_pn_beta"]),
        "out_g": f32(inputs["L0_ffn_out_gamma"]),
        "out_b": f32(inputs["L0_ffn_out_beta"]),
        "cos_t": np.cos(ang).astype(np.float32),
        "sin_t": np.sin(ang).astype(np.float32),
        "trimask": np.triu(np.ones((128, 128), np.float32)).astype(bf),
        "ones1": np.ones((1, 128), np.float32).astype(bf),
        "ones2d": np.ones((128, 128), np.float32).astype(bf),
        "ones2df": np.ones((128, 128), np.float32),
        "onescol": np.ones((128, 1), np.float32),
        "onesrow": np.ones((1, 128), np.float32),
        "ident_bf": np.eye(128, dtype=np.float32).astype(bf),
        "ident_f32": np.eye(128, dtype=np.float32),
        "consts": np.array(REG_CONSTS, np.float32),
    }
    d2 = f32(inputs["L0_ffn_d2_w"])
    d1bT = np.ascontiguousarray(
        f32(inputs["L0_ffn_d1_b"]).reshape(FF // 128, 128).T)
    in_maps = []
    for b in range(B):
        m = dict(common)
        m["x"] = np.ascontiguousarray(x[b])
        m["d1_slice"] = np.ascontiguousarray(d1r[8 * b : 8 * b + 8])
        m["d2_slice"] = np.ascontiguousarray(d2[1024 * b : 1024 * (b + 1), :])
        m["d1_bT"] = np.ascontiguousarray(d1bT[:, 8 * b : 8 * b + 8])
        sel = np.zeros(8, np.float32)
        sel[b] = 1.0
        m["bsel"] = sel
        in_maps.append(m)
    return in_maps


def kernel(**inputs):
    x = np.asarray(inputs["L0_attn_x"])
    B, T, _ = x.shape
    in_maps = pack_inputs(inputs, T)
    nc = build(T)
    res = run_bass_kernel_spmd(nc, in_maps, core_ids=list(range(B)))
    out = np.stack([np.asarray(res.results[i]["out"], np.float32)
                    for i in range(B)], axis=0)
    return out


# version 27
# speedup vs baseline: 1.1637x; 1.1637x over previous
import contextlib

import numpy as np
import ml_dtypes

import concourse.bass as bass
import concourse.mybir as mybir
from concourse.bass_utils import run_bass_kernel_spmd

F32 = mybir.dt.float32
BF16 = mybir.dt.bfloat16
AF = mybir.ActivationFunctionType
ALU = mybir.AluOpType
AX = mybir.AxisListType

D = 2048
NH = 16
NKV = 4
HD = 128
FF = 8192
TQ = 512
NC16 = D // 128
SQRT_D = 45.254833995939045

C_M40 = -40.0
C_EPS5 = 1e-5
C_EPS6 = 1e-6
C_NHALF = -0.5
C_SQD = SQRT_D
C_M1 = -1.0
C_P1 = 1.0
REG_CONSTS = [C_M40, C_EPS5, C_EPS6, C_NHALF, C_SQD, C_M1, C_P1]

L_R1, L_SUM, L_SQ, L_MU, L_VAR, L_RSTD, L_NMR, L_TMP = 0, 4, 5, 6, 7, 8, 9, 10

TC_TH = 0
TC_T = 16
TC_RSTD = 48
TC_NMR = 49
TC_SUM = 52
TC_SQ = 53
TC_MU = 54
TC_VAR = 55
TC_RS = 56
TC_NM = 57
TC_TMP = 58


class Sem:
    __slots__ = ("h", "n", "name")

    def __init__(self, stack, nc, name):
        self.h = stack.enter_context(nc.semaphore(name))
        self.n = 0
        self.name = name


class Prog:
    ENGS = ("tensor", "vector", "scalar", "sync", "gpsimd")

    def __init__(self):
        self.q = {e: [] for e in self.ENGS}
        self.hwm = {e: {} for e in self.ENGS}

    def emit(self, eng, fn):
        self.q[eng].append(fn)

    def wait(self, eng, sem, val):
        if val <= 0:
            return
        if self.hwm[eng].get(sem.name, 0) >= val:
            return
        self.hwm[eng][sem.name] = val
        h = sem.h
        self.q[eng].append(lambda e: e.wait_ge(h, val))


def build(T, debug=False):
    assert T % TQ == 0
    NT = T // TQ
    NSUB = T // 128

    nc = bass.Bass("TRN2", target_bir_lowering=False,
                   detect_race_conditions=False)

    def inp(name, shape, dt=F32):
        return nc.dram_tensor(name, list(shape), dt, kind="ExternalInput")

    x_h = inp("x", [T, D])
    wqkv_h = inp("wqkv", [D, 3072])
    bqkv_h = inp("bqkv", [1, 3072])
    wo_h = inp("wo", [D, D])
    bo_h = inp("bo", [1, D])
    gw_h = inp("gate_w", [D, D])
    gb_h = inp("gate_b", [1, D])
    aggw_h = inp("agg_w", [D, D])
    bcw_h = inp("bc_w", [D, D])
    d1s_h = inp("d1_slice", [8, 128, D])
    d2s_h = inp("d2_slice", [FF // 8, D])
    d1bT_h = inp("d1_bT", [128, 8])
    bsel_h = inp("bsel", [8])
    aggbT_h = inp("agg_bT", [128, NC16])
    d2bT_h = inp("d2_bT", [128, NC16])
    bcbT_h = inp("bc_bT", [128, NC16])
    ingT_h = inp("in_gT", [128, NC16])
    inbT_h = inp("in_bT", [128, NC16])
    pngT_h = inp("pn_gT", [128, NC16])
    pnbT_h = inp("pn_bT", [128, NC16])
    outg_h = inp("out_g", [D])
    outb_h = inp("out_b", [D])
    cos_h = inp("cos_t", [T, 64])
    sin_h = inp("sin_t", [T, 64])
    trimask_h = inp("trimask", [128, 128], BF16)
    ones1_h = inp("ones1", [1, 128], BF16)
    ones2d_h = inp("ones2d", [128, 128], BF16)
    onescol_h = inp("onescol", [128, 1])
    onesrow_h = inp("onesrow", [1, 128])
    ident_h = inp("ident_bf", [128, 128], BF16)
    identf_h = inp("ident_f32", [128, 128])
    ones2df_h = inp("ones2df", [128, 128])
    consts_h = inp("consts", [len(REG_CONSTS)])

    out_h = nc.dram_tensor("out", [T, D], F32, kind="ExternalOutput")

    def scratch(name, shape, dt):
        kind = "ExternalOutput" if debug else "Internal"
        return nc.dram_tensor(name, list(shape), dt, kind=kind)

    agin_h = nc.dram_tensor("cc_agin", [1, D], F32)
    agout_h = nc.dram_tensor("cc_agout", [8, D], F32)
    dbg_th_h = nc.dram_tensor("dbg_th", [8, 128, 128], F32,
                              kind="ExternalOutput")
    arin_h = [nc.dram_tensor(f"cc_arin{k}", [128, 128], F32) for k in range(4)]
    arout_h = [nc.dram_tensor(f"cc_arout{k}", [128, 128], F32) for k in range(4)]
    q_h = scratch("q_s", [T, D], BF16)
    k_h = scratch("k_s", [T, NKV * HD], BF16)
    v_h = scratch("v_s", [NKV, T // 128, 128, 128], BF16)
    x1_h = scratch("x1_s", [T, D], F32)
    x1b_h = scratch("x1b_s", [T, D], BF16)
    bcrow_h = scratch("bc_row", [D], F32)

    P = Prog()
    stk = contextlib.ExitStack()
    nc._keepalive_stack = stk

    def S(name):
        return Sem(stk, nc, name)

    d_csA = S("d_csA")
    d_csB = S("d_csB")
    d_csC = S("d_csC")
    d_csD = S("d_csD")
    d_xf = S("d_xf")
    e_XT = S("e_XT")
    e_XP = S("e_XP")
    d_wq = S("d_wq")
    d_arb = S("d_arb")
    e_CC = S("e_CC")
    d_st = S("d_st")
    d_wo = S("d_wo")
    d_qt = [S(f"d_qt{i}") for i in range(NH)]
    d_kv = [S("d_kv0"), S("d_kv1")]
    d_x = S("d_x")
    d_x1F = S("d_x1F")
    d_x1R = S("d_x1R")
    d_x1b = S("d_x1b")
    d_thr = [S(f"d_thr{i}") for i in range(4)]
    d_bias = S("d_bias")
    d_x1row = S("d_x1row")
    d_bcr = S("d_bcr")
    d_bcb = S("d_bcb")
    d_gw = S("d_gw")
    d_x1t = [S("d_x1t0"), S("d_x1t1")]
    d_x1l = [S("d_x1l0"), S("d_x1l1")]
    d_out = S("d_out")

    e_p1 = S("e_p1")
    e_evA = S("e_evA")
    e_evB = S("e_evB")
    e_S = S("e_S")
    e_E = S("e_E")
    e_M = S("e_M")
    e_AV = S("e_AV")
    e_SM = S("e_SM")
    e_ACv = S("e_ACv")
    e_ACg = S("e_ACg")
    e_QH = S("e_QH")
    e_KV = S("e_KV")
    e_OE = S("e_OE")
    e_SE = S("e_SE")
    e_OP = S("e_OP")
    e_OPE = S("e_OPE")
    e_LN = S("e_LN")
    e_TH = S("e_TH")
    e_G = S("e_G")
    e_GE = S("e_GE")
    e_Fp = [S("e_F0"), S("e_F1")]

    consts_sb = nc.alloc_sbuf_tensor("consts_sb", [128, 8], F32)
    cM40 = consts_sb[:, 0:1]
    cEPS5 = consts_sb[:, 1:2]
    cEPS6 = consts_sb[:, 2:3]
    cNH = consts_sb[:, 3:4]
    cSQD = consts_sb[:, 4:5]
    cM1s = consts_sb[:, 5:6]
    cP1 = consts_sb[:, 6:7]
    cM40r = consts_sb[0:1, 0:1]
    cEPS5r = consts_sb[0:1, 1:2]
    cEPS6r = consts_sb[0:1, 2:3]
    cNHr = consts_sb[0:1, 3:4]
    bo_sb = nc.alloc_sbuf_tensor("bo_sb", [1, D], BF16)
    ones1 = nc.alloc_sbuf_tensor("ones1_sb", [1, 128], BF16)
    ones2d = nc.alloc_sbuf_tensor("ones2d_sb", [128, 128], BF16)
    onescol = nc.alloc_sbuf_tensor("onescol_sb", [128, 1], F32)
    onesrow = nc.alloc_sbuf_tensor("onesrow_sb", [1, 128], F32)
    trimask = nc.alloc_sbuf_tensor("trimask_sb", [128, 128], BF16)
    ident_bf = nc.alloc_sbuf_tensor("ident_bf_sb", [128, 128], BF16)
    ident_f32 = nc.alloc_sbuf_tensor("ident_f32_sb", [128, 128], F32)
    ones2df = nc.alloc_sbuf_tensor("ones2df_sb", [128, 128], F32)
    lnv = nc.alloc_sbuf_tensor("lnv", [128, 32], F32)

    def ld(dst_ap, src_ap, sem=d_csA, q="sync"):
        sem.n += 16
        h = sem.h
        P.emit(q, lambda e, d=dst_ap, s=src_ap: e.dma_start(d, s).then_inc(h, 16))

    ld(consts_sb[:, 0 : len(REG_CONSTS)],
       consts_h.ap().unsqueeze(0).partition_broadcast(128))
    ld(ones1[:], ones1_h.ap())
    ld(ones2d[:], ones2d_h.ap())
    ld(onescol[:], onescol_h.ap())
    ld(onesrow[:], onesrow_h.ap())
    ld(trimask[:], trimask_h.ap())
    ld(ident_bf[:], ident_h.ap())
    ld(ident_f32[:], identf_h.ap())
    ld(ones2df[:], ones2df_h.ap())
    ld(bo_sb[:], bo_h.ap(), sem=d_bias, q="gpsimd")
    CONST_LDS = d_csA.n
    for eng in ("scalar", "vector", "tensor"):
        P.wait(eng, d_csA, CONST_LDS)

    p1_stack = contextlib.ExitStack()
    wqkv_sb = p1_stack.enter_context(
        nc.sbuf_tensor("wqkv_sb", [128, NC16 * 3072], BF16))
    xt_ring = p1_stack.enter_context(
        nc.sbuf_tensor("xt_ring", [128, 2 * NC16 * TQ], BF16))
    xf_ring = p1_stack.enter_context(nc.sbuf_tensor("xf_ring", [128, 2 * D], F32))
    qkv_out = p1_stack.enter_context(
        nc.sbuf_tensor("qkv_out", [128, 2 * 3072], BF16))
    cos_sb = p1_stack.enter_context(nc.sbuf_tensor("cos_sb", [128, NSUB * 64], F32))
    sin_sb = p1_stack.enter_context(nc.sbuf_tensor("sin_sb", [128, NSUB * 64], F32))
    rope_t = p1_stack.enter_context(nc.sbuf_tensor("rope_t", [128, 4 * 16 * 64], F32))
    psA = p1_stack.enter_context(nc.psum_tensor("psA", [128, 1536], F32))
    psB = p1_stack.enter_context(nc.psum_tensor("psB", [128, 1536], F32))
    xp_ps = p1_stack.enter_context(nc.psum_tensor("xp_ps", [128, 1024], F32))
    bqkv_sb = p1_stack.enter_context(nc.sbuf_tensor("bqkv_sb", [1, 3072], BF16))
    ld(bqkv_sb[:], bqkv_h.ap(), sem=d_bias, q="gpsimd")
    P.wait("tensor", d_bias, d_bias.n)

    ld(cos_sb[:], cos_h.ap().rearrange("(g p) j -> p g j", p=128), sem=d_csB)
    ld(sin_sb[:], sin_h.ap().rearrange("(g p) j -> p g j", p=128), sem=d_csB)
    P.wait("vector", d_csB, d_csB.n)

    wq_src = wqkv_h.ap().rearrange("(c p) f -> p c f", p=128)
    wq_dst = wqkv_sb.ap().rearrange("p (c f) -> p c f", f=3072)
    for c in range(NC16):
        d_wq.n += 16
        h = d_wq.h
        P.emit("gpsimd", lambda e, c=c, h=h: e.dma_start(
            wq_dst[:, c, :], wq_src[:, c, :]).then_inc(h, 16))

    def xt_slot(t, c):
        off = ((t % 2) * NC16 + c) * TQ
        return xt_ring[:, off : off + TQ]

    def xf_slot(g):
        return xf_ring[:, (g % 2) * D : (g % 2) * D + D]

    def emit_xload(g):
        if g >= 2:
            P.wait("sync", e_XT, 4 * (g - 1))
        d_xf.n += 16
        h = d_xf.h
        P.emit("sync", lambda e, g=g, h=h: e.dma_start(
            xf_slot(g), x_h.ap()[g * 128 : (g + 1) * 128, :]).then_inc(h, 16))

    def emit_xpose(t, s):
        g = t * 4 + s
        P.wait("tensor", d_xf, 16 * (g + 1))
        if t >= 2:
            P.wait("tensor", e_p1, (t - 1) * 8)
        for cg in range(4):
            gi4 = g * 4 + cg
            par = gi4 % 2
            if gi4 >= 2:
                P.wait("tensor", e_XP, gi4 - 1)
            hXT = e_XT.h
            for j in range(4):
                def tp(e, c=cg * 4 + j, par=par, j=j, g=g, last=(j == 3)):
                    ins = e.transpose(
                        xp_ps.ap()[:, par * 512 + j * 128 : par * 512 + (j + 1) * 128],
                        xf_slot(g)[:, c * 128 : (c + 1) * 128],
                        ident_f32[:])
                    if last:
                        ins.then_inc(hXT, 1)
                P.emit("tensor", tp)
            e_XT.n += 1
            P.wait("scalar", e_XT, e_XT.n)
            if t >= 2:
                P.wait("scalar", e_p1, (t - 1) * 8)
            slot0 = (t % 2) * NC16 + cg * 4
            dst = xt_ring.ap().rearrange("p (sl q) -> p sl q", q=TQ)[
                :, slot0 : slot0 + 4, s * 128 : (s + 1) * 128]
            e_XP.n += 1
            hXP = e_XP.h
            P.emit("scalar", lambda e, dst=dst, par=par: e.copy(
                dst, xp_ps.ap()[:, par * 512 : par * 512 + 512].rearrange(
                    "p (c f) -> p c f", f=128)).then_inc(hXP, 1))

    emit_xload(0)
    emit_xload(1)

    for t in range(NT):
        for s in range(4):
            if t * 4 + s + 2 < NSUB:
                emit_xload(t * 4 + s + 2)
            emit_xpose(t, s)
        for s in range(4):
            g = t * 4 + s
            par = g % 2
            qo = qkv_out[:, par * 3072 : par * 3072 + 3072]
            P.wait("tensor", e_XP, 4 * (g + 1))
            for half, ps, e_ev in ((0, psA, e_evA), (1, psB, e_evB)):
                if half == 0:
                    P.wait("tensor", e_evA, g)
                else:
                    P.wait("tensor", e_evB, 2 * g)
                for c in range(NC16):
                    lhs = xt_slot(t, c)[:, s * 128 : (s + 1) * 128]
                    P.wait("tensor", d_wq, 16 * (c + 1))
                    for sl in range(3):
                        col0 = half * 1536 + sl * 512
                        P.emit("tensor", lambda e, lhs=lhs, ps=ps, sl=sl, c=c, col0=col0:
                               e.matmul(ps.ap()[:, sl * 512 : sl * 512 + 512],
                                        lhs,
                                        wqkv_sb[:, c * 3072 + col0 : c * 3072 + col0 + 512],
                                        start=(c == 0), stop=False))
                e_p1.n += 1
                hp1 = e_p1.h
                for sl in range(3):
                    col0 = half * 1536 + sl * 512
                    last = sl == 2
                    def mmb(e, ps=ps, sl=sl, col0=col0, last=last):
                        ins = e.matmul(ps.ap()[:, sl * 512 : sl * 512 + 512],
                                       ones1[:],
                                       bqkv_sb[:, col0 : col0 + 512],
                                       start=False, stop=True)
                        if last:
                            ins.then_inc(hp1, 1)
                    P.emit("tensor", mmb)

                P.wait("vector", e_p1, e_p1.n)
                if g >= 2:
                    P.wait("vector", d_st, 96 * (g - 1))
                cos_ap = cos_sb[:, g * 64 : g * 64 + 64]
                sin_ap = sin_sb[:, g * 64 : g * 64 + 64]

                def rope(nheads, ps_off, out_off, inc_sem, ps=ps, qo=qo,
                         cos_ap=cos_ap, sin_ap=sin_ap):
                    pr = ps.ap()[:, ps_off : ps_off + nheads * 128].rearrange(
                        "p (h c) -> p h c", c=128)
                    xr = pr[:, :, 0:64]
                    xi = pr[:, :, 64:128]
                    cb = cos_ap.unsqueeze(1).broadcast_to([128, nheads, 64])
                    sb_ = sin_ap.unsqueeze(1).broadcast_to([128, nheads, 64])
                    t1 = rope_t.ap()[:, 0 : nheads * 64].rearrange(
                        "p (h c) -> p h c", c=64)
                    t2 = rope_t.ap()[:, 16 * 64 : 16 * 64 + nheads * 64].rearrange(
                        "p (h c) -> p h c", c=64)
                    t3 = rope_t.ap()[:, 32 * 64 : 32 * 64 + nheads * 64].rearrange(
                        "p (h c) -> p h c", c=64)
                    t4 = rope_t.ap()[:, 48 * 64 : 48 * 64 + nheads * 64].rearrange(
                        "p (h c) -> p h c", c=64)
                    ov = qo[:, out_off : out_off + nheads * 128].rearrange(
                        "p (h c) -> p h c", c=128)
                    P.emit("vector", lambda e: e.tensor_tensor(t1, xr, cb, ALU.mult))
                    P.emit("vector", lambda e: e.tensor_tensor(t2, xi, sb_, ALU.mult))
                    P.emit("vector", lambda e: e.tensor_tensor(t3, xr, sb_, ALU.mult))
                    P.emit("vector", lambda e: e.tensor_tensor(t4, xi, cb, ALU.mult))
                    P.emit("vector", lambda e: e.tensor_tensor(
                        ov[:, :, 0:64], t1, t2, ALU.subtract))
                    if inc_sem is None:
                        P.emit("vector", lambda e: e.tensor_tensor(
                            ov[:, :, 64:128], t3, t4, ALU.add))
                    else:
                        hh = inc_sem
                        P.emit("vector", lambda e: e.tensor_tensor(
                            ov[:, :, 64:128], t3, t4, ALU.add).then_inc(hh, 1))

                if half == 0:
                    e_evA.n += 1
                    rope(12, 0, 0, e_evA.h)
                else:
                    rope(4, 0, 1536, None)
                    e_evB.n += 1
                    rope(4, 512, 2048, e_evB.h)
                    P.wait("scalar", e_p1, e_p1.n)
                    if g >= 2:
                        P.wait("scalar", d_st, 96 * (g - 1))
                    e_evB.n += 1
                    hevB = e_evB.h
                    P.emit("scalar", lambda e, ps=ps, qo=qo: e.copy(
                        qo[:, 2560:3072], ps.ap()[:, 1024:1536]).then_inc(hevB, 1))

            P.wait("sync", e_evA, g + 1)
            P.wait("sync", e_evB, 2 * (g + 1))
            hst = d_st.h
            d_st.n += 16
            P.emit("sync", lambda e, g=g, qo=qo: e.dma_start(
                q_h.ap()[g * 128 : (g + 1) * 128, :], qo[:, 0:2048]).then_inc(hst, 16))
            d_st.n += 16
            P.emit("sync", lambda e, g=g, qo=qo: e.dma_start(
                k_h.ap()[g * 128 : (g + 1) * 128, :], qo[:, 2048:2560]).then_inc(hst, 16))
            for kv in range(NKV):
                d_st.n += 16
                P.emit("sync", lambda e, g=g, kv=kv, qo=qo: e.dma_start(
                    v_h.ap()[kv, g],
                    qo[:, 2560 + kv * 128 : 2560 + (kv + 1) * 128]).then_inc(hst, 16))

    P1_MM = e_p1.n
    P1_ST = d_st.n
    P1_EVA = e_evA.n
    P1_EVB = e_evB.n
    p1_stack.close()

    p2_stack = contextlib.ExitStack()
    wo_sb = p2_stack.enter_context(nc.sbuf_tensor("wo_sb", [128, NC16 * D], BF16))
    qt_ring = p2_stack.enter_context(nc.sbuf_tensor("qt_ring", [128, NH * TQ], BF16))
    kt_buf = p2_stack.enter_context(nc.sbuf_tensor("kt_buf", [128, 2 * T], BF16))
    v_buf = p2_stack.enter_context(nc.sbuf_tensor("v_buf", [128, 2 * T], BF16))
    pt_buf = p2_stack.enter_context(nc.sbuf_tensor("pt_buf", [128, 2 * 1024], BF16))
    ot_sb = p2_stack.enter_context(nc.sbuf_tensor("ot_sb", [128, NH * TQ], BF16))
    x_sb = p2_stack.enter_context(nc.sbuf_tensor("x_sb", [128, D], F32))
    x1_sb = p2_stack.enter_context(nc.sbuf_tensor("x1_sb", [128, D], F32))
    recip_sb = p2_stack.enter_context(nc.sbuf_tensor("recip_sb", [128, TQ], F32))
    thw_ring = p2_stack.enter_context(nc.sbuf_tensor("thw_ring", [128, 2 * D], BF16))
    thstg = p2_stack.enter_context(nc.sbuf_tensor("thstg", [128, D], F32))
    th_cols = p2_stack.enter_context(nc.sbuf_tensor("th_cols", [128, 96], F32))
    th_prm = p2_stack.enter_context(nc.sbuf_tensor("th_prm", [128, 7 * NC16], F32))
    th_d1bT = p2_stack.enter_context(nc.sbuf_tensor("th_d1bT", [128, 8], F32))
    th_pad = p2_stack.enter_context(nc.sbuf_tensor("th_pad", [128, 128], F32))
    th_sb128 = p2_stack.enter_context(nc.sbuf_tensor("th_sb128", [128, 128], F32))
    th_sdump = p2_stack.enter_context(nc.sbuf_tensor("th_sdump", [128, 16], F32))
    tb_sb = p2_stack.enter_context(nc.sbuf_tensor("tb_sb", [128, 128], F32))
    tn_sb = p2_stack.enter_context(nc.sbuf_tensor("tn_sb", [128, 128], F32))
    tnb_sb = p2_stack.enter_context(nc.sbuf_tensor("tnb_sb", [128, 128], BF16))
    hb_sb = p2_stack.enter_context(nc.sbuf_tensor("hb_sb", [128, 64], BF16))
    xraw_sb = p2_stack.enter_context(nc.sbuf_tensor("xraw_sb", [128, 128], F32))
    xnb_sb = p2_stack.enter_context(nc.sbuf_tensor("xnb_sb", [128, 128], BF16))
    arb_sb = p2_stack.enter_context(nc.sbuf_tensor("arb_sb", [128, 128], F32))
    bsel_sb = p2_stack.enter_context(nc.sbuf_tensor("bsel_sb", [128, 8], F32))

    sp0 = p2_stack.enter_context(nc.psum_tensor("sp0", [128, 1024], F32))
    sp1 = p2_stack.enter_context(nc.psum_tensor("sp1", [128, 1024], F32))
    o_ps = p2_stack.enter_context(nc.psum_tensor("o_ps", [128, TQ], F32))
    sm_ps = p2_stack.enter_context(nc.psum_tensor("sm_ps", [128, TQ], F32))
    op_ps = p2_stack.enter_context(nc.psum_tensor("op_ps", [128, TQ], F32))
    th_ps = p2_stack.enter_context(nc.psum_tensor("th_ps", [128, 384], F32))

    P.wait("sync", e_p1, P1_MM)
    P.wait("sync", e_evA, P1_EVA)
    P.wait("sync", e_evB, P1_EVB)
    ld(th_prm[:, 0:16], ingT_h.ap(), sem=d_csC)
    ld(th_prm[:, 16:32], inbT_h.ap(), sem=d_csC)
    ld(th_prm[:, 32:48], pngT_h.ap(), sem=d_csC)
    ld(th_prm[:, 48:64], pnbT_h.ap(), sem=d_csC)
    ld(th_prm[:, 64:80], aggbT_h.ap(), sem=d_csC)
    ld(th_prm[:, 80:96], d2bT_h.ap(), sem=d_csC)
    ld(th_prm[:, 96:112], bcbT_h.ap(), sem=d_csC)
    ld(th_d1bT[:], d1bT_h.ap(), sem=d_csC)
    ld(bsel_sb[:], bsel_h.ap().unsqueeze(0).partition_broadcast(128), sem=d_csC)
    TH_PRM_LDS = d_csC.n

    P.wait("gpsimd", e_p1, P1_MM)
    P.wait("gpsimd", e_evA, P1_EVA)
    P.wait("gpsimd", e_evB, P1_EVB)
    wo_src = wo_h.ap().rearrange("(c p) f -> p c f", p=128)
    wo_dst = wo_sb.ap().rearrange("p (c f) -> p c f", f=D)
    for c in range(NC16):
        d_wo.n += 16
        h = d_wo.h
        P.emit("gpsimd", lambda e, c=c, h=h: e.dma_start(
            wo_dst[:, c, :], wo_src[:, c, :]).then_inc(h, 16))

    if debug:
        dbg_ot = nc.dram_tensor("dbg_ot", [NH, 128, TQ], BF16, kind="ExternalOutput")
        dbg_sums = nc.dram_tensor("dbg_sums", [NH, 128, TQ], F32, kind="ExternalOutput")
        dbg_recip = nc.dram_tensor("dbg_recip", [NH, 128, TQ], F32, kind="ExternalOutput")
        dbg_r1 = nc.dram_tensor("dbg_r1", [NSUB, 128, D], F32, kind="ExternalOutput")
        dbg_lnv = nc.dram_tensor("dbg_lnv", [NSUB, 128, 32], F32, kind="ExternalOutput")
        d_dbg = S("d_dbg")

    tile_order = [0, NT - 1] + list(range(1, NT - 1))

    NB = 8
    C_SU, C_SQ2, C_RS2, C_NM2 = 0, 8, 16, 24
    R_MU, R_VAR, R_RS, R_NM, R_TMP = 40, 48, 56, 64, 72

    def th_step(eng, fn, waits=()):
        for sem, val in waits:
            P.wait(eng, sem, val)
        e_TH.n += 1
        h = e_TH.h
        if eng == "tensor":
            P.emit(eng, lambda e, fn=fn, h=h: fn(e).then_inc(h, 1))
        else:
            def run(e, fn=fn, h=h):
                fn(e)
                e.drain().then_inc(h, 1)
            P.emit(eng, run)

    def th_ln(src, dst_bf, g_off, b_off, eps_c):
        src3 = src.ap().rearrange("p (c b) -> p c b", b=NB)
        tn3 = tn_sb.ap().rearrange("p (c b) -> p c b", b=NB)
        for b in range(NB):
            th_step("vector", lambda e, b=b: e.reduce_sum(
                th_cols[:, C_SU + b : C_SU + b + 1], src3[:, :, b], axis=AX.X)
                and e.drain(),
                waits=([(e_TH, e_TH.n)] if b == 0 else ()))
        for b in range(NB):
            th_step("scalar", lambda e, b=b: e.activation(
                th_sdump[:], src3[:, :, b], AF.Square, bias=0.0, scale=1.0,
                accum_out=th_cols[:, C_SQ2 + b : C_SQ2 + b + 1]) and e.drain(),
                waits=([(e_TH, e_TH.n)] if b == 0 else ()))
        th_step("tensor", lambda e: e.matmul(
            th_ps.ap()[0:1, 256:272], onescol[:], th_cols[:, 0:16],
            start=True, stop=True), waits=[(e_TH, e_TH.n)])
        th_step("vector", lambda e: e.tensor_scalar(
            th_cols[0:1, R_MU : R_MU + 16], th_ps.ap()[0:1, 256:272],
            1.0 / D, None, ALU.mult), waits=[(e_TH, e_TH.n)])
        th_step("vector", lambda e: e.tensor_tensor(
            th_cols[0:1, R_TMP : R_TMP + 8], th_cols[0:1, R_MU : R_MU + 8],
            th_cols[0:1, R_MU : R_MU + 8], ALU.mult))
        th_step("vector", lambda e: e.tensor_tensor(
            th_cols[0:1, R_VAR : R_VAR + 8], th_cols[0:1, R_VAR : R_VAR + 8],
            th_cols[0:1, R_TMP : R_TMP + 8], ALU.subtract))
        th_step("scalar", lambda e, eps_c=eps_c: e.activation(
            th_cols[0:1, R_RS : R_RS + 8], th_cols[0:1, R_VAR : R_VAR + 8],
            AF.Ln, bias=eps_c, scale=1.0), waits=[(e_TH, e_TH.n)])
        th_step("scalar", lambda e: e.activation(
            th_cols[0:1, R_RS : R_RS + 8], th_cols[0:1, R_RS : R_RS + 8],
            AF.Exp, bias=0.0, scale=cNHr))
        th_step("vector", lambda e: e.scalar_tensor_tensor(
            th_cols[0:1, R_NM : R_NM + 8], th_cols[0:1, R_MU : R_MU + 8],
            C_M1, th_cols[0:1, R_RS : R_RS + 8], ALU.mult, ALU.mult),
            waits=[(e_TH, e_TH.n)])
        th_step("tensor", lambda e: e.matmul(
            th_ps.ap()[:, 272:288], onesrow[:], th_cols[0:1, R_RS : R_RS + 16],
            start=True, stop=True), waits=[(e_TH, e_TH.n)])
        th_step("vector", lambda e: e.tensor_copy(
            th_cols[:, C_RS2 : C_RS2 + 16], th_ps.ap()[:, 272:288]),
            waits=[(e_TH, e_TH.n)])
        for b in range(NB):
            th_step("scalar", lambda e, b=b: e.activation(
                tn3[:, :, b], src3[:, :, b], AF.Identity,
                bias=th_cols[:, C_NM2 + b : C_NM2 + b + 1],
                scale=th_cols[:, C_RS2 + b : C_RS2 + b + 1]),
                waits=([(e_TH, e_TH.n)] if b == 0 else ()))
        for ci in range(NC16):
            th_step("scalar", lambda e, ci=ci, dst_bf=dst_bf: e.activation(
                dst_bf[:, ci * 8 : ci * 8 + 8], tn_sb[:, ci * 8 : ci * 8 + 8],
                AF.Identity, bias=th_prm[:, b_off + ci : b_off + ci + 1],
                scale=th_prm[:, g_off + ci : g_off + ci + 1]),
                waits=([(e_TH, e_TH.n)] if ci == 0 else ()))

    slab_i = [0]
    slab_war = {0: 0, 1: 0}

    def th_slab(src_fn, mm_fn, width=D, dst3d=None):
        slot = slab_i[0] % 2
        slab_i[0] += 1
        sem = d_thr[slot]
        P.wait("gpsimd", e_TH, slab_war[slot])
        sem.n += 16
        h = sem.h
        dst = thw_ring[:, slot * D : slot * D + width]
        dd = dst if dst3d is None else dst.rearrange("p (c f) -> p c f",
                                                     f=dst3d)
        P.emit("gpsimd", lambda e, dd=dd, src=src_fn(), h=h:
               e.dma_start(dd, src).then_inc(h, 16))
        P.wait("tensor", sem, sem.n)
        P.wait("tensor", e_TH, e_TH.n)
        mm_fn(dst)
        slab_war[slot] = e_TH.n

    def f_ag():
        for eng in ("vector", "scalar", "tensor"):
            P.wait(eng, d_csC, TH_PRM_LDS)
        P.wait("gpsimd", d_x1F, 16 * 4)
        P.wait("gpsimd", d_x1R, 16 * 4)
        d_arb.n += 16
        ha0 = d_arb.h
        P.emit("gpsimd", lambda e, ha0=ha0: e.dma_start(
            agin_h.ap(), x1_h.ap()[T - 1 : T, :]).then_inc(ha0, 16))
        P.wait("gpsimd", d_arb, d_arb.n)
        e_CC.n += 1
        hc = e_CC.h
        P.emit("gpsimd", lambda e, hc=hc: e.collective_compute(
            "AllGather", ALU.bypass, replica_groups=[list(range(8))],
            ins=[agin_h.ap().opt()],
            outs=[agout_h.ap().opt()]).then_inc(hc, 1))
        P.wait("gpsimd", e_CC, e_CC.n)
        d_arb.n += 16
        ha = d_arb.h
        P.emit("gpsimd", lambda e, ha=ha: e.dma_start(
            thstg[0:8, :], agout_h.ap()).then_inc(ha, 16))

    def f_tx():
        for ci in range(NC16):
            th_step("tensor", lambda e, ci=ci: e.transpose(
                th_ps.ap()[:, 128:256], thstg[:, ci * 128 : (ci + 1) * 128],
                ident_f32[:]), waits=[(d_arb, d_arb.n), (e_TH, e_TH.n)])
            th_step("vector", lambda e, ci=ci: e.tensor_copy(
                xraw_sb[:, ci * 8 : ci * 8 + 8], th_ps.ap()[:, 128:136]),
                waits=[(e_TH, e_TH.n)])

    def f_agg(ocs, w_h, mov_sb):
        for oc in ocs:
            def mm(slab, oc=oc):
                hTH = e_TH.h
                for ci in range(NC16):
                    def m(e, oc=oc, ci=ci, slab=slab,
                          last=(ci == NC16 - 1)):
                        ins = e.matmul(
                            th_ps.ap()[:, 128 + oc * 8 : 128 + oc * 8 + 8],
                            slab[:, ci * 128 : (ci + 1) * 128],
                            mov_sb[:, ci * 8 : ci * 8 + 8],
                            start=(ci == 0), stop=(ci == NC16 - 1))
                        if last:
                            ins.then_inc(hTH, 1)
                    P.emit("tensor", m)
                e_TH.n += 1
            th_slab(lambda oc=oc: w_h.ap()[:, oc * 128 : (oc + 1) * 128]
                    .rearrange("(c p) f -> p c f", p=128), mm, dst3d=128)

    def f_agg_ev():
        for ci in range(NC16):
            th_step("scalar", lambda e, ci=ci: e.activation(
                tb_sb[:, ci * 8 : ci * 8 + 8],
                th_ps.ap()[:, 128 + ci * 8 : 128 + ci * 8 + 8],
                AF.Identity, bias=th_prm[:, 64 + ci : 64 + ci + 1],
                scale=1.0), waits=([(e_TH, e_TH.n)] if ci == 0 else ()))

    def f_d1(js):
        for hjl in js:
            def mm(slab, hjl=hjl):
                hTH = e_TH.h
                for ci in range(NC16):
                    def m(e, ci=ci, hjl=hjl, slab=slab,
                          last=(ci == NC16 - 1)):
                        ins = e.matmul(
                            th_ps.ap()[:, hjl * 8 : hjl * 8 + 8],
                            slab[:, ci * 128 : (ci + 1) * 128],
                            tnb_sb[:, ci * 8 : ci * 8 + 8],
                            start=(ci == 0), stop=(ci == NC16 - 1))
                        if last:
                            ins.then_inc(hTH, 1)
                    P.emit("tensor", m)
                e_TH.n += 1
            th_slab(lambda hjl=hjl: d1s_h.ap()[hjl], mm)
            th_step("scalar", lambda e, hjl=hjl: e.activation(
                hb_sb[:, hjl * 8 : hjl * 8 + 8],
                th_ps.ap()[:, hjl * 8 : hjl * 8 + 8], AF.Gelu,
                bias=th_d1bT[:, hjl : hjl + 1], scale=1.0),
                waits=[(e_TH, e_TH.n)])

    def f_d2(ocs):
        for oc in ocs:
            def mm(slab, oc=oc):
                hTH = e_TH.h
                for hjl in range(8):
                    def m(e, oc=oc, hjl=hjl, slab=slab, last=(hjl == 7)):
                        ins = e.matmul(
                            th_ps.ap()[:, 128 + oc * 8 : 128 + oc * 8 + 8],
                            slab[:, hjl * 128 : (hjl + 1) * 128],
                            hb_sb[:, hjl * 8 : hjl * 8 + 8],
                            start=(hjl == 0), stop=(hjl == 7))
                        if last:
                            ins.then_inc(hTH, 1)
                    P.emit("tensor", m)
                e_TH.n += 1
            th_slab(lambda oc=oc: d2s_h.ap()[:, oc * 128 : (oc + 1) * 128]
                    .rearrange("(h p) f -> p h f", p=128), mm,
                    width=1024, dst3d=128)

    def f_ar(k):
        th_step("vector", lambda e: e.tensor_copy(
            arb_sb[:], th_ps.ap()[:, 128:256]), waits=[(e_TH, e_TH.n)])
        P.wait("gpsimd", e_TH, e_TH.n)
        d_arb.n += 16
        ha = d_arb.h
        P.emit("gpsimd", lambda e, k=k, ha=ha: e.dma_start(
            arin_h[k].ap(), arb_sb[:]).then_inc(ha, 16))
        P.wait("gpsimd", d_arb, d_arb.n)
        e_CC.n += 1
        hc = e_CC.h
        P.emit("gpsimd", lambda e, k=k, hc=hc: e.collective_compute(
            "AllReduce", ALU.add, replica_groups=[list(range(8))],
            ins=[arin_h[k].ap().opt()],
            outs=[arout_h[k].ap().opt()]).then_inc(hc, 1))
        P.wait("gpsimd", e_CC, e_CC.n)
        d_arb.n += 16
        P.emit("gpsimd", lambda e, k=k, ha=ha: e.dma_start(
            arb_sb[:], arout_h[k].ap()).then_inc(ha, 16))

    def f_add():
        for ci in range(NC16):
            th_step("scalar", lambda e, ci=ci: e.activation(
                arb_sb[:, ci * 8 : ci * 8 + 8],
                arb_sb[:, ci * 8 : ci * 8 + 8],
                AF.Identity, bias=th_prm[:, 80 + ci : 80 + ci + 1],
                scale=1.0),
                waits=([(d_arb, d_arb.n), (e_TH, e_TH.n)] if ci == 0 else ()))
        th_step("vector", lambda e: e.tensor_tensor(
            tb_sb[:], tb_sb[:], arb_sb[:, 0:128], ALU.add),
            waits=[(e_TH, e_TH.n)])

    def f_thb():
        th_step("scalar", lambda e: e.copy(tnb_sb[:], tb_sb[:]),
                waits=[(e_TH, e_TH.n)])

    def f_bsel():
        arb3 = arb_sb.ap().rearrange("p (c b) -> p c b", b=NB)
        for ci in range(NC16):
            th_step("scalar", lambda e, ci=ci: e.activation(
                arb_sb[:, ci * 8 : ci * 8 + 8],
                th_ps.ap()[:, 128 + ci * 8 : 128 + ci * 8 + 8],
                AF.Identity, bias=th_prm[:, 96 + ci : 96 + ci + 1],
                scale=1.0), waits=([(e_TH, e_TH.n)] if ci == 0 else ()))
        th_step("vector", lambda e, arb3=arb3: e.tensor_tensor(
            arb3, arb3,
            bsel_sb.ap().unsqueeze(1).broadcast_to([128, NC16, NB]),
            ALU.mult), waits=[(e_TH, e_TH.n)])
        th_step("vector", lambda e, arb3=arb3: e.tensor_reduce(
            th_cols[:, 0:16], arb3, AX.X, ALU.add))
        th_step("vector", lambda e: e.memset(th_pad[:], 0.0))
        th_step("vector", lambda e: e.tensor_copy(
            th_pad[:, 0:16], th_cols[:, 0:16]))
        th_step("tensor", lambda e: e.transpose(
            th_ps.ap()[:, 128:256], th_pad[:], ident_f32[:]),
            waits=[(e_TH, e_TH.n)])
        th_step("vector", lambda e: e.tensor_copy(
            th_sb128[:], th_ps.ap()[:, 128:256]), waits=[(e_TH, e_TH.n)])
        P.wait("gpsimd", e_TH, e_TH.n)
        d_bcr.n += 16
        hbc = d_bcr.h
        P.emit("gpsimd", lambda e: e.dma_start(
            bcrow_h.ap().rearrange("(c f) -> c f", f=128),
            th_sb128[0:16, :]).then_inc(hbc, 16))

    fragments = []

    def piece(fn):
        fragments.append(fn)

    d_dmp = S("d_dmp")

    def th_dump(slot, src):
        def fr():
            P.wait("gpsimd", e_TH, e_TH.n)
            d_dmp.n += 16
            hd = d_dmp.h
            P.emit("gpsimd", lambda e, slot=slot, src=src, hd=hd: e.dma_start(
                dbg_th_h.ap()[slot], src).then_inc(hd, 16))
        return fr

    piece(f_ag)
    piece(f_tx)
    piece(th_dump(0, xraw_sb[:]))
    piece(lambda: th_ln(xraw_sb, xnb_sb, 0, 16, cEPS6r))
    piece(th_dump(1, xnb_sb[:]))
    for g_ in range(8):
        piece(lambda g_=g_: f_agg(range(2 * g_, 2 * g_ + 2), aggw_h, xnb_sb))
    piece(f_agg_ev)
    piece(th_dump(2, tb_sb[:]))
    for _k in range(4):
        piece(lambda: th_ln(tb_sb, tnb_sb, 32, 48, cEPS6r))
        for g_ in range(4):
            piece(lambda g_=g_: f_d1(range(2 * g_, 2 * g_ + 2)))
        for g_ in range(4):
            piece(lambda g_=g_: f_d2(range(4 * g_, 4 * g_ + 4)))
        piece(lambda _k=_k: f_ar(_k))
        piece(f_add)
        piece(th_dump(3 + _k, tb_sb[:]))
    piece(f_thb)
    for g_ in range(8):
        piece(lambda g_=g_: f_agg(range(2 * g_, 2 * g_ + 2), bcw_h, tnb_sb))
    piece(f_bsel)
    piece(th_dump(7, th_pad[:]))

    pair_i = [0]

    def emit_qt_loads(ti, tile):
        tstart = tile * TQ
        P.wait("sync", d_st, 96 * 4 * (tile + 1))
        for qh in range(NH):
            if ti >= 1:
                P.wait("sync", e_QH, (ti - 1) * NH + qh + 1)
            d_qt[qh].n += 16
            h = d_qt[qh].h
            P.emit("sync", lambda e, qh=qh, tstart=tstart, h=h: e.dma_start(
                qt_ring[:, qh * TQ : (qh + 1) * TQ],
                q_h.ap()[tstart : tstart + TQ, qh * 128 : (qh + 1) * 128],
                transpose=True).then_inc(h, 16))

    def emit_kv_loads(ti, tile):
        tstart = tile * TQ
        keys = tstart + TQ
        nblk = keys // 128
        for kvh in range(NKV):
            slot = ti * NKV + kvh
            par = slot % 2
            if slot >= 2:
                P.wait("sync", e_KV, slot - 1)
            d_kv[par].n += 32
            h = d_kv[par].h
            P.emit("sync", lambda e, kvh=kvh, par=par, keys=keys, h=h: e.dma_start(
                kt_buf[:, par * T : par * T + keys],
                k_h.ap()[0:keys, kvh * 128 : (kvh + 1) * 128],
                transpose=True).then_inc(h, 16))
            P.emit("sync", lambda e, kvh=kvh, par=par, nblk=nblk, h=h: e.dma_start(
                v_buf.ap()[:, par * T : par * T + nblk * 128].rearrange(
                    "p (b f) -> p b f", f=128),
                v_h.ap()[kvh, 0:nblk].rearrange("b p f -> p b f")).then_inc(h, 16))

    emit_qt_loads(0, tile_order[0])
    emit_kv_loads(0, tile_order[0])

    for ti, tile in enumerate(tile_order):
        tstart = tile * TQ
        nch = (tstart + TQ) // 128
        if ti + 1 < NT:
            emit_qt_loads(ti + 1, tile_order[ti + 1])
            emit_kv_loads(ti + 1, tile_order[ti + 1])

        for kvh in range(NKV):
            kvslot = ti * NKV + kvh
            par_kv = kvslot % 2
            P.wait("tensor", d_kv[kvslot % 2], 32 * (kvslot // 2 + 1))
            for qh4 in range(4):
                qh = kvh * 4 + qh4
                qidx = ti * NH + qh
                P.wait("tensor", d_qt[qh], 16 * (ti + 1))
                npairs = nch // 2
                pid0 = pair_i[0]

                def emit_S(p):
                    c0, c1 = 2 * p, 2 * p + 1
                    a0 = max(0, c0 * 128 - tstart)
                    pid = pid0 + p
                    par = pid % 2
                    sp = sp0 if par == 0 else sp1
                    P.wait("tensor", e_E, pid - 1)
                    hS = e_S.h
                    for hh, c in ((0, c0), (1, c1)):
                        def mmS(e, sp=sp, hh=hh, c=c, a0=a0, par_kv=par_kv,
                                qh=qh, last=(hh == 1)):
                            ins = e.matmul(
                                sp.ap()[:, hh * 512 + a0 : hh * 512 + 512],
                                kt_buf[:, par_kv * T + c * 128 : par_kv * T + (c + 1) * 128],
                                qt_ring[:, qh * TQ + a0 : (qh + 1) * TQ],
                                start=True, stop=True)
                            if last:
                                ins.then_inc(hS, 1)
                        P.emit("tensor", mmS)
                    e_S.n += 1
                    ptb = pt_buf[:, par * 1024 : par * 1024 + 1024]
                    P.wait("scalar", e_S, e_S.n)
                    P.wait("scalar", e_AV, pid - 1)
                    e_E.n += 1
                    hE = e_E.h
                    P.emit("scalar", lambda e, sp=sp, ptb=ptb, a0=a0:
                           e.activation(
                               ptb.rearrange("p (h f) -> p h f", f=512)[:, :, a0:512],
                               sp.ap().rearrange("p (h f) -> p h f", f=512)[:, :, a0:512],
                               AF.Exp, bias=cM40, scale=1.0).then_inc(hE, 1))
                    diag = [(hh, c) for hh, c in ((0, c0), (1, c1))
                            if 0 <= c * 128 - tstart < TQ]
                    if diag:
                        P.wait("vector", e_E, e_E.n)
                        e_M.n += 1
                        hM = e_M.h
                        for idx, (hh, c) in enumerate(diag):
                            doff = c * 128 - tstart
                            def mk(e, hh=hh, doff=doff, ptb=ptb,
                                   last=(idx == len(diag) - 1)):
                                ins = e.tensor_tensor(
                                    ptb[:, hh * 512 + doff : hh * 512 + doff + 128],
                                    ptb[:, hh * 512 + doff : hh * 512 + doff + 128],
                                    trimask[:], ALU.mult)
                                if last:
                                    ins.then_inc(hM, 1)
                            P.emit("vector", mk)
                        return ("M", e_M.n)
                    return ("E", e_E.n)

                def emit_AV(p, gate):
                    c0, c1 = 2 * p, 2 * p + 1
                    a0 = max(0, c0 * 128 - tstart)
                    a1 = max(0, c1 * 128 - tstart)
                    pid = pid0 + p
                    par = pid % 2
                    ptb = pt_buf[:, par * 1024 : par * 1024 + 1024]
                    if gate[0] == "M":
                        P.wait("tensor", e_M, gate[1])
                    else:
                        P.wait("tensor", e_E, gate[1])
                    if p == 0:
                        P.wait("tensor", e_OE, qidx)
                        P.wait("tensor", e_SE, qidx)
                    for hh, c, ac in ((0, c0, a0), (1, c1, a1)):
                        P.emit("tensor", lambda e, c=c, ac=ac, hh=hh, ptb=ptb,
                               par_kv=par_kv, nch=nch:
                               e.matmul(o_ps.ap()[:, ac:512],
                                        v_buf[:, par_kv * T + c * 128 : par_kv * T + (c + 1) * 128],
                                        ptb[:, hh * 512 + ac : hh * 512 + 512],
                                        start=(c == 0), stop=(c == nch - 1)))
                    e_AV.n += 1
                    hAV = e_AV.h
                    is_last_pair = p == npairs - 1
                    for hh, c, ac in ((0, c0, a0), (1, c1, a1)):
                        def mmO(e, c=c, ac=ac, hh=hh, ptb=ptb, last=(hh == 1),
                                ilp=is_last_pair, q4=qh4, nch=nch):
                            ins = e.matmul(sm_ps.ap()[:, ac:512],
                                           ones2d[:],
                                           ptb[:, hh * 512 + ac : hh * 512 + 512],
                                           start=(c == 0), stop=(c == nch - 1))
                            if last:
                                ins.then_inc(hAV, 1)
                                if ilp:
                                    e.nop().then_inc(e_QH.h, 1)
                                    if q4 == 3:
                                        e.nop().then_inc(e_KV.h, 1)
                        P.emit("tensor", mmO)
                    if is_last_pair:
                        e_QH.n += 1
                        if qh4 == 3:
                            e_KV.n += 1

                gates = {}
                gates[0] = emit_S(0)
                for p in range(npairs):
                    if p + 1 < npairs:
                        gates[p + 1] = emit_S(p + 1)
                    emit_AV(p, gates.pop(p))
                pair_i[0] += npairs
                P.wait("vector", e_AV, e_AV.n)
                if ti >= 1:
                    P.wait("vector", e_OP, ti * 16)
                e_SE.n += 1
                hSE = e_SE.h
                P.emit("vector", lambda e: e.reciprocal(
                    recip_sb[:], sm_ps.ap()[:]).then_inc(hSE, 1))
                e_OE.n += 1
                hOE = e_OE.h
                P.emit("vector", lambda e, qh=qh: e.scalar_tensor_tensor(
                    ot_sb[:, qh * TQ : (qh + 1) * TQ], o_ps.ap()[:],
                    0.02209708691207961, recip_sb[:],
                    ALU.mult, ALU.mult).then_inc(hOE, 1))
                if ti >= 2 and fragments:
                    fragments.pop(0)()
                if debug and ti == 0:
                    P.wait("sync", e_OE, e_OE.n)
                    d_dbg.n += 16
                    hdbg = d_dbg.h
                    P.emit("sync", lambda e, qh=qh: e.dma_start(
                        dbg_ot.ap()[qh], ot_sb[:, qh * TQ : (qh + 1) * TQ]).then_inc(hdbg, 16))

        P.wait("tensor", d_wo, 16 * NC16)
        _ = None
        for s in range(4):
            g = tile * 4 + s
            gi = ti * 4 + s
            par_x = gi % 2
            P.wait("scalar", e_OPE, 4 * gi)
            d_x.n += 16
            hx = d_x.h
            P.emit("scalar", lambda e, g=g, hx=hx: e.dma_start(
                x_sb[:],
                x_h.ap()[g * 128 : (g + 1) * 128, :]).then_inc(hx, 16))
            for oc in range(4):
                qtr = gi * 4 + oc
                P.wait("tensor", e_OE, (ti + 1) * NH)
                P.wait("tensor", e_OPE, qtr)
                for dc in range(NC16):
                    P.emit("tensor", lambda e, dc=dc, s=s, oc=oc:
                           e.matmul(op_ps.ap()[:],
                                    ot_sb[:, dc * TQ + s * 128 : dc * TQ + (s + 1) * 128],
                                    wo_sb[:, dc * D + oc * 512 : dc * D + (oc + 1) * 512],
                                    start=(dc == 0), stop=False))
                e_OP.n += 1
                hOP = e_OP.h
                P.emit("tensor", lambda e, oc=oc: e.matmul(
                    op_ps.ap()[:], ones1[:], bo_sb[:, oc * 512 : (oc + 1) * 512],
                    start=False, stop=True).then_inc(hOP, 1))
                P.wait("vector", e_OP, e_OP.n)
                P.wait("vector", d_x, 16 * (gi + 1))
                P.wait("vector", d_x1F, 16 * min(gi, 4))
                P.wait("vector", d_x1R, 16 * max(0, gi - 4))
                P.wait("vector", d_x1b, 16 * gi)
                e_OPE.n += 1
                hOPE = e_OPE.h
                P.emit("vector", lambda e, oc=oc: e.scalar_tensor_tensor(
                    x1_sb[:, oc * 512 : (oc + 1) * 512], op_ps.ap()[:], 1.0,
                    x_sb[:, oc * 512 : (oc + 1) * 512],
                    ALU.mult, ALU.add,
                    accum_out=lnv[:, L_R1 + oc : L_R1 + oc + 1]).then_inc(hOPE, 1))
            if debug:
                P.wait("scalar", e_OPE, e_OPE.n)
                d_dbg.n += 16
                hdbg2 = d_dbg.h
                P.emit("scalar", lambda e, g=g, hdbg2=hdbg2: e.dma_start(
                    dbg_r1.ap()[g], x1_sb[:]).then_inc(hdbg2, 16))
                P.wait("scalar", d_dbg, d_dbg.n)
            P.wait("scalar", e_OPE, e_OPE.n)
            e_LN.n += 1
            hLN = e_LN.h
            P.emit("scalar", lambda e: e.activation(
                x_sb[:], x1_sb[:], AF.Square, bias=0.0, scale=1.0,
                accum_out=lnv[:, L_SQ : L_SQ + 1]))
            P.emit("scalar", lambda e: e.drain().then_inc(hLN, 1))
            P.wait("vector", e_LN, e_LN.n)
            e_LN.n += 1
            def _ln1v(e):
                e.drain()
                e.tensor_reduce(lnv[:, L_SUM : L_SUM + 1],
                                lnv[:, L_R1 : L_R1 + 4], AX.X, ALU.add)
                e.drain()
                e.tensor_scalar(lnv[:, L_MU : L_MU + 1],
                                lnv[:, L_SUM : L_SUM + 1], 1.0 / D, None, ALU.mult)
                e.tensor_scalar(lnv[:, L_VAR : L_VAR + 1],
                                lnv[:, L_SQ : L_SQ + 1], 1.0 / D, None, ALU.mult)
                e.drain()
                e.tensor_tensor(lnv[:, L_TMP : L_TMP + 1],
                                lnv[:, L_MU : L_MU + 1],
                                lnv[:, L_MU : L_MU + 1], ALU.mult)
                e.drain()
                e.tensor_tensor(lnv[:, L_VAR : L_VAR + 1],
                                lnv[:, L_VAR : L_VAR + 1],
                                lnv[:, L_TMP : L_TMP + 1], ALU.subtract)
                e.drain().then_inc(hLN, 1)
            P.emit("vector", _ln1v)
            P.wait("scalar", e_LN, e_LN.n)
            e_LN.n += 1
            def _ln1s(e):
                e.activation(lnv[:, L_RSTD : L_RSTD + 1],
                             lnv[:, L_VAR : L_VAR + 1], AF.Ln, bias=cEPS5, scale=1.0)
                e.drain()
                e.activation(lnv[:, L_RSTD : L_RSTD + 1],
                             lnv[:, L_RSTD : L_RSTD + 1], AF.Exp, bias=0.0, scale=cNH)
                e.drain().then_inc(hLN, 1)
            P.emit("scalar", _ln1s)
            P.wait("vector", e_LN, e_LN.n)
            e_LN.n += 1
            def _ln1n(e):
                e.scalar_tensor_tensor(lnv[:, L_NMR : L_NMR + 1],
                                       lnv[:, L_MU : L_MU + 1], C_M1,
                                       lnv[:, L_RSTD : L_RSTD + 1], ALU.mult, ALU.mult)
                e.drain().then_inc(hLN, 1)
            P.emit("vector", _ln1n)
            P.wait("scalar", e_LN, e_LN.n)
            if debug:
                d_dbg.n += 16
                hdbg3 = d_dbg.h
                P.emit("scalar", lambda e, g=g, hdbg3=hdbg3: e.dma_start(
                    dbg_lnv.ap()[g], lnv[:]).then_inc(hdbg3, 16))
                P.wait("scalar", d_dbg, d_dbg.n)
            e_LN.n += 1
            P.emit("scalar", lambda e: e.activation(
                x1_sb[:], x1_sb[:], AF.Identity, bias=lnv[:, L_NMR : L_NMR + 1],
                scale=lnv[:, L_RSTD : L_RSTD + 1]).then_inc(hLN, 1))
            P.wait("scalar", e_LN, e_LN.n)
            sx1 = d_x1F if gi < 4 else d_x1R
            sx1.n += 16
            hx1 = sx1.h
            P.emit("scalar", lambda e, g=g, hx1=hx1: e.dma_start(
                x1_h.ap()[g * 128 : (g + 1) * 128, :], x1_sb[:]).then_inc(hx1, 16))
            P.wait("gpsimd", e_LN, e_LN.n)
            d_x1b.n += 16
            hx1b = d_x1b.h
            P.emit("gpsimd", lambda e, g=g: e.dma_start(
                x1b_h.ap()[g * 128 : (g + 1) * 128, :], x1_sb[:]).then_inc(hx1b, 16))

    while fragments:
        fragments.pop(0)()
    P.wait("gpsimd", d_dmp, d_dmp.n)

    P2_OP = e_OP.n
    P2_OPE = e_OPE.n
    P2_AV = e_AV.n
    P2_OE = e_OE.n
    P2_SE = e_SE.n
    P2_LN = e_LN.n
    TH_ALL = e_TH.n
    p2_stack.close()

    p3_stack = contextlib.ExitStack()
    gw_sb = p3_stack.enter_context(nc.sbuf_tensor("gw_sb", [128, NC16 * D], BF16))
    x1t_ring = p3_stack.enter_context(
        nc.sbuf_tensor("x1t_ring", [128, 2 * NC16 * TQ], BF16))
    bc_bc = p3_stack.enter_context(nc.sbuf_tensor("bc_bc", [128, D], F32))
    outg_bc = p3_stack.enter_context(nc.sbuf_tensor("outg_bc", [128, D], F32))
    outb_bc = p3_stack.enter_context(nc.sbuf_tensor("outb_bc", [128, D], F32))
    x1l_sb = p3_stack.enter_context(nc.sbuf_tensor("x1l_sb", [128, 2 * D], F32))
    g_sb2 = p3_stack.enter_context(nc.sbuf_tensor("g_sb2", [128, 2 * D], F32))
    o3_sb2 = p3_stack.enter_context(nc.sbuf_tensor("o3_sb2", [128, 2 * D], F32))
    st3 = p3_stack.enter_context(nc.sbuf_tensor("st3", [128, 64], F32))
    g0_ps = p3_stack.enter_context(nc.psum_tensor("g0_ps", [128, TQ], F32))
    g1_ps = p3_stack.enter_context(nc.psum_tensor("g1_ps", [128, TQ], F32))
    gb_sb = p3_stack.enter_context(nc.sbuf_tensor("gb_sb", [1, D], BF16))

    for eng in ("sync", "gpsimd", "scalar"):
        P.wait(eng, e_OP, P2_OP)
        P.wait(eng, e_OPE, P2_OPE)
        P.wait(eng, e_AV, P2_AV)
        P.wait(eng, e_OE, P2_OE)
        P.wait(eng, e_SE, P2_SE)
        P.wait(eng, e_SM, e_SM.n)
        P.wait(eng, e_ACv, e_ACv.n)
        P.wait(eng, e_ACg, e_ACg.n)
        P.wait(eng, e_LN, P2_LN)
        P.wait(eng, e_TH, TH_ALL)
        for s_ in d_thr:
            P.wait(eng, s_, s_.n)
        for s_ in d_qt:
            P.wait(eng, s_, s_.n)
        P.wait(eng, d_kv[0], d_kv[0].n)
        P.wait(eng, d_kv[1], d_kv[1].n)
        P.wait(eng, d_x, d_x.n)
        P.wait(eng, d_x1F, 16 * min(NSUB, 4))
        P.wait(eng, d_x1R, 16 * max(0, NSUB - 4))
        P.wait(eng, d_x1b, 16 * NSUB)

    ld(outg_bc[:], outg_h.ap().unsqueeze(0).partition_broadcast(128), sem=d_csD)
    ld(outb_bc[:], outb_h.ap().unsqueeze(0).partition_broadcast(128), sem=d_csD)
    P3_CS = d_csD.n
    P.wait("sync", d_bcr, 16)
    d_bcb.n += 16
    hbc2 = d_bcb.h
    P.emit("sync", lambda e: e.dma_start(
        bc_bc[:], bcrow_h.ap().unsqueeze(0).partition_broadcast(128)
    ).then_inc(hbc2, 16))
    BC_READY = d_bcb.n

    ld(gb_sb[:], gb_h.ap(), sem=d_bias, q="gpsimd")
    GB_LD = d_bias.n
    P.wait("tensor", d_bias, GB_LD)
    gw_src = gw_h.ap().rearrange("(c p) f -> p c f", p=128)
    gw_dst = gw_sb.ap().rearrange("p (c f) -> p c f", f=D)
    for c in range(NC16):
        d_gw.n += 16
        h = d_gw.h
        P.emit("gpsimd", lambda e, c=c, h=h: e.dma_start(
            gw_dst[:, c, :], gw_src[:, c, :]).then_inc(h, 16))

    def x1t_slot(t, c):
        off = ((t % 2) * NC16 + c) * TQ
        return x1t_ring[:, off : off + TQ]

    F_PER_SUB = 9
    pending = []

    def flush_pending(k=None):
        n = len(pending) if k is None else min(k, len(pending))
        for _ in range(n):
            eng, waits, fns = pending.pop(0)
            for sem_, val_ in waits:
                P.wait(eng, sem_, val_)
            for fn in fns:
                P.emit(eng, fn)

    def build_finals(g):
        par = g % 2
        PB = 32 * par
        f_ = e_Fp[par]
        hF = f_.h
        base = f_.n
        gv = g_sb2[:, par * D : par * D + D]
        ov = o3_sb2[:, par * D : par * D + D]
        x1v = x1l_sb[:, par * D : par * D + D]
        bn = st3[:, PB : PB + 24]
        mu_c = st3[:, PB + 24 : PB + 25]
        var_c = st3[:, PB + 25 : PB + 26]
        mv_c = st3[:, PB + 24 : PB + 26]
        rstd_c = st3[:, PB + 26 : PB + 27]
        nmr_c = st3[:, PB + 27 : PB + 28]

        def bn_block(e, src):
            for q in range(4):
                e.bn_stats(bn[:, 6 * q : 6 * q + 6],
                           src[:, 512 * q : 512 * q + 512])
            e.drain()
            e.bn_aggr(mv_c, bn)

        def _b1(e):
            e.scalar_tensor_tensor(ov, gv, 1.0, x1v, ALU.mult, ALU.add)
            bn_block(e, ov)
            e.drain().then_inc(hF, 1)
        pending.append(("vector",
                        [(e_GE, e_GE.n), (d_csD, P3_CS),
                         (d_x1l[par], 16 * (g // 2 + 1)), (d_out, 16 * (g - 1))],
                        [_b1]))

        def _b2(e):
            e.activation(rstd_c, var_c, AF.Ln, bias=cEPS6, scale=1.0)
            e.drain()
            e.activation(rstd_c, rstd_c, AF.Exp, bias=0.0, scale=cNH)
            e.drain().then_inc(hF, 1)
        pending.append(("scalar", [(f_, base + 1)], [_b2]))

        def _b3(e):
            e.scalar_tensor_tensor(nmr_c, mu_c, C_M1, rstd_c, ALU.mult, ALU.mult)
            e.drain().then_inc(hF, 1)
        pending.append(("vector", [(f_, base + 2)], [_b3]))

        def _b4(e):
            e.activation(gv, ov, AF.Identity, bias=nmr_c,
                         scale=rstd_c).then_inc(hF, 1)
        pending.append(("scalar", [(f_, base + 3)], [_b4]))

        def _b5a(e):
            e.tensor_tensor(gv, gv, outg_bc[:], ALU.mult)
            e.tensor_tensor(gv, gv, outb_bc[:], ALU.add)
            e.drain().then_inc(hF, 1)
        pending.append(("vector", [(f_, base + 4), (d_csD, P3_CS)], [_b5a]))

        def _b5b(e):
            e.scalar_tensor_tensor(ov, gv, 1.0, x1v, ALU.mult, ALU.add)
            bn_block(e, ov)
            e.drain().then_inc(hF, 1)
        pending.append(("vector", [(f_, base + 5)], [_b5b]))

        def _b6(e):
            e.activation(rstd_c, var_c, AF.Ln, bias=cEPS5, scale=1.0)
            e.drain()
            e.activation(rstd_c, rstd_c, AF.Exp, bias=0.0, scale=cNH)
            e.drain().then_inc(hF, 1)
        pending.append(("scalar", [(f_, base + 6)], [_b6]))

        def _b7(e):
            e.scalar_tensor_tensor(nmr_c, mu_c, C_M1, rstd_c, ALU.mult, ALU.mult)
            e.drain().then_inc(hF, 1)
        pending.append(("vector", [(f_, base + 7)], [_b7]))

        def _b8(e):
            e.activation(ov, ov, AF.Identity, bias=nmr_c,
                         scale=rstd_c).then_inc(hF, 1)
        pending.append(("scalar", [(f_, base + 8)], [_b8]))

        d_out.n += 16
        hout = d_out.h

        def _b9(e):
            e.dma_start(out_h.ap()[g * 128 : (g + 1) * 128, :],
                        ov).then_inc(hout, 16)
        pending.append(("scalar", [(f_, base + 9)], [_b9]))
        f_.n += F_PER_SUB

    for t in range(NT):
        P.wait("sync", d_x1b, 16 * NSUB)
        if t >= 2:
            P.wait("sync", e_G, (t - 1) * 16)
        for c in range(NC16):
            d_x1t[t % 2].n += 16
            h = d_x1t[t % 2].h
            P.emit("sync", lambda e, t=t, c=c, h=h: e.dma_start(
                x1t_slot(t, c),
                x1b_h.ap()[t * TQ : (t + 1) * TQ, c * 128 : (c + 1) * 128],
                transpose=True).then_inc(h, 16))

        for s in range(4):
            g = t * 4 + s
            par_x = g % 2
            if g == 0:
                d_x1l[0].n += 16
                hxl = d_x1l[0].h
                P.emit("scalar", lambda e, hxl=hxl: e.dma_start(
                    x1l_sb[:, 0:D],
                    x1_h.ap()[0:128, :]).then_inc(hxl, 16))
            for oc in range(4):
                qtr = g * 4 + oc
                gps = g0_ps if qtr % 2 == 0 else g1_ps
                P.wait("tensor", d_x1t[t % 2], 16 * NC16 * (t // 2 + 1))
                P.wait("tensor", d_gw, 16 * NC16)
                if qtr >= 2:
                    P.wait("tensor", e_GE, 2 * (qtr - 2) + 1)
                for dc in range(NC16):
                    P.emit("tensor", lambda e, dc=dc, s=s, oc=oc, t=t, gps=gps:
                           e.matmul(gps.ap()[:],
                                    x1t_slot(t, dc)[:, s * 128 : (s + 1) * 128],
                                    gw_sb[:, dc * D + oc * 512 : dc * D + (oc + 1) * 512],
                                    start=(dc == 0), stop=False))
                e_G.n += 1
                hG = e_G.h
                P.emit("tensor", lambda e, oc=oc, gps=gps: e.matmul(
                    gps.ap()[:], ones1[:], gb_sb[:, oc * 512 : (oc + 1) * 512],
                    start=False, stop=True).then_inc(hG, 1))
                P.wait("scalar", e_G, e_G.n)
                if g >= 2:
                    P.wait("scalar", e_Fp[g % 2], F_PER_SUB * (g // 2 - 1) + 6)
                gv = g_sb2[:, par_x * D : par_x * D + D]
                e_GE.n += 1
                hGE = e_GE.h
                P.emit("scalar", lambda e, oc=oc, gps=gps, gv=gv, hGE=hGE:
                       e.activation(
                    gv[:, oc * 512 : (oc + 1) * 512], gps.ap()[:],
                    AF.Sigmoid, bias=0.0, scale=1.0).then_inc(hGE, 1))
                P.wait("vector", d_bcb, BC_READY)
                P.wait("vector", e_GE, e_GE.n)
                e_GE.n += 1

                def _bcq(e, oc=oc, gv=gv, hGE=hGE):
                    e.tensor_tensor(gv[:, oc * 512 : (oc + 1) * 512],
                                    gv[:, oc * 512 : (oc + 1) * 512],
                                    bc_bc[:, oc * 512 : (oc + 1) * 512],
                                    ALU.mult)
                    e.drain().then_inc(hGE, 1)
                P.emit("vector", _bcq)
                flush_pending(3 if oc < 3 else 1)
            gl = g + 1
            if gl < NSUB:
                if gl >= 2:
                    P.wait("scalar", e_Fp[gl % 2], F_PER_SUB * (gl // 2 - 1) + 6)
                d_x1l[gl % 2].n += 16
                hxl = d_x1l[gl % 2].h
                P.emit("scalar", lambda e, gl=gl, par=gl % 2, hxl=hxl: e.dma_start(
                    x1l_sb[:, par * D : par * D + D],
                    x1_h.ap()[gl * 128 : (gl + 1) * 128, :]).then_inc(hxl, 16))
            build_finals(g)

    flush_pending()
    P.wait("scalar", d_out, 16 * NSUB)
    p3_stack.close()

    with nc.Block() as block:
        @block.tensor
        def _(e):
            for fn in P.q["tensor"]:
                fn(e)

        @block.vector
        def _(e):
            for fn in P.q["vector"]:
                fn(e)

        @block.scalar
        def _(e):
            for fn in P.q["scalar"]:
                fn(e)

        @block.sync
        def _(e):
            for fn in P.q["sync"]:
                fn(e)

        @block.gpsimd
        def _(e):
            for fn in P.q["gpsimd"]:
                fn(e)

    return nc


def _perm_for_heads(nheads):
    perm = []
    for h in range(nheads):
        base = h * HD
        perm.extend(base + 2 * np.arange(64))
        perm.extend(base + 2 * np.arange(64) + 1)
    return np.array(perm)


def _colform(v):
    return np.ascontiguousarray(np.asarray(v, np.float32).reshape(NC16, 128).T)


def pack_inputs(inputs, T):
    f32 = lambda a: np.ascontiguousarray(np.asarray(a, np.float32))
    x = f32(inputs["L0_attn_x"])
    B = x.shape[0]
    wq = f32(inputs["L0_attn_Wq"])
    bq = f32(inputs["L0_attn_bq"])
    wk = f32(inputs["L0_attn_Wk"])
    bk = f32(inputs["L0_attn_bk"])
    wv = f32(inputs["L0_attn_Wv"])
    bv = f32(inputs["L0_attn_bv"])
    qperm = _perm_for_heads(NH)
    kperm = _perm_for_heads(NKV)
    wqkv = np.ascontiguousarray(
        np.concatenate([wq[:, qperm], wk[:, kperm], wv], axis=1))
    bqkv = np.ascontiguousarray(
        np.concatenate([bq[qperm], bk[kperm], bv])[None, :])

    d1 = f32(inputs["L0_ffn_d1_w"])
    d1r = np.ascontiguousarray(
        d1.reshape(NC16, 128, FF // 128, 128).transpose(2, 1, 0, 3).reshape(
            FF // 128, 128, D))

    j = np.arange(0, D, 2, dtype=np.float32) / np.float32(D)
    freqs = (np.float32(10000.0) ** (-j))[:64]
    t = np.arange(T, dtype=np.float32)
    ang = np.outer(t, freqs).astype(np.float32)

    bf = ml_dtypes.bfloat16
    common = {
        "wqkv": wqkv,
        "bqkv": bqkv,
        "wo": f32(inputs["L0_attn_Wo"]),
        "bo": f32(inputs["L0_attn_bo"]).reshape(1, D),
        "gate_w": f32(inputs["L0_ffn_gate_w"]),
        "gate_b": f32(inputs["L0_ffn_gate_b"]).reshape(1, D),
        "agg_w": f32(inputs["L0_ffn_agg_w"]),
        "bc_w": f32(inputs["L0_ffn_bc_w"]),
        "agg_bT": _colform(inputs["L0_ffn_agg_b"]),
        "d2_bT": _colform(inputs["L0_ffn_d2_b"]),
        "bc_bT": _colform(inputs["L0_ffn_bc_b"]),
        "in_gT": _colform(inputs["L0_ffn_in_gamma"]),
        "in_bT": _colform(inputs["L0_ffn_in_beta"]),
        "pn_gT": _colform(inputs["L0_ffn_pn_gamma"]),
        "pn_bT": _colform(inputs["L0_ffn_pn_beta"]),
        "out_g": f32(inputs["L0_ffn_out_gamma"]),
        "out_b": f32(inputs["L0_ffn_out_beta"]),
        "cos_t": np.cos(ang).astype(np.float32),
        "sin_t": np.sin(ang).astype(np.float32),
        "trimask": np.triu(np.ones((128, 128), np.float32)).astype(bf),
        "ones1": np.ones((1, 128), np.float32).astype(bf),
        "ones2d": np.ones((128, 128), np.float32).astype(bf),
        "ones2df": np.ones((128, 128), np.float32),
        "onescol": np.ones((128, 1), np.float32),
        "onesrow": np.ones((1, 128), np.float32),
        "ident_bf": np.eye(128, dtype=np.float32).astype(bf),
        "ident_f32": np.eye(128, dtype=np.float32),
        "consts": np.array(REG_CONSTS, np.float32),
    }
    d2 = f32(inputs["L0_ffn_d2_w"])
    d1bT = np.ascontiguousarray(
        f32(inputs["L0_ffn_d1_b"]).reshape(FF // 128, 128).T)
    in_maps = []
    for b in range(B):
        m = dict(common)
        m["x"] = np.ascontiguousarray(x[b])
        m["d1_slice"] = np.ascontiguousarray(d1r[8 * b : 8 * b + 8])
        m["d2_slice"] = np.ascontiguousarray(d2[1024 * b : 1024 * (b + 1), :])
        m["d1_bT"] = np.ascontiguousarray(d1bT[:, 8 * b : 8 * b + 8])
        sel = np.zeros(8, np.float32)
        sel[b] = 1.0
        m["bsel"] = sel
        in_maps.append(m)
    return in_maps


def kernel(**inputs):
    x = np.asarray(inputs["L0_attn_x"])
    B, T, _ = x.shape
    in_maps = pack_inputs(inputs, T)
    nc = build(T)
    res = run_bass_kernel_spmd(nc, in_maps, core_ids=list(range(B)))
    out = np.stack([np.asarray(res.results[i]["out"], np.float32)
                    for i in range(B)], axis=0)
    return out


# version 28
# speedup vs baseline: 1.1647x; 1.0008x over previous
import contextlib

import numpy as np
import ml_dtypes

import concourse.bass as bass
import concourse.mybir as mybir
from concourse.bass_utils import run_bass_kernel_spmd

F32 = mybir.dt.float32
BF16 = mybir.dt.bfloat16
AF = mybir.ActivationFunctionType
ALU = mybir.AluOpType
AX = mybir.AxisListType

D = 2048
NH = 16
NKV = 4
HD = 128
FF = 8192
TQ = 512
NC16 = D // 128
SQRT_D = 45.254833995939045

C_M40 = -40.0
C_EPS5 = 1e-5
C_EPS6 = 1e-6
C_NHALF = -0.5
C_SQD = SQRT_D
C_M1 = -1.0
C_P1 = 1.0
REG_CONSTS = [C_M40, C_EPS5, C_EPS6, C_NHALF, C_SQD, C_M1, C_P1]

L_R1, L_SUM, L_SQ, L_MU, L_VAR, L_RSTD, L_NMR, L_TMP = 0, 4, 5, 6, 7, 8, 9, 10

TC_TH = 0
TC_T = 16
TC_RSTD = 48
TC_NMR = 49
TC_SUM = 52
TC_SQ = 53
TC_MU = 54
TC_VAR = 55
TC_RS = 56
TC_NM = 57
TC_TMP = 58


class Sem:
    __slots__ = ("h", "n", "name")

    def __init__(self, stack, nc, name):
        self.h = stack.enter_context(nc.semaphore(name))
        self.n = 0
        self.name = name


class Prog:
    ENGS = ("tensor", "vector", "scalar", "sync", "gpsimd")

    def __init__(self):
        self.q = {e: [] for e in self.ENGS}
        self.hwm = {e: {} for e in self.ENGS}

    def emit(self, eng, fn):
        self.q[eng].append(fn)

    def wait(self, eng, sem, val):
        if val <= 0:
            return
        if self.hwm[eng].get(sem.name, 0) >= val:
            return
        self.hwm[eng][sem.name] = val
        h = sem.h
        self.q[eng].append(lambda e: e.wait_ge(h, val))


def build(T, debug=False):
    assert T % TQ == 0
    NT = T // TQ
    NSUB = T // 128

    nc = bass.Bass("TRN2", target_bir_lowering=False,
                   detect_race_conditions=False)

    def inp(name, shape, dt=F32):
        return nc.dram_tensor(name, list(shape), dt, kind="ExternalInput")

    x_h = inp("x", [T, D])
    wqkv_h = inp("wqkv", [D, 3072])
    bqkv_h = inp("bqkv", [1, 3072])
    wo_h = inp("wo", [D, D])
    bo_h = inp("bo", [1, D])
    gw_h = inp("gate_w", [D, D])
    gb_h = inp("gate_b", [1, D])
    aggw_h = inp("agg_w", [D, D])
    bcw_h = inp("bc_w", [D, D])
    d1s_h = inp("d1_slice", [8, 128, D])
    d2s_h = inp("d2_slice", [FF // 8, D])
    d1bT_h = inp("d1_bT", [128, 8])
    bsel_h = inp("bsel", [8])
    aggbT_h = inp("agg_bT", [128, NC16])
    d2bT_h = inp("d2_bT", [128, NC16])
    bcbT_h = inp("bc_bT", [128, NC16])
    ingT_h = inp("in_gT", [128, NC16])
    inbT_h = inp("in_bT", [128, NC16])
    pngT_h = inp("pn_gT", [128, NC16])
    pnbT_h = inp("pn_bT", [128, NC16])
    outg_h = inp("out_g", [D])
    outb_h = inp("out_b", [D])
    cos_h = inp("cos_t", [T, 64])
    sin_h = inp("sin_t", [T, 64])
    trimask_h = inp("trimask", [128, 128], BF16)
    ones1_h = inp("ones1", [1, 128], BF16)
    ones2d_h = inp("ones2d", [128, 128], BF16)
    onescol_h = inp("onescol", [128, 1])
    onesrow_h = inp("onesrow", [1, 128])
    ident_h = inp("ident_bf", [128, 128], BF16)
    identf_h = inp("ident_f32", [128, 128])
    ones2df_h = inp("ones2df", [128, 128])
    consts_h = inp("consts", [len(REG_CONSTS)])

    out_h = nc.dram_tensor("out", [T, D], F32, kind="ExternalOutput")

    def scratch(name, shape, dt):
        kind = "ExternalOutput" if debug else "Internal"
        return nc.dram_tensor(name, list(shape), dt, kind=kind)

    agin_h = nc.dram_tensor("cc_agin", [1, D], F32)
    agout_h = nc.dram_tensor("cc_agout", [8, D], F32)
    dbg_th_h = nc.dram_tensor("dbg_th", [8, 128, 128], F32,
                              kind="ExternalOutput")
    arin_h = [nc.dram_tensor(f"cc_arin{k}", [128, 128], F32) for k in range(4)]
    arout_h = [nc.dram_tensor(f"cc_arout{k}", [128, 128], F32) for k in range(4)]
    q_h = scratch("q_s", [T, D], BF16)
    k_h = scratch("k_s", [T, NKV * HD], BF16)
    v_h = scratch("v_s", [NKV, T // 128, 128, 128], BF16)
    x1_h = scratch("x1_s", [T, D], F32)
    x1b_h = scratch("x1b_s", [T, D], BF16)
    bcrow_h = scratch("bc_row", [D], F32)

    P = Prog()
    stk = contextlib.ExitStack()
    nc._keepalive_stack = stk

    def S(name):
        return Sem(stk, nc, name)

    d_csA = S("d_csA")
    d_csB = S("d_csB")
    d_csC = S("d_csC")
    d_csD = S("d_csD")
    d_xf = S("d_xf")
    e_XT = S("e_XT")
    e_XP = S("e_XP")
    d_wq = S("d_wq")
    d_arb = S("d_arb")
    e_CC = S("e_CC")
    d_st = S("d_st")
    d_wo = S("d_wo")
    d_qt = [S(f"d_qt{i}") for i in range(NH)]
    d_kv = [S("d_kv0"), S("d_kv1")]
    d_x = S("d_x")
    d_x1F = S("d_x1F")
    d_x1R = S("d_x1R")
    d_x1b = S("d_x1b")
    d_thr = [S(f"d_thr{i}") for i in range(4)]
    d_bias = S("d_bias")
    d_x1row = S("d_x1row")
    d_bcr = S("d_bcr")
    d_bcb = S("d_bcb")
    d_gw = S("d_gw")
    d_x1t = [S("d_x1t0"), S("d_x1t1")]
    d_x1l = [S("d_x1l0"), S("d_x1l1")]
    d_out = S("d_out")

    e_p1 = S("e_p1")
    e_evA = S("e_evA")
    e_evB = S("e_evB")
    e_S = S("e_S")
    e_E = S("e_E")
    e_M = S("e_M")
    e_AV = S("e_AV")
    e_SM = S("e_SM")
    e_ACv = S("e_ACv")
    e_ACg = S("e_ACg")
    e_QH = S("e_QH")
    e_KV = S("e_KV")
    e_OE = S("e_OE")
    e_SE = S("e_SE")
    e_OP = S("e_OP")
    e_OPE = S("e_OPE")
    e_LN = S("e_LN")
    e_TH = S("e_TH")
    e_G = S("e_G")
    e_GE = S("e_GE")
    e_Fp = [S("e_F0"), S("e_F1")]

    consts_sb = nc.alloc_sbuf_tensor("consts_sb", [128, 8], F32)
    cM40 = consts_sb[:, 0:1]
    cEPS5 = consts_sb[:, 1:2]
    cEPS6 = consts_sb[:, 2:3]
    cNH = consts_sb[:, 3:4]
    cSQD = consts_sb[:, 4:5]
    cM1s = consts_sb[:, 5:6]
    cP1 = consts_sb[:, 6:7]
    cM40r = consts_sb[0:1, 0:1]
    cEPS5r = consts_sb[0:1, 1:2]
    cEPS6r = consts_sb[0:1, 2:3]
    cNHr = consts_sb[0:1, 3:4]
    bo_sb = nc.alloc_sbuf_tensor("bo_sb", [1, D], BF16)
    ones1 = nc.alloc_sbuf_tensor("ones1_sb", [1, 128], BF16)
    ones2d = nc.alloc_sbuf_tensor("ones2d_sb", [128, 128], BF16)
    onescol = nc.alloc_sbuf_tensor("onescol_sb", [128, 1], F32)
    onesrow = nc.alloc_sbuf_tensor("onesrow_sb", [1, 128], F32)
    trimask = nc.alloc_sbuf_tensor("trimask_sb", [128, 128], BF16)
    ident_bf = nc.alloc_sbuf_tensor("ident_bf_sb", [128, 128], BF16)
    ident_f32 = nc.alloc_sbuf_tensor("ident_f32_sb", [128, 128], F32)
    ones2df = nc.alloc_sbuf_tensor("ones2df_sb", [128, 128], F32)
    lnv = nc.alloc_sbuf_tensor("lnv", [128, 32], F32)

    def ld(dst_ap, src_ap, sem=d_csA, q="sync"):
        sem.n += 16
        h = sem.h
        P.emit(q, lambda e, d=dst_ap, s=src_ap: e.dma_start(d, s).then_inc(h, 16))

    ld(consts_sb[:, 0 : len(REG_CONSTS)],
       consts_h.ap().unsqueeze(0).partition_broadcast(128))
    ld(ones1[:], ones1_h.ap())
    ld(ones2d[:], ones2d_h.ap())
    ld(onescol[:], onescol_h.ap())
    ld(onesrow[:], onesrow_h.ap())
    ld(trimask[:], trimask_h.ap())
    ld(ident_bf[:], ident_h.ap())
    ld(ident_f32[:], identf_h.ap())
    ld(ones2df[:], ones2df_h.ap())
    ld(bo_sb[:], bo_h.ap(), sem=d_bias, q="gpsimd")
    CONST_LDS = d_csA.n
    for eng in ("scalar", "vector", "tensor"):
        P.wait(eng, d_csA, CONST_LDS)

    p1_stack = contextlib.ExitStack()
    wqkv_sb = p1_stack.enter_context(
        nc.sbuf_tensor("wqkv_sb", [128, NC16 * 3072], BF16))
    xt_ring = p1_stack.enter_context(
        nc.sbuf_tensor("xt_ring", [128, 2 * NC16 * TQ], BF16))
    xf_ring = p1_stack.enter_context(nc.sbuf_tensor("xf_ring", [128, 2 * D], F32))
    qkv_out = p1_stack.enter_context(
        nc.sbuf_tensor("qkv_out", [128, 2 * 3072], BF16))
    cos_sb = p1_stack.enter_context(nc.sbuf_tensor("cos_sb", [128, NSUB * 64], F32))
    sin_sb = p1_stack.enter_context(nc.sbuf_tensor("sin_sb", [128, NSUB * 64], F32))
    rope_t = p1_stack.enter_context(nc.sbuf_tensor("rope_t", [128, 4 * 16 * 64], F32))
    psA = p1_stack.enter_context(nc.psum_tensor("psA", [128, 1536], F32))
    psB = p1_stack.enter_context(nc.psum_tensor("psB", [128, 1536], F32))
    xp_ps = p1_stack.enter_context(nc.psum_tensor("xp_ps", [128, 1024], F32))
    bqkv_sb = p1_stack.enter_context(nc.sbuf_tensor("bqkv_sb", [1, 3072], BF16))
    ld(bqkv_sb[:], bqkv_h.ap(), sem=d_bias, q="gpsimd")
    P.wait("tensor", d_bias, d_bias.n)

    ld(cos_sb[:], cos_h.ap().rearrange("(g p) j -> p g j", p=128), sem=d_csB)
    ld(sin_sb[:], sin_h.ap().rearrange("(g p) j -> p g j", p=128), sem=d_csB)
    P.wait("vector", d_csB, d_csB.n)

    wq_src = wqkv_h.ap().rearrange("(c p) f -> p c f", p=128)
    wq_dst = wqkv_sb.ap().rearrange("p (c f) -> p c f", f=3072)
    for c in range(NC16):
        d_wq.n += 16
        h = d_wq.h
        P.emit("gpsimd", lambda e, c=c, h=h: e.dma_start(
            wq_dst[:, c, :], wq_src[:, c, :]).then_inc(h, 16))

    def xt_slot(t, c):
        off = ((t % 2) * NC16 + c) * TQ
        return xt_ring[:, off : off + TQ]

    def xf_slot(g):
        return xf_ring[:, (g % 2) * D : (g % 2) * D + D]

    def emit_xload(g):
        if g >= 2:
            P.wait("sync", e_XT, 4 * (g - 1))
        d_xf.n += 16
        h = d_xf.h
        P.emit("sync", lambda e, g=g, h=h: e.dma_start(
            xf_slot(g), x_h.ap()[g * 128 : (g + 1) * 128, :]).then_inc(h, 16))

    def emit_xpose(t, s):
        g = t * 4 + s
        P.wait("tensor", d_xf, 16 * (g + 1))
        if t >= 2:
            P.wait("tensor", e_p1, (t - 1) * 8)
        for cg in range(4):
            gi4 = g * 4 + cg
            par = gi4 % 2
            if gi4 >= 2:
                P.wait("tensor", e_XP, gi4 - 1)
            hXT = e_XT.h
            for j in range(4):
                def tp(e, c=cg * 4 + j, par=par, j=j, g=g, last=(j == 3)):
                    ins = e.transpose(
                        xp_ps.ap()[:, par * 512 + j * 128 : par * 512 + (j + 1) * 128],
                        xf_slot(g)[:, c * 128 : (c + 1) * 128],
                        ident_f32[:])
                    if last:
                        ins.then_inc(hXT, 1)
                P.emit("tensor", tp)
            e_XT.n += 1
            P.wait("scalar", e_XT, e_XT.n)
            if t >= 2:
                P.wait("scalar", e_p1, (t - 1) * 8)
            slot0 = (t % 2) * NC16 + cg * 4
            dst = xt_ring.ap().rearrange("p (sl q) -> p sl q", q=TQ)[
                :, slot0 : slot0 + 4, s * 128 : (s + 1) * 128]
            e_XP.n += 1
            hXP = e_XP.h
            P.emit("scalar", lambda e, dst=dst, par=par: e.copy(
                dst, xp_ps.ap()[:, par * 512 : par * 512 + 512].rearrange(
                    "p (c f) -> p c f", f=128)).then_inc(hXP, 1))

    emit_xload(0)
    emit_xload(1)

    for t in range(NT):
        for s in range(4):
            if t * 4 + s + 2 < NSUB:
                emit_xload(t * 4 + s + 2)
            emit_xpose(t, s)
        for s in range(4):
            g = t * 4 + s
            par = g % 2
            qo = qkv_out[:, par * 3072 : par * 3072 + 3072]
            P.wait("tensor", e_XP, 4 * (g + 1))
            for half, ps, e_ev in ((0, psA, e_evA), (1, psB, e_evB)):
                if half == 0:
                    P.wait("tensor", e_evA, g)
                else:
                    P.wait("tensor", e_evB, 2 * g)
                for c in range(NC16):
                    lhs = xt_slot(t, c)[:, s * 128 : (s + 1) * 128]
                    P.wait("tensor", d_wq, 16 * (c + 1))
                    for sl in range(3):
                        col0 = half * 1536 + sl * 512
                        P.emit("tensor", lambda e, lhs=lhs, ps=ps, sl=sl, c=c, col0=col0:
                               e.matmul(ps.ap()[:, sl * 512 : sl * 512 + 512],
                                        lhs,
                                        wqkv_sb[:, c * 3072 + col0 : c * 3072 + col0 + 512],
                                        start=(c == 0), stop=False))
                e_p1.n += 1
                hp1 = e_p1.h
                for sl in range(3):
                    col0 = half * 1536 + sl * 512
                    last = sl == 2
                    def mmb(e, ps=ps, sl=sl, col0=col0, last=last):
                        ins = e.matmul(ps.ap()[:, sl * 512 : sl * 512 + 512],
                                       ones1[:],
                                       bqkv_sb[:, col0 : col0 + 512],
                                       start=False, stop=True)
                        if last:
                            ins.then_inc(hp1, 1)
                    P.emit("tensor", mmb)

                P.wait("vector", e_p1, e_p1.n)
                if g >= 2:
                    P.wait("vector", d_st, 96 * (g - 1))
                cos_ap = cos_sb[:, g * 64 : g * 64 + 64]
                sin_ap = sin_sb[:, g * 64 : g * 64 + 64]

                def rope(nheads, ps_off, out_off, inc_sem, ps=ps, qo=qo,
                         cos_ap=cos_ap, sin_ap=sin_ap):
                    pr = ps.ap()[:, ps_off : ps_off + nheads * 128].rearrange(
                        "p (h c) -> p h c", c=128)
                    xr = pr[:, :, 0:64]
                    xi = pr[:, :, 64:128]
                    cb = cos_ap.unsqueeze(1).broadcast_to([128, nheads, 64])
                    sb_ = sin_ap.unsqueeze(1).broadcast_to([128, nheads, 64])
                    t1 = rope_t.ap()[:, 0 : nheads * 64].rearrange(
                        "p (h c) -> p h c", c=64)
                    t2 = rope_t.ap()[:, 16 * 64 : 16 * 64 + nheads * 64].rearrange(
                        "p (h c) -> p h c", c=64)
                    t3 = rope_t.ap()[:, 32 * 64 : 32 * 64 + nheads * 64].rearrange(
                        "p (h c) -> p h c", c=64)
                    t4 = rope_t.ap()[:, 48 * 64 : 48 * 64 + nheads * 64].rearrange(
                        "p (h c) -> p h c", c=64)
                    ov = qo[:, out_off : out_off + nheads * 128].rearrange(
                        "p (h c) -> p h c", c=128)
                    P.emit("vector", lambda e: e.tensor_tensor(t1, xr, cb, ALU.mult))
                    P.emit("vector", lambda e: e.tensor_tensor(t2, xi, sb_, ALU.mult))
                    P.emit("vector", lambda e: e.tensor_tensor(t3, xr, sb_, ALU.mult))
                    P.emit("vector", lambda e: e.tensor_tensor(t4, xi, cb, ALU.mult))
                    P.emit("vector", lambda e: e.tensor_tensor(
                        ov[:, :, 0:64], t1, t2, ALU.subtract))
                    if inc_sem is None:
                        P.emit("vector", lambda e: e.tensor_tensor(
                            ov[:, :, 64:128], t3, t4, ALU.add))
                    else:
                        hh = inc_sem
                        P.emit("vector", lambda e: e.tensor_tensor(
                            ov[:, :, 64:128], t3, t4, ALU.add).then_inc(hh, 1))

                if half == 0:
                    e_evA.n += 1
                    rope(12, 0, 0, e_evA.h)
                else:
                    rope(4, 0, 1536, None)
                    e_evB.n += 1
                    rope(4, 512, 2048, e_evB.h)
                    P.wait("scalar", e_p1, e_p1.n)
                    if g >= 2:
                        P.wait("scalar", d_st, 96 * (g - 1))
                    e_evB.n += 1
                    hevB = e_evB.h
                    P.emit("scalar", lambda e, ps=ps, qo=qo: e.copy(
                        qo[:, 2560:3072], ps.ap()[:, 1024:1536]).then_inc(hevB, 1))

            P.wait("sync", e_evA, g + 1)
            P.wait("sync", e_evB, 2 * (g + 1))
            hst = d_st.h
            d_st.n += 16
            P.emit("sync", lambda e, g=g, qo=qo: e.dma_start(
                q_h.ap()[g * 128 : (g + 1) * 128, :], qo[:, 0:2048]).then_inc(hst, 16))
            d_st.n += 16
            P.emit("sync", lambda e, g=g, qo=qo: e.dma_start(
                k_h.ap()[g * 128 : (g + 1) * 128, :], qo[:, 2048:2560]).then_inc(hst, 16))
            for kv in range(NKV):
                d_st.n += 16
                P.emit("sync", lambda e, g=g, kv=kv, qo=qo: e.dma_start(
                    v_h.ap()[kv, g],
                    qo[:, 2560 + kv * 128 : 2560 + (kv + 1) * 128]).then_inc(hst, 16))

    P1_MM = e_p1.n
    P1_ST = d_st.n
    P1_EVA = e_evA.n
    P1_EVB = e_evB.n
    p1_stack.close()

    p2_stack = contextlib.ExitStack()
    wo_sb = p2_stack.enter_context(nc.sbuf_tensor("wo_sb", [128, NC16 * D], BF16))
    qt_ring = p2_stack.enter_context(nc.sbuf_tensor("qt_ring", [128, NH * TQ], BF16))
    kt_buf = p2_stack.enter_context(nc.sbuf_tensor("kt_buf", [128, 2 * T], BF16))
    v_buf = p2_stack.enter_context(nc.sbuf_tensor("v_buf", [128, 2 * T], BF16))
    pt_buf = p2_stack.enter_context(nc.sbuf_tensor("pt_buf", [128, 4 * 1024], BF16))
    ot_sb = p2_stack.enter_context(nc.sbuf_tensor("ot_sb", [128, NH * TQ], BF16))
    x_sb = p2_stack.enter_context(nc.sbuf_tensor("x_sb", [128, D], F32))
    x1_sb = p2_stack.enter_context(nc.sbuf_tensor("x1_sb", [128, D], F32))
    recip_sb = p2_stack.enter_context(nc.sbuf_tensor("recip_sb", [128, 2 * TQ], F32))
    acc2 = p2_stack.enter_context(nc.sbuf_tensor("acc2", [128, 4 * TQ], F32))
    thw_ring = p2_stack.enter_context(nc.sbuf_tensor("thw_ring", [128, 2 * D], BF16))
    thstg = p2_stack.enter_context(nc.sbuf_tensor("thstg", [128, D], F32))
    th_cols = p2_stack.enter_context(nc.sbuf_tensor("th_cols", [128, 96], F32))
    th_prm = p2_stack.enter_context(nc.sbuf_tensor("th_prm", [128, 7 * NC16], F32))
    th_d1bT = p2_stack.enter_context(nc.sbuf_tensor("th_d1bT", [128, 8], F32))
    th_pad = p2_stack.enter_context(nc.sbuf_tensor("th_pad", [128, 128], F32))
    th_sb128 = p2_stack.enter_context(nc.sbuf_tensor("th_sb128", [128, 128], F32))
    th_sdump = p2_stack.enter_context(nc.sbuf_tensor("th_sdump", [128, 16], F32))
    tb_sb = p2_stack.enter_context(nc.sbuf_tensor("tb_sb", [128, 128], F32))
    tn_sb = p2_stack.enter_context(nc.sbuf_tensor("tn_sb", [128, 128], F32))
    tnb_sb = p2_stack.enter_context(nc.sbuf_tensor("tnb_sb", [128, 128], BF16))
    hb_sb = p2_stack.enter_context(nc.sbuf_tensor("hb_sb", [128, 64], BF16))
    xraw_sb = p2_stack.enter_context(nc.sbuf_tensor("xraw_sb", [128, 128], F32))
    xnb_sb = p2_stack.enter_context(nc.sbuf_tensor("xnb_sb", [128, 128], BF16))
    arb_sb = p2_stack.enter_context(nc.sbuf_tensor("arb_sb", [128, 128], F32))
    bsel_sb = p2_stack.enter_context(nc.sbuf_tensor("bsel_sb", [128, 8], F32))

    sp0 = p2_stack.enter_context(nc.psum_tensor("sp0", [128, 1024], F32))
    sp1 = p2_stack.enter_context(nc.psum_tensor("sp1", [128, 1024], F32))
    o_ps = p2_stack.enter_context(nc.psum_tensor("o_ps", [128, TQ], F32))
    sm_ps = p2_stack.enter_context(nc.psum_tensor("sm_ps", [128, TQ], F32))
    op_ps = p2_stack.enter_context(nc.psum_tensor("op_ps", [128, TQ], F32))
    th_ps = p2_stack.enter_context(nc.psum_tensor("th_ps", [128, 384], F32))

    P.wait("sync", e_p1, P1_MM)
    P.wait("sync", e_evA, P1_EVA)
    P.wait("sync", e_evB, P1_EVB)
    ld(th_prm[:, 0:16], ingT_h.ap(), sem=d_csC)
    ld(th_prm[:, 16:32], inbT_h.ap(), sem=d_csC)
    ld(th_prm[:, 32:48], pngT_h.ap(), sem=d_csC)
    ld(th_prm[:, 48:64], pnbT_h.ap(), sem=d_csC)
    ld(th_prm[:, 64:80], aggbT_h.ap(), sem=d_csC)
    ld(th_prm[:, 80:96], d2bT_h.ap(), sem=d_csC)
    ld(th_prm[:, 96:112], bcbT_h.ap(), sem=d_csC)
    ld(th_d1bT[:], d1bT_h.ap(), sem=d_csC)
    ld(bsel_sb[:], bsel_h.ap().unsqueeze(0).partition_broadcast(128), sem=d_csC)
    TH_PRM_LDS = d_csC.n

    P.wait("gpsimd", e_p1, P1_MM)
    P.wait("gpsimd", e_evA, P1_EVA)
    P.wait("gpsimd", e_evB, P1_EVB)
    wo_src = wo_h.ap().rearrange("(c p) f -> p c f", p=128)
    wo_dst = wo_sb.ap().rearrange("p (c f) -> p c f", f=D)
    for c in range(NC16):
        d_wo.n += 16
        h = d_wo.h
        P.emit("gpsimd", lambda e, c=c, h=h: e.dma_start(
            wo_dst[:, c, :], wo_src[:, c, :]).then_inc(h, 16))

    if debug:
        dbg_ot = nc.dram_tensor("dbg_ot", [NH, 128, TQ], BF16, kind="ExternalOutput")
        dbg_sums = nc.dram_tensor("dbg_sums", [NH, 128, TQ], F32, kind="ExternalOutput")
        dbg_recip = nc.dram_tensor("dbg_recip", [NH, 128, TQ], F32, kind="ExternalOutput")
        dbg_r1 = nc.dram_tensor("dbg_r1", [NSUB, 128, D], F32, kind="ExternalOutput")
        dbg_lnv = nc.dram_tensor("dbg_lnv", [NSUB, 128, 32], F32, kind="ExternalOutput")
        d_dbg = S("d_dbg")

    tile_order = [0, NT - 1] + list(range(1, NT - 1))

    NB = 8
    C_SU, C_SQ2, C_RS2, C_NM2 = 0, 8, 16, 24
    R_MU, R_VAR, R_RS, R_NM, R_TMP = 40, 48, 56, 64, 72

    def th_step(eng, fn, waits=()):
        for sem, val in waits:
            P.wait(eng, sem, val)
        e_TH.n += 1
        h = e_TH.h
        if eng == "tensor":
            P.emit(eng, lambda e, fn=fn, h=h: fn(e).then_inc(h, 1))
        else:
            def run(e, fn=fn, h=h):
                fn(e)
                e.drain().then_inc(h, 1)
            P.emit(eng, run)

    def th_ln(src, dst_bf, g_off, b_off, eps_c):
        src3 = src.ap().rearrange("p (c b) -> p c b", b=NB)
        tn3 = tn_sb.ap().rearrange("p (c b) -> p c b", b=NB)
        for b in range(NB):
            th_step("vector", lambda e, b=b: e.reduce_sum(
                th_cols[:, C_SU + b : C_SU + b + 1], src3[:, :, b], axis=AX.X)
                and e.drain(),
                waits=([(e_TH, e_TH.n)] if b == 0 else ()))
        for b in range(NB):
            th_step("scalar", lambda e, b=b: e.activation(
                th_sdump[:], src3[:, :, b], AF.Square, bias=0.0, scale=1.0,
                accum_out=th_cols[:, C_SQ2 + b : C_SQ2 + b + 1]) and e.drain(),
                waits=([(e_TH, e_TH.n)] if b == 0 else ()))
        th_step("tensor", lambda e: e.matmul(
            th_ps.ap()[0:1, 256:272], onescol[:], th_cols[:, 0:16],
            start=True, stop=True), waits=[(e_TH, e_TH.n)])
        th_step("vector", lambda e: e.tensor_scalar(
            th_cols[0:1, R_MU : R_MU + 16], th_ps.ap()[0:1, 256:272],
            1.0 / D, None, ALU.mult), waits=[(e_TH, e_TH.n)])
        th_step("vector", lambda e: e.tensor_tensor(
            th_cols[0:1, R_TMP : R_TMP + 8], th_cols[0:1, R_MU : R_MU + 8],
            th_cols[0:1, R_MU : R_MU + 8], ALU.mult))
        th_step("vector", lambda e: e.tensor_tensor(
            th_cols[0:1, R_VAR : R_VAR + 8], th_cols[0:1, R_VAR : R_VAR + 8],
            th_cols[0:1, R_TMP : R_TMP + 8], ALU.subtract))
        th_step("scalar", lambda e, eps_c=eps_c: e.activation(
            th_cols[0:1, R_RS : R_RS + 8], th_cols[0:1, R_VAR : R_VAR + 8],
            AF.Ln, bias=eps_c, scale=1.0), waits=[(e_TH, e_TH.n)])
        th_step("scalar", lambda e: e.activation(
            th_cols[0:1, R_RS : R_RS + 8], th_cols[0:1, R_RS : R_RS + 8],
            AF.Exp, bias=0.0, scale=cNHr))
        th_step("vector", lambda e: e.scalar_tensor_tensor(
            th_cols[0:1, R_NM : R_NM + 8], th_cols[0:1, R_MU : R_MU + 8],
            C_M1, th_cols[0:1, R_RS : R_RS + 8], ALU.mult, ALU.mult),
            waits=[(e_TH, e_TH.n)])
        th_step("tensor", lambda e: e.matmul(
            th_ps.ap()[:, 272:288], onesrow[:], th_cols[0:1, R_RS : R_RS + 16],
            start=True, stop=True), waits=[(e_TH, e_TH.n)])
        th_step("vector", lambda e: e.tensor_copy(
            th_cols[:, C_RS2 : C_RS2 + 16], th_ps.ap()[:, 272:288]),
            waits=[(e_TH, e_TH.n)])
        for b in range(NB):
            th_step("scalar", lambda e, b=b: e.activation(
                tn3[:, :, b], src3[:, :, b], AF.Identity,
                bias=th_cols[:, C_NM2 + b : C_NM2 + b + 1],
                scale=th_cols[:, C_RS2 + b : C_RS2 + b + 1]),
                waits=([(e_TH, e_TH.n)] if b == 0 else ()))
        for ci in range(NC16):
            th_step("scalar", lambda e, ci=ci, dst_bf=dst_bf: e.activation(
                dst_bf[:, ci * 8 : ci * 8 + 8], tn_sb[:, ci * 8 : ci * 8 + 8],
                AF.Identity, bias=th_prm[:, b_off + ci : b_off + ci + 1],
                scale=th_prm[:, g_off + ci : g_off + ci + 1]),
                waits=([(e_TH, e_TH.n)] if ci == 0 else ()))

    slab_i = [0]
    slab_war = {0: 0, 1: 0}

    def th_slab(src_fn, mm_fn, width=D, dst3d=None):
        slot = slab_i[0] % 2
        slab_i[0] += 1
        sem = d_thr[slot]
        P.wait("gpsimd", e_TH, slab_war[slot])
        sem.n += 16
        h = sem.h
        dst = thw_ring[:, slot * D : slot * D + width]
        dd = dst if dst3d is None else dst.rearrange("p (c f) -> p c f",
                                                     f=dst3d)
        P.emit("gpsimd", lambda e, dd=dd, src=src_fn(), h=h:
               e.dma_start(dd, src).then_inc(h, 16))
        P.wait("tensor", sem, sem.n)
        P.wait("tensor", e_TH, e_TH.n)
        mm_fn(dst)
        slab_war[slot] = e_TH.n

    def f_ag():
        for eng in ("vector", "scalar", "tensor"):
            P.wait(eng, d_csC, TH_PRM_LDS)
        P.wait("gpsimd", d_x1F, 16 * 4)
        P.wait("gpsimd", d_x1R, 16 * 4)
        d_arb.n += 16
        ha0 = d_arb.h
        P.emit("gpsimd", lambda e, ha0=ha0: e.dma_start(
            agin_h.ap(), x1_h.ap()[T - 1 : T, :]).then_inc(ha0, 16))
        P.wait("gpsimd", d_arb, d_arb.n)
        e_CC.n += 1
        hc = e_CC.h
        P.emit("gpsimd", lambda e, hc=hc: e.collective_compute(
            "AllGather", ALU.bypass, replica_groups=[list(range(8))],
            ins=[agin_h.ap().opt()],
            outs=[agout_h.ap().opt()]).then_inc(hc, 1))
        P.wait("gpsimd", e_CC, e_CC.n)
        d_arb.n += 16
        ha = d_arb.h
        P.emit("gpsimd", lambda e, ha=ha: e.dma_start(
            thstg[0:8, :], agout_h.ap()).then_inc(ha, 16))

    def f_tx():
        for ci in range(NC16):
            th_step("tensor", lambda e, ci=ci: e.transpose(
                th_ps.ap()[:, 128:256], thstg[:, ci * 128 : (ci + 1) * 128],
                ident_f32[:]), waits=[(d_arb, d_arb.n), (e_TH, e_TH.n)])
            th_step("vector", lambda e, ci=ci: e.tensor_copy(
                xraw_sb[:, ci * 8 : ci * 8 + 8], th_ps.ap()[:, 128:136]),
                waits=[(e_TH, e_TH.n)])

    def f_agg(ocs, w_h, mov_sb):
        for oc in ocs:
            def mm(slab, oc=oc):
                hTH = e_TH.h
                for ci in range(NC16):
                    def m(e, oc=oc, ci=ci, slab=slab,
                          last=(ci == NC16 - 1)):
                        ins = e.matmul(
                            th_ps.ap()[:, 128 + oc * 8 : 128 + oc * 8 + 8],
                            slab[:, ci * 128 : (ci + 1) * 128],
                            mov_sb[:, ci * 8 : ci * 8 + 8],
                            start=(ci == 0), stop=(ci == NC16 - 1))
                        if last:
                            ins.then_inc(hTH, 1)
                    P.emit("tensor", m)
                e_TH.n += 1
            th_slab(lambda oc=oc: w_h.ap()[:, oc * 128 : (oc + 1) * 128]
                    .rearrange("(c p) f -> p c f", p=128), mm, dst3d=128)

    def f_agg_ev():
        for ci in range(NC16):
            th_step("scalar", lambda e, ci=ci: e.activation(
                tb_sb[:, ci * 8 : ci * 8 + 8],
                th_ps.ap()[:, 128 + ci * 8 : 128 + ci * 8 + 8],
                AF.Identity, bias=th_prm[:, 64 + ci : 64 + ci + 1],
                scale=1.0), waits=([(e_TH, e_TH.n)] if ci == 0 else ()))

    def f_d1(js):
        for hjl in js:
            def mm(slab, hjl=hjl):
                hTH = e_TH.h
                for ci in range(NC16):
                    def m(e, ci=ci, hjl=hjl, slab=slab,
                          last=(ci == NC16 - 1)):
                        ins = e.matmul(
                            th_ps.ap()[:, hjl * 8 : hjl * 8 + 8],
                            slab[:, ci * 128 : (ci + 1) * 128],
                            tnb_sb[:, ci * 8 : ci * 8 + 8],
                            start=(ci == 0), stop=(ci == NC16 - 1))
                        if last:
                            ins.then_inc(hTH, 1)
                    P.emit("tensor", m)
                e_TH.n += 1
            th_slab(lambda hjl=hjl: d1s_h.ap()[hjl], mm)
            th_step("scalar", lambda e, hjl=hjl: e.activation(
                hb_sb[:, hjl * 8 : hjl * 8 + 8],
                th_ps.ap()[:, hjl * 8 : hjl * 8 + 8], AF.Gelu,
                bias=th_d1bT[:, hjl : hjl + 1], scale=1.0),
                waits=[(e_TH, e_TH.n)])

    def f_d2(ocs):
        for oc in ocs:
            def mm(slab, oc=oc):
                hTH = e_TH.h
                for hjl in range(8):
                    def m(e, oc=oc, hjl=hjl, slab=slab, last=(hjl == 7)):
                        ins = e.matmul(
                            th_ps.ap()[:, 128 + oc * 8 : 128 + oc * 8 + 8],
                            slab[:, hjl * 128 : (hjl + 1) * 128],
                            hb_sb[:, hjl * 8 : hjl * 8 + 8],
                            start=(hjl == 0), stop=(hjl == 7))
                        if last:
                            ins.then_inc(hTH, 1)
                    P.emit("tensor", m)
                e_TH.n += 1
            th_slab(lambda oc=oc: d2s_h.ap()[:, oc * 128 : (oc + 1) * 128]
                    .rearrange("(h p) f -> p h f", p=128), mm,
                    width=1024, dst3d=128)

    def f_ar(k):
        th_step("vector", lambda e: e.tensor_copy(
            arb_sb[:], th_ps.ap()[:, 128:256]), waits=[(e_TH, e_TH.n)])
        P.wait("gpsimd", e_TH, e_TH.n)
        d_arb.n += 16
        ha = d_arb.h
        P.emit("gpsimd", lambda e, k=k, ha=ha: e.dma_start(
            arin_h[k].ap(), arb_sb[:]).then_inc(ha, 16))
        P.wait("gpsimd", d_arb, d_arb.n)
        e_CC.n += 1
        hc = e_CC.h
        P.emit("gpsimd", lambda e, k=k, hc=hc: e.collective_compute(
            "AllReduce", ALU.add, replica_groups=[list(range(8))],
            ins=[arin_h[k].ap().opt()],
            outs=[arout_h[k].ap().opt()]).then_inc(hc, 1))
        P.wait("gpsimd", e_CC, e_CC.n)
        d_arb.n += 16
        P.emit("gpsimd", lambda e, k=k, ha=ha: e.dma_start(
            arb_sb[:], arout_h[k].ap()).then_inc(ha, 16))

    def f_add():
        for ci in range(NC16):
            th_step("scalar", lambda e, ci=ci: e.activation(
                arb_sb[:, ci * 8 : ci * 8 + 8],
                arb_sb[:, ci * 8 : ci * 8 + 8],
                AF.Identity, bias=th_prm[:, 80 + ci : 80 + ci + 1],
                scale=1.0),
                waits=([(d_arb, d_arb.n), (e_TH, e_TH.n)] if ci == 0 else ()))
        th_step("vector", lambda e: e.tensor_tensor(
            tb_sb[:], tb_sb[:], arb_sb[:, 0:128], ALU.add),
            waits=[(e_TH, e_TH.n)])

    def f_thb():
        th_step("scalar", lambda e: e.copy(tnb_sb[:], tb_sb[:]),
                waits=[(e_TH, e_TH.n)])

    def f_bsel():
        arb3 = arb_sb.ap().rearrange("p (c b) -> p c b", b=NB)
        for ci in range(NC16):
            th_step("scalar", lambda e, ci=ci: e.activation(
                arb_sb[:, ci * 8 : ci * 8 + 8],
                th_ps.ap()[:, 128 + ci * 8 : 128 + ci * 8 + 8],
                AF.Identity, bias=th_prm[:, 96 + ci : 96 + ci + 1],
                scale=1.0), waits=([(e_TH, e_TH.n)] if ci == 0 else ()))
        th_step("vector", lambda e, arb3=arb3: e.tensor_tensor(
            arb3, arb3,
            bsel_sb.ap().unsqueeze(1).broadcast_to([128, NC16, NB]),
            ALU.mult), waits=[(e_TH, e_TH.n)])
        th_step("vector", lambda e, arb3=arb3: e.tensor_reduce(
            th_cols[:, 0:16], arb3, AX.X, ALU.add))
        th_step("vector", lambda e: e.memset(th_pad[:], 0.0))
        th_step("vector", lambda e: e.tensor_copy(
            th_pad[:, 0:16], th_cols[:, 0:16]))
        th_step("tensor", lambda e: e.transpose(
            th_ps.ap()[:, 128:256], th_pad[:], ident_f32[:]),
            waits=[(e_TH, e_TH.n)])
        th_step("vector", lambda e: e.tensor_copy(
            th_sb128[:], th_ps.ap()[:, 128:256]), waits=[(e_TH, e_TH.n)])
        P.wait("gpsimd", e_TH, e_TH.n)
        d_bcr.n += 16
        hbc = d_bcr.h
        P.emit("gpsimd", lambda e: e.dma_start(
            bcrow_h.ap().rearrange("(c f) -> c f", f=128),
            th_sb128[0:16, :]).then_inc(hbc, 16))

    fragments = []

    def piece(fn):
        fragments.append(fn)

    d_dmp = S("d_dmp")

    def th_dump(slot, src):
        def fr():
            P.wait("gpsimd", e_TH, e_TH.n)
            d_dmp.n += 16
            hd = d_dmp.h
            P.emit("gpsimd", lambda e, slot=slot, src=src, hd=hd: e.dma_start(
                dbg_th_h.ap()[slot], src).then_inc(hd, 16))
        return fr

    piece(f_ag)
    piece(f_tx)
    piece(th_dump(0, xraw_sb[:]))
    piece(lambda: th_ln(xraw_sb, xnb_sb, 0, 16, cEPS6r))
    piece(th_dump(1, xnb_sb[:]))
    for g_ in range(8):
        piece(lambda g_=g_: f_agg(range(2 * g_, 2 * g_ + 2), aggw_h, xnb_sb))
    piece(f_agg_ev)
    piece(th_dump(2, tb_sb[:]))
    for _k in range(4):
        piece(lambda: th_ln(tb_sb, tnb_sb, 32, 48, cEPS6r))
        for g_ in range(4):
            piece(lambda g_=g_: f_d1(range(2 * g_, 2 * g_ + 2)))
        for g_ in range(4):
            piece(lambda g_=g_: f_d2(range(4 * g_, 4 * g_ + 4)))
        piece(lambda _k=_k: f_ar(_k))
        piece(f_add)
        piece(th_dump(3 + _k, tb_sb[:]))
    piece(f_thb)
    for g_ in range(8):
        piece(lambda g_=g_: f_agg(range(2 * g_, 2 * g_ + 2), bcw_h, tnb_sb))
    piece(f_bsel)
    piece(th_dump(7, th_pad[:]))

    pair_i = [0]
    sp_last = {0: 0, 1: 0}

    def emit_qt_loads(ti, tile):
        tstart = tile * TQ
        P.wait("sync", d_st, 96 * 4 * (tile + 1))
        for qh in range(NH):
            if ti >= 1:
                P.wait("sync", e_QH, (ti - 1) * NH + qh + 1)
            d_qt[qh].n += 16
            h = d_qt[qh].h
            P.emit("sync", lambda e, qh=qh, tstart=tstart, h=h: e.dma_start(
                qt_ring[:, qh * TQ : (qh + 1) * TQ],
                q_h.ap()[tstart : tstart + TQ, qh * 128 : (qh + 1) * 128],
                transpose=True).then_inc(h, 16))

    def emit_kv_loads(ti, tile):
        tstart = tile * TQ
        keys = tstart + TQ
        nblk = keys // 128
        for kvh in range(NKV):
            slot = ti * NKV + kvh
            par = slot % 2
            if slot >= 2:
                P.wait("sync", e_KV, slot - 1)
            d_kv[par].n += 32
            h = d_kv[par].h
            P.emit("sync", lambda e, kvh=kvh, par=par, keys=keys, h=h: e.dma_start(
                kt_buf[:, par * T : par * T + keys],
                k_h.ap()[0:keys, kvh * 128 : (kvh + 1) * 128],
                transpose=True).then_inc(h, 16))
            P.emit("sync", lambda e, kvh=kvh, par=par, nblk=nblk, h=h: e.dma_start(
                v_buf.ap()[:, par * T : par * T + nblk * 128].rearrange(
                    "p (b f) -> p b f", f=128),
                v_h.ap()[kvh, 0:nblk].rearrange("b p f -> p b f")).then_inc(h, 16))

    emit_qt_loads(0, tile_order[0])
    emit_kv_loads(0, tile_order[0])

    for ti, tile in enumerate(tile_order):
        tstart = tile * TQ
        nch = (tstart + TQ) // 128
        if ti + 1 < NT:
            emit_qt_loads(ti + 1, tile_order[ti + 1])
            emit_kv_loads(ti + 1, tile_order[ti + 1])

        for kvh in range(NKV):
            kvslot = ti * NKV + kvh
            par_kv = kvslot % 2
            P.wait("tensor", d_kv[kvslot % 2], 32 * (kvslot // 2 + 1))
            for qp in range(2):
                qA = kvh * 4 + 2 * qp
                npairs = nch // 2
                HS = {}
                for h in (0, 1):
                    qh_ = qA + h
                    P.wait("tensor", d_qt[qh_], 16 * (ti + 1))
                    HS[h] = dict(qh=qh_, qidx=ti * NH + qh_,
                                 sp=(sp0 if h == 0 else sp1),
                                 ob=(o_ps if h == 0 else op_ps), pslot=2 * h,
                                 gate=[None] * npairs, ecnt=[0] * npairs,
                                 avcnt=[0] * npairs, accnt=[0] * npairs)

                def emit_S(h, p):
                    st = HS[h]
                    c0, c1 = 2 * p, 2 * p + 1
                    a0 = max(0, c0 * 128 - tstart)
                    sp = st["sp"]
                    P.wait("tensor", e_E,
                           st["ecnt"][p - 1] if p >= 1 else sp_last[h])
                    hS = e_S.h
                    for hh, c in ((0, c0), (1, c1)):
                        def mmS(e, sp=sp, hh=hh, c=c, a0=a0, par_kv=par_kv,
                                qh=st["qh"], last=(hh == 1)):
                            ins = e.matmul(
                                sp.ap()[:, hh * 512 + a0 : hh * 512 + 512],
                                kt_buf[:, par_kv * T + c * 128 : par_kv * T + (c + 1) * 128],
                                qt_ring[:, qh * TQ + a0 : (qh + 1) * TQ],
                                start=True, stop=True)
                            if last:
                                ins.then_inc(hS, 1)
                        P.emit("tensor", mmS)
                    e_S.n += 1
                    po = (st["pslot"] + p % 2) * 1024
                    ptb = pt_buf[:, po : po + 1024]
                    P.wait("scalar", e_S, e_S.n)
                    if p >= 2:
                        P.wait("scalar", e_AV, st["avcnt"][p - 2])
                        P.wait("scalar", e_ACv, st["accnt"][p - 2])
                    e_E.n += 1
                    hE = e_E.h
                    P.emit("scalar", lambda e, sp=sp, ptb=ptb, a0=a0:
                           e.activation(
                               ptb.rearrange("p (h f) -> p h f", f=512)[:, :, a0:512],
                               sp.ap().rearrange("p (h f) -> p h f", f=512)[:, :, a0:512],
                               AF.Exp, bias=cM40, scale=1.0).then_inc(hE, 1))
                    st["ecnt"][p] = e_E.n
                    diag = [(hh, c) for hh, c in ((0, c0), (1, c1))
                            if 0 <= c * 128 - tstart < TQ]
                    if diag:
                        P.wait("vector", e_E, e_E.n)
                        e_M.n += 1
                        hM = e_M.h
                        for idx, (hh, c) in enumerate(diag):
                            doff = c * 128 - tstart
                            def mk(e, hh=hh, doff=doff, ptb=ptb,
                                   last=(idx == len(diag) - 1)):
                                ins = e.tensor_tensor(
                                    ptb[:, hh * 512 + doff : hh * 512 + doff + 128],
                                    ptb[:, hh * 512 + doff : hh * 512 + doff + 128],
                                    trimask[:], ALU.mult)
                                if last:
                                    ins.then_inc(hM, 1)
                            P.emit("vector", mk)
                        st["gate"][p] = ("M", e_M.n)
                    else:
                        st["gate"][p] = ("E", e_E.n)

                def emit_AV(h, p):
                    st = HS[h]
                    c0, c1 = 2 * p, 2 * p + 1
                    a0 = max(0, c0 * 128 - tstart)
                    a1 = max(0, c1 * 128 - tstart)
                    po = (st["pslot"] + p % 2) * 1024
                    ptb = pt_buf[:, po : po + 1024]
                    gate = st["gate"][p]
                    P.wait("tensor", e_M if gate[0] == "M" else e_E, gate[1])
                    if p == 0:
                        P.wait("tensor", e_OE, st["qidx"] - 1)
                        if h == 1:
                            P.wait("tensor", e_OPE, 16 * ti)
                    e_AV.n += 1
                    hAV = e_AV.h
                    ilp = p == npairs - 1
                    for hh, c, ac in ((0, c0, a0), (1, c1, a1)):
                        def mmO(e, c=c, ac=ac, hh=hh, ptb=ptb, ob=st["ob"],
                                last=(hh == 1), ilp=ilp, h=h, qp=qp,
                                par_kv=par_kv, nch=nch):
                            ins = e.matmul(ob.ap()[:, ac:512],
                                           v_buf[:, par_kv * T + c * 128 : par_kv * T + (c + 1) * 128],
                                           ptb[:, hh * 512 + ac : hh * 512 + 512],
                                           start=(c == 0), stop=(c == nch - 1))
                            if last:
                                ins.then_inc(hAV, 1)
                                if ilp:
                                    e.nop().then_inc(e_QH.h, 1)
                                    if h == 1 and qp == 1:
                                        e.nop().then_inc(e_KV.h, 1)
                        P.emit("tensor", mmO)
                    st["avcnt"][p] = e_AV.n
                    if ilp:
                        e_QH.n += 1
                        if h == 1 and qp == 1:
                            e_KV.n += 1

                def emit_ACC(h, p):
                    st = HS[h]
                    c0, c1 = 2 * p, 2 * p + 1
                    a0 = max(0, c0 * 128 - tstart)
                    a1 = max(0, c1 * 128 - tstart)
                    po = (st["pslot"] + p % 2) * 1024
                    ptb = pt_buf[:, po : po + 1024]
                    ao = (st["qidx"] % 4) * 512
                    acc = acc2[:, ao : ao + 512]
                    gate = st["gate"][p]
                    P.wait("vector", e_M if gate[0] == "M" else e_E, gate[1])
                    if p == 0:
                        P.wait("vector", e_SM, st["qidx"] - 3)
                    e_ACv.n += 1
                    hA = e_ACv.h

                    def ops(e, first=(p == 0), a0=a0, a1=a1, acc=acc, ptb=ptb,
                            hA=hA):
                        if first:
                            e.tensor_copy(acc[:, a0:512], ptb[:, a0:512])
                        else:
                            e.tensor_tensor(acc[:, a0:512], acc[:, a0:512],
                                            ptb[:, a0:512], ALU.add)
                        e.tensor_tensor(acc[:, a1:512], acc[:, a1:512],
                                        ptb[:, 512 + a1 : 1024],
                                        ALU.add).then_inc(hA, 1)
                    P.emit("vector", ops)
                    st["accnt"][p] = e_ACv.n

                def finish(h):
                    st = HS[h]
                    P.wait("tensor", e_ACv, st["accnt"][npairs - 1])
                    P.wait("tensor", e_SE, st["qidx"])
                    e_SM.n += 1
                    hSM = e_SM.h
                    ao = (st["qidx"] % 4) * 512
                    P.emit("tensor", lambda e, ao=ao, hSM=hSM: e.matmul(
                        sm_ps.ap()[:], ones2df[:], acc2[:, ao : ao + 512],
                        start=True, stop=True).then_inc(hSM, 1))
                    sp_last[h] = st["ecnt"][npairs - 1]
                    ro = (st["qidx"] % 2) * TQ
                    P.wait("vector", e_SM, e_SM.n)
                    P.wait("vector", e_AV, st["avcnt"][npairs - 1])
                    if ti >= 1:
                        P.wait("vector", e_OP, ti * 16)
                    e_SE.n += 1
                    hSE = e_SE.h
                    P.emit("vector", lambda e, ro=ro: e.reciprocal(
                        recip_sb[:, ro : ro + TQ], sm_ps.ap()[:]).then_inc(hSE, 1))
                    e_OE.n += 1
                    hOE = e_OE.h
                    P.emit("vector", lambda e, qh=st["qh"], ob=st["ob"], ro=ro:
                           e.scalar_tensor_tensor(
                        ot_sb[:, qh * TQ : (qh + 1) * TQ], ob.ap()[:],
                        0.02209708691207961, recip_sb[:, ro : ro + TQ],
                        ALU.mult, ALU.mult).then_inc(hOE, 1))
                    if ti >= 2 and fragments:
                        fragments.pop(0)()
                    if ti >= 4 and fragments:
                        fragments.pop(0)()

                emit_S(0, 0)
                emit_S(1, 0)
                for p in range(npairs):
                    if p + 1 < npairs:
                        emit_S(0, p + 1)
                    emit_AV(0, p)
                    emit_ACC(0, p)
                    if p + 1 < npairs:
                        emit_S(1, p + 1)
                    emit_AV(1, p)
                    emit_ACC(1, p)
                finish(0)
                finish(1)

        P.wait("tensor", d_wo, 16 * NC16)
        _ = None
        for s in range(4):
            g = tile * 4 + s
            gi = ti * 4 + s
            par_x = gi % 2
            P.wait("scalar", e_OPE, 4 * gi)
            d_x.n += 16
            hx = d_x.h
            P.emit("scalar", lambda e, g=g, hx=hx: e.dma_start(
                x_sb[:],
                x_h.ap()[g * 128 : (g + 1) * 128, :]).then_inc(hx, 16))
            for oc in range(4):
                qtr = gi * 4 + oc
                P.wait("tensor", e_OE, (ti + 1) * NH)
                P.wait("tensor", e_OPE, qtr)
                for dc in range(NC16):
                    P.emit("tensor", lambda e, dc=dc, s=s, oc=oc:
                           e.matmul(op_ps.ap()[:],
                                    ot_sb[:, dc * TQ + s * 128 : dc * TQ + (s + 1) * 128],
                                    wo_sb[:, dc * D + oc * 512 : dc * D + (oc + 1) * 512],
                                    start=(dc == 0), stop=False))
                e_OP.n += 1
                hOP = e_OP.h
                P.emit("tensor", lambda e, oc=oc: e.matmul(
                    op_ps.ap()[:], ones1[:], bo_sb[:, oc * 512 : (oc + 1) * 512],
                    start=False, stop=True).then_inc(hOP, 1))
                P.wait("vector", e_OP, e_OP.n)
                P.wait("vector", d_x, 16 * (gi + 1))
                P.wait("vector", d_x1F, 16 * min(gi, 4))
                P.wait("vector", d_x1R, 16 * max(0, gi - 4))
                P.wait("vector", d_x1b, 16 * gi)
                e_OPE.n += 1
                hOPE = e_OPE.h
                P.emit("vector", lambda e, oc=oc: e.scalar_tensor_tensor(
                    x1_sb[:, oc * 512 : (oc + 1) * 512], op_ps.ap()[:], 1.0,
                    x_sb[:, oc * 512 : (oc + 1) * 512],
                    ALU.mult, ALU.add,
                    accum_out=lnv[:, L_R1 + oc : L_R1 + oc + 1]).then_inc(hOPE, 1))
            if debug:
                P.wait("scalar", e_OPE, e_OPE.n)
                d_dbg.n += 16
                hdbg2 = d_dbg.h
                P.emit("scalar", lambda e, g=g, hdbg2=hdbg2: e.dma_start(
                    dbg_r1.ap()[g], x1_sb[:]).then_inc(hdbg2, 16))
                P.wait("scalar", d_dbg, d_dbg.n)
            P.wait("scalar", e_OPE, e_OPE.n)
            e_LN.n += 1
            hLN = e_LN.h
            P.emit("scalar", lambda e: e.activation(
                x_sb[:], x1_sb[:], AF.Square, bias=0.0, scale=1.0,
                accum_out=lnv[:, L_SQ : L_SQ + 1]))
            P.emit("scalar", lambda e: e.drain().then_inc(hLN, 1))
            P.wait("vector", e_LN, e_LN.n)
            e_LN.n += 1
            def _ln1v(e):
                e.drain()
                e.tensor_reduce(lnv[:, L_SUM : L_SUM + 1],
                                lnv[:, L_R1 : L_R1 + 4], AX.X, ALU.add)
                e.drain()
                e.tensor_scalar(lnv[:, L_MU : L_MU + 1],
                                lnv[:, L_SUM : L_SUM + 1], 1.0 / D, None, ALU.mult)
                e.tensor_scalar(lnv[:, L_VAR : L_VAR + 1],
                                lnv[:, L_SQ : L_SQ + 1], 1.0 / D, None, ALU.mult)
                e.drain()
                e.tensor_tensor(lnv[:, L_TMP : L_TMP + 1],
                                lnv[:, L_MU : L_MU + 1],
                                lnv[:, L_MU : L_MU + 1], ALU.mult)
                e.drain()
                e.tensor_tensor(lnv[:, L_VAR : L_VAR + 1],
                                lnv[:, L_VAR : L_VAR + 1],
                                lnv[:, L_TMP : L_TMP + 1], ALU.subtract)
                e.drain().then_inc(hLN, 1)
            P.emit("vector", _ln1v)
            P.wait("scalar", e_LN, e_LN.n)
            e_LN.n += 1
            def _ln1s(e):
                e.activation(lnv[:, L_RSTD : L_RSTD + 1],
                             lnv[:, L_VAR : L_VAR + 1], AF.Ln, bias=cEPS5, scale=1.0)
                e.drain()
                e.activation(lnv[:, L_RSTD : L_RSTD + 1],
                             lnv[:, L_RSTD : L_RSTD + 1], AF.Exp, bias=0.0, scale=cNH)
                e.drain().then_inc(hLN, 1)
            P.emit("scalar", _ln1s)
            P.wait("vector", e_LN, e_LN.n)
            e_LN.n += 1
            def _ln1n(e):
                e.scalar_tensor_tensor(lnv[:, L_NMR : L_NMR + 1],
                                       lnv[:, L_MU : L_MU + 1], C_M1,
                                       lnv[:, L_RSTD : L_RSTD + 1], ALU.mult, ALU.mult)
                e.drain().then_inc(hLN, 1)
            P.emit("vector", _ln1n)
            P.wait("scalar", e_LN, e_LN.n)
            if debug:
                d_dbg.n += 16
                hdbg3 = d_dbg.h
                P.emit("scalar", lambda e, g=g, hdbg3=hdbg3: e.dma_start(
                    dbg_lnv.ap()[g], lnv[:]).then_inc(hdbg3, 16))
                P.wait("scalar", d_dbg, d_dbg.n)
            e_LN.n += 1
            P.emit("scalar", lambda e: e.activation(
                x1_sb[:], x1_sb[:], AF.Identity, bias=lnv[:, L_NMR : L_NMR + 1],
                scale=lnv[:, L_RSTD : L_RSTD + 1]).then_inc(hLN, 1))
            P.wait("scalar", e_LN, e_LN.n)
            sx1 = d_x1F if gi < 4 else d_x1R
            sx1.n += 16
            hx1 = sx1.h
            P.emit("scalar", lambda e, g=g, hx1=hx1: e.dma_start(
                x1_h.ap()[g * 128 : (g + 1) * 128, :], x1_sb[:]).then_inc(hx1, 16))
            P.wait("gpsimd", e_LN, e_LN.n)
            d_x1b.n += 16
            hx1b = d_x1b.h
            P.emit("gpsimd", lambda e, g=g: e.dma_start(
                x1b_h.ap()[g * 128 : (g + 1) * 128, :], x1_sb[:]).then_inc(hx1b, 16))

    while fragments:
        fragments.pop(0)()
    P.wait("gpsimd", d_dmp, d_dmp.n)

    P2_OP = e_OP.n
    P2_OPE = e_OPE.n
    P2_AV = e_AV.n
    P2_OE = e_OE.n
    P2_SE = e_SE.n
    P2_LN = e_LN.n
    TH_ALL = e_TH.n
    p2_stack.close()

    p3_stack = contextlib.ExitStack()
    gw_sb = p3_stack.enter_context(nc.sbuf_tensor("gw_sb", [128, NC16 * D], BF16))
    x1t_ring = p3_stack.enter_context(
        nc.sbuf_tensor("x1t_ring", [128, 2 * NC16 * TQ], BF16))
    bc_bc = p3_stack.enter_context(nc.sbuf_tensor("bc_bc", [128, D], F32))
    outg_bc = p3_stack.enter_context(nc.sbuf_tensor("outg_bc", [128, D], F32))
    outb_bc = p3_stack.enter_context(nc.sbuf_tensor("outb_bc", [128, D], F32))
    x1l_sb = p3_stack.enter_context(nc.sbuf_tensor("x1l_sb", [128, 2 * D], F32))
    g_sb2 = p3_stack.enter_context(nc.sbuf_tensor("g_sb2", [128, 2 * D], F32))
    o3_sb2 = p3_stack.enter_context(nc.sbuf_tensor("o3_sb2", [128, 2 * D], F32))
    st3 = p3_stack.enter_context(nc.sbuf_tensor("st3", [128, 64], F32))
    g0_ps = p3_stack.enter_context(nc.psum_tensor("g0_ps", [128, TQ], F32))
    g1_ps = p3_stack.enter_context(nc.psum_tensor("g1_ps", [128, TQ], F32))
    gb_sb = p3_stack.enter_context(nc.sbuf_tensor("gb_sb", [1, D], BF16))

    for eng in ("sync", "gpsimd", "scalar"):
        P.wait(eng, e_OP, P2_OP)
        P.wait(eng, e_OPE, P2_OPE)
        P.wait(eng, e_AV, P2_AV)
        P.wait(eng, e_OE, P2_OE)
        P.wait(eng, e_SE, P2_SE)
        P.wait(eng, e_SM, e_SM.n)
        P.wait(eng, e_ACv, e_ACv.n)
        P.wait(eng, e_ACg, e_ACg.n)
        P.wait(eng, e_LN, P2_LN)
        P.wait(eng, e_TH, TH_ALL)
        for s_ in d_thr:
            P.wait(eng, s_, s_.n)
        for s_ in d_qt:
            P.wait(eng, s_, s_.n)
        P.wait(eng, d_kv[0], d_kv[0].n)
        P.wait(eng, d_kv[1], d_kv[1].n)
        P.wait(eng, d_x, d_x.n)
        P.wait(eng, d_x1F, 16 * min(NSUB, 4))
        P.wait(eng, d_x1R, 16 * max(0, NSUB - 4))
        P.wait(eng, d_x1b, 16 * NSUB)

    ld(outg_bc[:], outg_h.ap().unsqueeze(0).partition_broadcast(128), sem=d_csD)
    ld(outb_bc[:], outb_h.ap().unsqueeze(0).partition_broadcast(128), sem=d_csD)
    P3_CS = d_csD.n
    P.wait("sync", d_bcr, 16)
    d_bcb.n += 16
    hbc2 = d_bcb.h
    P.emit("sync", lambda e: e.dma_start(
        bc_bc[:], bcrow_h.ap().unsqueeze(0).partition_broadcast(128)
    ).then_inc(hbc2, 16))
    BC_READY = d_bcb.n

    ld(gb_sb[:], gb_h.ap(), sem=d_bias, q="gpsimd")
    GB_LD = d_bias.n
    P.wait("tensor", d_bias, GB_LD)
    gw_src = gw_h.ap().rearrange("(c p) f -> p c f", p=128)
    gw_dst = gw_sb.ap().rearrange("p (c f) -> p c f", f=D)
    for c in range(NC16):
        d_gw.n += 16
        h = d_gw.h
        P.emit("gpsimd", lambda e, c=c, h=h: e.dma_start(
            gw_dst[:, c, :], gw_src[:, c, :]).then_inc(h, 16))

    def x1t_slot(t, c):
        off = ((t % 2) * NC16 + c) * TQ
        return x1t_ring[:, off : off + TQ]

    F_PER_SUB = 9
    pending = []

    def flush_pending(k=None):
        n = len(pending) if k is None else min(k, len(pending))
        for _ in range(n):
            eng, waits, fns = pending.pop(0)
            for sem_, val_ in waits:
                P.wait(eng, sem_, val_)
            for fn in fns:
                P.emit(eng, fn)

    def build_finals(g):
        par = g % 2
        PB = 32 * par
        f_ = e_Fp[par]
        hF = f_.h
        base = f_.n
        gv = g_sb2[:, par * D : par * D + D]
        ov = o3_sb2[:, par * D : par * D + D]
        x1v = x1l_sb[:, par * D : par * D + D]
        bn = st3[:, PB : PB + 24]
        mu_c = st3[:, PB + 24 : PB + 25]
        var_c = st3[:, PB + 25 : PB + 26]
        mv_c = st3[:, PB + 24 : PB + 26]
        rstd_c = st3[:, PB + 26 : PB + 27]
        nmr_c = st3[:, PB + 27 : PB + 28]

        def bn_block(e, src):
            for q in range(4):
                e.bn_stats(bn[:, 6 * q : 6 * q + 6],
                           src[:, 512 * q : 512 * q + 512])
            e.drain()
            e.bn_aggr(mv_c, bn)

        def _b1(e):
            e.scalar_tensor_tensor(ov, gv, 1.0, x1v, ALU.mult, ALU.add)
            bn_block(e, ov)
            e.drain().then_inc(hF, 1)
        pending.append(("vector",
                        [(e_GE, e_GE.n), (d_csD, P3_CS),
                         (d_x1l[par], 16 * (g // 2 + 1)), (d_out, 16 * (g - 1))],
                        [_b1]))

        def _b2(e):
            e.activation(rstd_c, var_c, AF.Ln, bias=cEPS6, scale=1.0)
            e.drain()
            e.activation(rstd_c, rstd_c, AF.Exp, bias=0.0, scale=cNH)
            e.drain().then_inc(hF, 1)
        pending.append(("scalar", [(f_, base + 1)], [_b2]))

        def _b3(e):
            e.scalar_tensor_tensor(nmr_c, mu_c, C_M1, rstd_c, ALU.mult, ALU.mult)
            e.drain().then_inc(hF, 1)
        pending.append(("vector", [(f_, base + 2)], [_b3]))

        def _b4(e):
            e.activation(gv, ov, AF.Identity, bias=nmr_c,
                         scale=rstd_c).then_inc(hF, 1)
        pending.append(("scalar", [(f_, base + 3)], [_b4]))

        def _b5a(e):
            e.tensor_tensor(gv, gv, outg_bc[:], ALU.mult)
            e.tensor_tensor(gv, gv, outb_bc[:], ALU.add)
            e.drain().then_inc(hF, 1)
        pending.append(("vector", [(f_, base + 4), (d_csD, P3_CS)], [_b5a]))

        def _b5b(e):
            e.scalar_tensor_tensor(ov, gv, 1.0, x1v, ALU.mult, ALU.add)
            bn_block(e, ov)
            e.drain().then_inc(hF, 1)
        pending.append(("vector", [(f_, base + 5)], [_b5b]))

        def _b6(e):
            e.activation(rstd_c, var_c, AF.Ln, bias=cEPS5, scale=1.0)
            e.drain()
            e.activation(rstd_c, rstd_c, AF.Exp, bias=0.0, scale=cNH)
            e.drain().then_inc(hF, 1)
        pending.append(("scalar", [(f_, base + 6)], [_b6]))

        def _b7(e):
            e.scalar_tensor_tensor(nmr_c, mu_c, C_M1, rstd_c, ALU.mult, ALU.mult)
            e.drain().then_inc(hF, 1)
        pending.append(("vector", [(f_, base + 7)], [_b7]))

        def _b8(e):
            e.activation(ov, ov, AF.Identity, bias=nmr_c,
                         scale=rstd_c).then_inc(hF, 1)
        pending.append(("scalar", [(f_, base + 8)], [_b8]))

        d_out.n += 16
        hout = d_out.h

        def _b9(e):
            e.dma_start(out_h.ap()[g * 128 : (g + 1) * 128, :],
                        ov).then_inc(hout, 16)
        pending.append(("scalar", [(f_, base + 9)], [_b9]))
        f_.n += F_PER_SUB

    for t in range(NT):
        P.wait("sync", d_x1b, 16 * NSUB)
        if t >= 2:
            P.wait("sync", e_G, (t - 1) * 16)
        for c in range(NC16):
            d_x1t[t % 2].n += 16
            h = d_x1t[t % 2].h
            P.emit("sync", lambda e, t=t, c=c, h=h: e.dma_start(
                x1t_slot(t, c),
                x1b_h.ap()[t * TQ : (t + 1) * TQ, c * 128 : (c + 1) * 128],
                transpose=True).then_inc(h, 16))

        for s in range(4):
            g = t * 4 + s
            par_x = g % 2
            if g == 0:
                d_x1l[0].n += 16
                hxl = d_x1l[0].h
                P.emit("scalar", lambda e, hxl=hxl: e.dma_start(
                    x1l_sb[:, 0:D],
                    x1_h.ap()[0:128, :]).then_inc(hxl, 16))
            for oc in range(4):
                qtr = g * 4 + oc
                gps = g0_ps if qtr % 2 == 0 else g1_ps
                P.wait("tensor", d_x1t[t % 2], 16 * NC16 * (t // 2 + 1))
                P.wait("tensor", d_gw, 16 * NC16)
                if qtr >= 2:
                    P.wait("tensor", e_GE, 2 * (qtr - 2) + 1)
                for dc in range(NC16):
                    P.emit("tensor", lambda e, dc=dc, s=s, oc=oc, t=t, gps=gps:
                           e.matmul(gps.ap()[:],
                                    x1t_slot(t, dc)[:, s * 128 : (s + 1) * 128],
                                    gw_sb[:, dc * D + oc * 512 : dc * D + (oc + 1) * 512],
                                    start=(dc == 0), stop=False))
                e_G.n += 1
                hG = e_G.h
                P.emit("tensor", lambda e, oc=oc, gps=gps: e.matmul(
                    gps.ap()[:], ones1[:], gb_sb[:, oc * 512 : (oc + 1) * 512],
                    start=False, stop=True).then_inc(hG, 1))
                P.wait("scalar", e_G, e_G.n)
                if g >= 2:
                    P.wait("scalar", e_Fp[g % 2], F_PER_SUB * (g // 2 - 1) + 6)
                gv = g_sb2[:, par_x * D : par_x * D + D]
                e_GE.n += 1
                hGE = e_GE.h
                P.emit("scalar", lambda e, oc=oc, gps=gps, gv=gv, hGE=hGE:
                       e.activation(
                    gv[:, oc * 512 : (oc + 1) * 512], gps.ap()[:],
                    AF.Sigmoid, bias=0.0, scale=1.0).then_inc(hGE, 1))
                P.wait("vector", d_bcb, BC_READY)
                P.wait("vector", e_GE, e_GE.n)
                e_GE.n += 1

                def _bcq(e, oc=oc, gv=gv, hGE=hGE):
                    e.tensor_tensor(gv[:, oc * 512 : (oc + 1) * 512],
                                    gv[:, oc * 512 : (oc + 1) * 512],
                                    bc_bc[:, oc * 512 : (oc + 1) * 512],
                                    ALU.mult)
                    e.drain().then_inc(hGE, 1)
                P.emit("vector", _bcq)
                flush_pending(3 if oc < 3 else 1)
            gl = g + 1
            if gl < NSUB:
                if gl >= 2:
                    P.wait("scalar", e_Fp[gl % 2], F_PER_SUB * (gl // 2 - 1) + 6)
                d_x1l[gl % 2].n += 16
                hxl = d_x1l[gl % 2].h
                P.emit("scalar", lambda e, gl=gl, par=gl % 2, hxl=hxl: e.dma_start(
                    x1l_sb[:, par * D : par * D + D],
                    x1_h.ap()[gl * 128 : (gl + 1) * 128, :]).then_inc(hxl, 16))
            build_finals(g)

    flush_pending()
    P.wait("scalar", d_out, 16 * NSUB)
    p3_stack.close()

    with nc.Block() as block:
        @block.tensor
        def _(e):
            for fn in P.q["tensor"]:
                fn(e)

        @block.vector
        def _(e):
            for fn in P.q["vector"]:
                fn(e)

        @block.scalar
        def _(e):
            for fn in P.q["scalar"]:
                fn(e)

        @block.sync
        def _(e):
            for fn in P.q["sync"]:
                fn(e)

        @block.gpsimd
        def _(e):
            for fn in P.q["gpsimd"]:
                fn(e)

    return nc


def _perm_for_heads(nheads):
    perm = []
    for h in range(nheads):
        base = h * HD
        perm.extend(base + 2 * np.arange(64))
        perm.extend(base + 2 * np.arange(64) + 1)
    return np.array(perm)


def _colform(v):
    return np.ascontiguousarray(np.asarray(v, np.float32).reshape(NC16, 128).T)


def pack_inputs(inputs, T):
    f32 = lambda a: np.ascontiguousarray(np.asarray(a, np.float32))
    x = f32(inputs["L0_attn_x"])
    B = x.shape[0]
    wq = f32(inputs["L0_attn_Wq"])
    bq = f32(inputs["L0_attn_bq"])
    wk = f32(inputs["L0_attn_Wk"])
    bk = f32(inputs["L0_attn_bk"])
    wv = f32(inputs["L0_attn_Wv"])
    bv = f32(inputs["L0_attn_bv"])
    qperm = _perm_for_heads(NH)
    kperm = _perm_for_heads(NKV)
    wqkv = np.ascontiguousarray(
        np.concatenate([wq[:, qperm], wk[:, kperm], wv], axis=1))
    bqkv = np.ascontiguousarray(
        np.concatenate([bq[qperm], bk[kperm], bv])[None, :])

    d1 = f32(inputs["L0_ffn_d1_w"])
    d1r = np.ascontiguousarray(
        d1.reshape(NC16, 128, FF // 128, 128).transpose(2, 1, 0, 3).reshape(
            FF // 128, 128, D))

    j = np.arange(0, D, 2, dtype=np.float32) / np.float32(D)
    freqs = (np.float32(10000.0) ** (-j))[:64]
    t = np.arange(T, dtype=np.float32)
    ang = np.outer(t, freqs).astype(np.float32)

    bf = ml_dtypes.bfloat16
    common = {
        "wqkv": wqkv,
        "bqkv": bqkv,
        "wo": f32(inputs["L0_attn_Wo"]),
        "bo": f32(inputs["L0_attn_bo"]).reshape(1, D),
        "gate_w": f32(inputs["L0_ffn_gate_w"]),
        "gate_b": f32(inputs["L0_ffn_gate_b"]).reshape(1, D),
        "agg_w": f32(inputs["L0_ffn_agg_w"]),
        "bc_w": f32(inputs["L0_ffn_bc_w"]),
        "agg_bT": _colform(inputs["L0_ffn_agg_b"]),
        "d2_bT": _colform(inputs["L0_ffn_d2_b"]),
        "bc_bT": _colform(inputs["L0_ffn_bc_b"]),
        "in_gT": _colform(inputs["L0_ffn_in_gamma"]),
        "in_bT": _colform(inputs["L0_ffn_in_beta"]),
        "pn_gT": _colform(inputs["L0_ffn_pn_gamma"]),
        "pn_bT": _colform(inputs["L0_ffn_pn_beta"]),
        "out_g": f32(inputs["L0_ffn_out_gamma"]),
        "out_b": f32(inputs["L0_ffn_out_beta"]),
        "cos_t": np.cos(ang).astype(np.float32),
        "sin_t": np.sin(ang).astype(np.float32),
        "trimask": np.triu(np.ones((128, 128), np.float32)).astype(bf),
        "ones1": np.ones((1, 128), np.float32).astype(bf),
        "ones2d": np.ones((128, 128), np.float32).astype(bf),
        "ones2df": np.ones((128, 128), np.float32),
        "onescol": np.ones((128, 1), np.float32),
        "onesrow": np.ones((1, 128), np.float32),
        "ident_bf": np.eye(128, dtype=np.float32).astype(bf),
        "ident_f32": np.eye(128, dtype=np.float32),
        "consts": np.array(REG_CONSTS, np.float32),
    }
    d2 = f32(inputs["L0_ffn_d2_w"])
    d1bT = np.ascontiguousarray(
        f32(inputs["L0_ffn_d1_b"]).reshape(FF // 128, 128).T)
    in_maps = []
    for b in range(B):
        m = dict(common)
        m["x"] = np.ascontiguousarray(x[b])
        m["d1_slice"] = np.ascontiguousarray(d1r[8 * b : 8 * b + 8])
        m["d2_slice"] = np.ascontiguousarray(d2[1024 * b : 1024 * (b + 1), :])
        m["d1_bT"] = np.ascontiguousarray(d1bT[:, 8 * b : 8 * b + 8])
        sel = np.zeros(8, np.float32)
        sel[b] = 1.0
        m["bsel"] = sel
        in_maps.append(m)
    return in_maps


def kernel(**inputs):
    x = np.asarray(inputs["L0_attn_x"])
    B, T, _ = x.shape
    in_maps = pack_inputs(inputs, T)
    nc = build(T)
    res = run_bass_kernel_spmd(nc, in_maps, core_ids=list(range(B)))
    out = np.stack([np.asarray(res.results[i]["out"], np.float32)
                    for i in range(B)], axis=0)
    return out


# version 30
# speedup vs baseline: 1.2038x; 1.0335x over previous
import contextlib

import numpy as np
import ml_dtypes

import concourse.bass as bass
import concourse.mybir as mybir
from concourse.bass_utils import run_bass_kernel_spmd

F32 = mybir.dt.float32
BF16 = mybir.dt.bfloat16
AF = mybir.ActivationFunctionType
ALU = mybir.AluOpType
AX = mybir.AxisListType

D = 2048
NH = 16
NKV = 4
HD = 128
FF = 8192
TQ = 512
NC16 = D // 128
SQRT_D = 45.254833995939045

C_M40 = -40.0
C_EPS5 = 1e-5
C_EPS6 = 1e-6
C_NHALF = -0.5
C_SQD = SQRT_D
C_M1 = -1.0
C_P1 = 1.0
REG_CONSTS = [C_M40, C_EPS5, C_EPS6, C_NHALF, C_SQD, C_M1, C_P1]

L_R1, L_SUM, L_SQ, L_MU, L_VAR, L_RSTD, L_NMR, L_TMP = 0, 4, 5, 6, 7, 8, 9, 10

TC_TH = 0
TC_T = 16
TC_RSTD = 48
TC_NMR = 49
TC_SUM = 52
TC_SQ = 53
TC_MU = 54
TC_VAR = 55
TC_RS = 56
TC_NM = 57
TC_TMP = 58


class Sem:
    __slots__ = ("h", "n", "name")

    def __init__(self, stack, nc, name):
        self.h = stack.enter_context(nc.semaphore(name))
        self.n = 0
        self.name = name


class Prog:
    ENGS = ("tensor", "vector", "scalar", "sync", "gpsimd")

    def __init__(self):
        self.q = {e: [] for e in self.ENGS}
        self.hwm = {e: {} for e in self.ENGS}

    def emit(self, eng, fn):
        self.q[eng].append(fn)

    def wait(self, eng, sem, val):
        if val <= 0:
            return
        if self.hwm[eng].get(sem.name, 0) >= val:
            return
        self.hwm[eng][sem.name] = val
        h = sem.h
        self.q[eng].append(lambda e: e.wait_ge(h, val))


def build(T, debug=False):
    assert T % TQ == 0
    NT = T // TQ
    NSUB = T // 128

    nc = bass.Bass("TRN2", target_bir_lowering=False,
                   detect_race_conditions=False)

    def inp(name, shape, dt=F32):
        return nc.dram_tensor(name, list(shape), dt, kind="ExternalInput")

    x_h = inp("x", [T, D])
    wqkv_h = inp("wqkv", [D, 3072])
    bqkv_h = inp("bqkv", [1, 3072])
    wo_h = inp("wo", [D, D])
    bo_h = inp("bo", [1, D])
    gw_h = inp("gate_w", [D, D])
    gb_h = inp("gate_b", [1, D])
    aggw_h = inp("agg_w", [D, D])
    bcw_h = inp("bc_w", [D, D])
    d1s_h = inp("d1_slice", [8, 128, D])
    d2s_h = inp("d2_slice", [FF // 8, D])
    d1bT_h = inp("d1_bT", [128, 8])
    bsel_h = inp("bsel", [8])
    aggbT_h = inp("agg_bT", [128, NC16])
    d2bT_h = inp("d2_bT", [128, NC16])
    bcbT_h = inp("bc_bT", [128, NC16])
    ingT_h = inp("in_gT", [128, NC16])
    inbT_h = inp("in_bT", [128, NC16])
    pngT_h = inp("pn_gT", [128, NC16])
    pnbT_h = inp("pn_bT", [128, NC16])
    outg_h = inp("out_g", [D])
    outb_h = inp("out_b", [D])
    cos_h = inp("cos_t", [T, 64])
    sin_h = inp("sin_t", [T, 64])
    trimask_h = inp("trimask", [128, 128], BF16)
    ones1_h = inp("ones1", [1, 128], BF16)
    ones2d_h = inp("ones2d", [128, 128], BF16)
    onescol_h = inp("onescol", [128, 1])
    onesrow_h = inp("onesrow", [1, 128])
    ident_h = inp("ident_bf", [128, 128], BF16)
    identf_h = inp("ident_f32", [128, 128])
    ones2df_h = inp("ones2df", [128, 128])
    consts_h = inp("consts", [len(REG_CONSTS)])

    out_h = nc.dram_tensor("out", [T, D], F32, kind="ExternalOutput")

    def scratch(name, shape, dt):
        kind = "ExternalOutput" if debug else "Internal"
        return nc.dram_tensor(name, list(shape), dt, kind=kind)

    d1b_h = nc.dram_tensor("d1b", [8, 128, D], BF16)
    d2b_h = nc.dram_tensor("d2b", [NC16, 128, FF // 8], BF16)
    aggb_h = nc.dram_tensor("aggb", [NC16, 128, D], BF16)
    bcb_h = nc.dram_tensor("bcb", [NC16, 128, D], BF16)
    agin_h = nc.dram_tensor("cc_agin", [1, D], F32)
    agout_h = nc.dram_tensor("cc_agout", [8, D], F32)
    dbg_th_h = nc.dram_tensor("dbg_th", [8, 128, 128], F32,
                              kind="ExternalOutput")
    arin_h = [nc.dram_tensor(f"cc_arin{k}", [128, 128], F32) for k in range(4)]
    arout_h = [nc.dram_tensor(f"cc_arout{k}", [128, 128], F32) for k in range(4)]
    q_h = scratch("q_s", [T, D], BF16)
    k_h = scratch("k_s", [T, NKV * HD], BF16)
    v_h = scratch("v_s", [NKV, T // 128, 128, 128], BF16)
    x1_h = scratch("x1_s", [T, D], F32)
    x1b_h = scratch("x1b_s", [T, D], BF16)
    bcrow_h = scratch("bc_row", [D], F32)

    P = Prog()
    stk = contextlib.ExitStack()
    nc._keepalive_stack = stk

    def S(name):
        return Sem(stk, nc, name)

    d_csA = S("d_csA")
    d_csB = S("d_csB")
    d_csC = S("d_csC")
    d_csD = S("d_csD")
    d_xf = S("d_xf")
    e_XT = S("e_XT")
    e_XP = S("e_XP")
    d_wq = S("d_wq")
    d_arb = S("d_arb")
    d_pre = S("d_pre")
    e_CC = S("e_CC")
    d_st = S("d_st")
    d_wo = S("d_wo")
    d_qt = [S(f"d_qt{i}") for i in range(NH)]
    d_kv = [S("d_kv0"), S("d_kv1")]
    d_x = S("d_x")
    d_x1F = S("d_x1F")
    d_x1R = S("d_x1R")
    d_x1b = S("d_x1b")
    d_thr = [S(f"d_thr{i}") for i in range(4)]
    d_bias = S("d_bias")
    d_x1row = S("d_x1row")
    d_bcr = S("d_bcr")
    d_bcb = S("d_bcb")
    d_gw = S("d_gw")
    d_x1t = [S("d_x1t0"), S("d_x1t1")]
    d_x1l = [S("d_x1l0"), S("d_x1l1")]
    d_out = S("d_out")

    e_p1 = S("e_p1")
    e_evA = S("e_evA")
    e_evB = S("e_evB")
    e_S = S("e_S")
    e_E = S("e_E")
    e_M = S("e_M")
    e_AV = S("e_AV")
    e_SM = S("e_SM")
    e_ACv = S("e_ACv")
    e_ACg = S("e_ACg")
    e_QH = S("e_QH")
    e_KV = S("e_KV")
    e_OE = S("e_OE")
    e_SE = S("e_SE")
    e_OP = S("e_OP")
    e_OPE = S("e_OPE")
    e_LN = S("e_LN")
    e_TH = S("e_TH")
    e_G = S("e_G")
    e_GE = S("e_GE")
    e_Fp = [S("e_F0"), S("e_F1")]

    consts_sb = nc.alloc_sbuf_tensor("consts_sb", [128, 8], F32)
    cM40 = consts_sb[:, 0:1]
    cEPS5 = consts_sb[:, 1:2]
    cEPS6 = consts_sb[:, 2:3]
    cNH = consts_sb[:, 3:4]
    cSQD = consts_sb[:, 4:5]
    cM1s = consts_sb[:, 5:6]
    cP1 = consts_sb[:, 6:7]
    cM40r = consts_sb[0:1, 0:1]
    cEPS5r = consts_sb[0:1, 1:2]
    cEPS6r = consts_sb[0:1, 2:3]
    cNHr = consts_sb[0:1, 3:4]
    bo_sb = nc.alloc_sbuf_tensor("bo_sb", [1, D], BF16)
    ones1 = nc.alloc_sbuf_tensor("ones1_sb", [1, 128], BF16)
    ones2d = nc.alloc_sbuf_tensor("ones2d_sb", [128, 128], BF16)
    onescol = nc.alloc_sbuf_tensor("onescol_sb", [128, 1], F32)
    onesrow = nc.alloc_sbuf_tensor("onesrow_sb", [1, 128], F32)
    trimask = nc.alloc_sbuf_tensor("trimask_sb", [128, 128], BF16)
    ident_bf = nc.alloc_sbuf_tensor("ident_bf_sb", [128, 128], BF16)
    ident_f32 = nc.alloc_sbuf_tensor("ident_f32_sb", [128, 128], F32)
    ones2df = nc.alloc_sbuf_tensor("ones2df_sb", [128, 128], F32)
    lnv = nc.alloc_sbuf_tensor("lnv", [128, 32], F32)

    def ld(dst_ap, src_ap, sem=d_csA, q="sync"):
        sem.n += 16
        h = sem.h
        P.emit(q, lambda e, d=dst_ap, s=src_ap: e.dma_start(d, s).then_inc(h, 16))

    ld(consts_sb[:, 0 : len(REG_CONSTS)],
       consts_h.ap().unsqueeze(0).partition_broadcast(128))
    ld(ones1[:], ones1_h.ap())
    ld(ones2d[:], ones2d_h.ap())
    ld(onescol[:], onescol_h.ap())
    ld(onesrow[:], onesrow_h.ap())
    ld(trimask[:], trimask_h.ap())
    ld(ident_bf[:], ident_h.ap())
    ld(ident_f32[:], identf_h.ap())
    ld(ones2df[:], ones2df_h.ap())
    ld(bo_sb[:], bo_h.ap(), sem=d_bias, q="gpsimd")
    CONST_LDS = d_csA.n
    for eng in ("scalar", "vector", "tensor"):
        P.wait(eng, d_csA, CONST_LDS)

    p1_stack = contextlib.ExitStack()
    wqkv_sb = p1_stack.enter_context(
        nc.sbuf_tensor("wqkv_sb", [128, NC16 * 3072], BF16))
    xt_ring = p1_stack.enter_context(
        nc.sbuf_tensor("xt_ring", [128, 2 * NC16 * TQ], BF16))
    xf_ring = p1_stack.enter_context(nc.sbuf_tensor("xf_ring", [128, 2 * D], F32))
    qkv_out = p1_stack.enter_context(
        nc.sbuf_tensor("qkv_out", [128, 2 * 3072], BF16))
    cos_sb = p1_stack.enter_context(nc.sbuf_tensor("cos_sb", [128, NSUB * 64], F32))
    sin_sb = p1_stack.enter_context(nc.sbuf_tensor("sin_sb", [128, NSUB * 64], F32))
    rope_t = p1_stack.enter_context(nc.sbuf_tensor("rope_t", [128, 4 * 16 * 64], F32))
    psA = p1_stack.enter_context(nc.psum_tensor("psA", [128, 1536], F32))
    psB = p1_stack.enter_context(nc.psum_tensor("psB", [128, 1536], F32))
    xp_ps = p1_stack.enter_context(nc.psum_tensor("xp_ps", [128, 1024], F32))
    bqkv_sb = p1_stack.enter_context(nc.sbuf_tensor("bqkv_sb", [1, 3072], BF16))
    ld(bqkv_sb[:], bqkv_h.ap(), sem=d_bias, q="gpsimd")
    P.wait("tensor", d_bias, d_bias.n)

    ld(cos_sb[:], cos_h.ap().rearrange("(g p) j -> p g j", p=128), sem=d_csB)
    ld(sin_sb[:], sin_h.ap().rearrange("(g p) j -> p g j", p=128), sem=d_csB)
    P.wait("vector", d_csB, d_csB.n)

    wq_src = wqkv_h.ap().rearrange("(c p) f -> p c f", p=128)
    wq_dst = wqkv_sb.ap().rearrange("p (c f) -> p c f", f=3072)
    for c in range(NC16):
        d_wq.n += 16
        h = d_wq.h
        P.emit("gpsimd", lambda e, c=c, h=h: e.dma_start(
            wq_dst[:, c, :], wq_src[:, c, :]).then_inc(h, 16))

    def pre(dst_ap, src_ap):
        d_pre.n += 16
        h = d_pre.h
        P.emit("gpsimd", lambda e, d=dst_ap, s=src_ap, h=h:
               e.dma_start(d, s).then_inc(h, 16))

    pre(d1b_h.ap(), d1s_h.ap())
    for oc in range(NC16):
        pre(d2b_h.ap()[oc].rearrange("p (h f) -> p h f", f=128),
            d2s_h.ap()[:, oc * 128 : (oc + 1) * 128]
            .rearrange("(h p) f -> p h f", p=128))
        pre(aggb_h.ap()[oc].rearrange("p (c f) -> p c f", f=128),
            aggw_h.ap()[:, oc * 128 : (oc + 1) * 128]
            .rearrange("(c p) f -> p c f", p=128))
        pre(bcb_h.ap()[oc].rearrange("p (c f) -> p c f", f=128),
            bcw_h.ap()[:, oc * 128 : (oc + 1) * 128]
            .rearrange("(c p) f -> p c f", p=128))
    PRE_CNT = d_pre.n

    def xt_slot(t, c):
        off = ((t % 2) * NC16 + c) * TQ
        return xt_ring[:, off : off + TQ]

    def xf_slot(g):
        return xf_ring[:, (g % 2) * D : (g % 2) * D + D]

    def emit_xload(g):
        if g >= 2:
            P.wait("sync", e_XT, 4 * (g - 1))
        d_xf.n += 16
        h = d_xf.h
        P.emit("sync", lambda e, g=g, h=h: e.dma_start(
            xf_slot(g), x_h.ap()[g * 128 : (g + 1) * 128, :]).then_inc(h, 16))

    def emit_xpose(t, s):
        g = t * 4 + s
        P.wait("tensor", d_xf, 16 * (g + 1))
        if t >= 2:
            P.wait("tensor", e_p1, (t - 1) * 8)
        for cg in range(4):
            gi4 = g * 4 + cg
            par = gi4 % 2
            if gi4 >= 2:
                P.wait("tensor", e_XP, gi4 - 1)
            hXT = e_XT.h
            for j in range(4):
                def tp(e, c=cg * 4 + j, par=par, j=j, g=g, last=(j == 3)):
                    ins = e.transpose(
                        xp_ps.ap()[:, par * 512 + j * 128 : par * 512 + (j + 1) * 128],
                        xf_slot(g)[:, c * 128 : (c + 1) * 128],
                        ident_f32[:])
                    if last:
                        ins.then_inc(hXT, 1)
                P.emit("tensor", tp)
            e_XT.n += 1
            P.wait("scalar", e_XT, e_XT.n)
            if t >= 2:
                P.wait("scalar", e_p1, (t - 1) * 8)
            slot0 = (t % 2) * NC16 + cg * 4
            dst = xt_ring.ap().rearrange("p (sl q) -> p sl q", q=TQ)[
                :, slot0 : slot0 + 4, s * 128 : (s + 1) * 128]
            e_XP.n += 1
            hXP = e_XP.h
            P.emit("scalar", lambda e, dst=dst, par=par: e.copy(
                dst, xp_ps.ap()[:, par * 512 : par * 512 + 512].rearrange(
                    "p (c f) -> p c f", f=128)).then_inc(hXP, 1))

    emit_xload(0)
    emit_xload(1)

    for t in range(NT):
        for s in range(4):
            if t * 4 + s + 2 < NSUB:
                emit_xload(t * 4 + s + 2)
            emit_xpose(t, s)
        for s in range(4):
            g = t * 4 + s
            par = g % 2
            qo = qkv_out[:, par * 3072 : par * 3072 + 3072]
            P.wait("tensor", e_XP, 4 * (g + 1))
            for half, ps, e_ev in ((0, psA, e_evA), (1, psB, e_evB)):
                if half == 0:
                    P.wait("tensor", e_evA, g)
                else:
                    P.wait("tensor", e_evB, 2 * g)
                for c in range(NC16):
                    lhs = xt_slot(t, c)[:, s * 128 : (s + 1) * 128]
                    P.wait("tensor", d_wq, 16 * (c + 1))
                    for sl in range(3):
                        col0 = half * 1536 + sl * 512
                        P.emit("tensor", lambda e, lhs=lhs, ps=ps, sl=sl, c=c, col0=col0:
                               e.matmul(ps.ap()[:, sl * 512 : sl * 512 + 512],
                                        lhs,
                                        wqkv_sb[:, c * 3072 + col0 : c * 3072 + col0 + 512],
                                        start=(c == 0), stop=False))
                e_p1.n += 1
                hp1 = e_p1.h
                for sl in range(3):
                    col0 = half * 1536 + sl * 512
                    last = sl == 2
                    def mmb(e, ps=ps, sl=sl, col0=col0, last=last):
                        ins = e.matmul(ps.ap()[:, sl * 512 : sl * 512 + 512],
                                       ones1[:],
                                       bqkv_sb[:, col0 : col0 + 512],
                                       start=False, stop=True)
                        if last:
                            ins.then_inc(hp1, 1)
                    P.emit("tensor", mmb)

                P.wait("vector", e_p1, e_p1.n)
                if g >= 2:
                    P.wait("vector", d_st, 96 * (g - 1))
                cos_ap = cos_sb[:, g * 64 : g * 64 + 64]
                sin_ap = sin_sb[:, g * 64 : g * 64 + 64]

                def rope(nheads, ps_off, out_off, inc_sem, ps=ps, qo=qo,
                         cos_ap=cos_ap, sin_ap=sin_ap):
                    pr = ps.ap()[:, ps_off : ps_off + nheads * 128].rearrange(
                        "p (h c) -> p h c", c=128)
                    xr = pr[:, :, 0:64]
                    xi = pr[:, :, 64:128]
                    cb = cos_ap.unsqueeze(1).broadcast_to([128, nheads, 64])
                    sb_ = sin_ap.unsqueeze(1).broadcast_to([128, nheads, 64])
                    t1 = rope_t.ap()[:, 0 : nheads * 64].rearrange(
                        "p (h c) -> p h c", c=64)
                    t2 = rope_t.ap()[:, 16 * 64 : 16 * 64 + nheads * 64].rearrange(
                        "p (h c) -> p h c", c=64)
                    t3 = rope_t.ap()[:, 32 * 64 : 32 * 64 + nheads * 64].rearrange(
                        "p (h c) -> p h c", c=64)
                    t4 = rope_t.ap()[:, 48 * 64 : 48 * 64 + nheads * 64].rearrange(
                        "p (h c) -> p h c", c=64)
                    ov = qo[:, out_off : out_off + nheads * 128].rearrange(
                        "p (h c) -> p h c", c=128)
                    P.emit("vector", lambda e: e.tensor_tensor(t1, xr, cb, ALU.mult))
                    P.emit("vector", lambda e: e.tensor_tensor(t2, xi, sb_, ALU.mult))
                    P.emit("vector", lambda e: e.tensor_tensor(t3, xr, sb_, ALU.mult))
                    P.emit("vector", lambda e: e.tensor_tensor(t4, xi, cb, ALU.mult))
                    P.emit("vector", lambda e: e.tensor_tensor(
                        ov[:, :, 0:64], t1, t2, ALU.subtract))
                    if inc_sem is None:
                        P.emit("vector", lambda e: e.tensor_tensor(
                            ov[:, :, 64:128], t3, t4, ALU.add))
                    else:
                        hh = inc_sem
                        P.emit("vector", lambda e: e.tensor_tensor(
                            ov[:, :, 64:128], t3, t4, ALU.add).then_inc(hh, 1))

                if half == 0:
                    e_evA.n += 1
                    rope(12, 0, 0, e_evA.h)
                else:
                    rope(4, 0, 1536, None)
                    e_evB.n += 1
                    rope(4, 512, 2048, e_evB.h)
                    P.wait("scalar", e_p1, e_p1.n)
                    if g >= 2:
                        P.wait("scalar", d_st, 96 * (g - 1))
                    e_evB.n += 1
                    hevB = e_evB.h
                    P.emit("scalar", lambda e, ps=ps, qo=qo: e.copy(
                        qo[:, 2560:3072], ps.ap()[:, 1024:1536]).then_inc(hevB, 1))

            P.wait("sync", e_evA, g + 1)
            P.wait("sync", e_evB, 2 * (g + 1))
            hst = d_st.h
            d_st.n += 16
            P.emit("sync", lambda e, g=g, qo=qo: e.dma_start(
                q_h.ap()[g * 128 : (g + 1) * 128, :], qo[:, 0:2048]).then_inc(hst, 16))
            d_st.n += 16
            P.emit("sync", lambda e, g=g, qo=qo: e.dma_start(
                k_h.ap()[g * 128 : (g + 1) * 128, :], qo[:, 2048:2560]).then_inc(hst, 16))
            for kv in range(NKV):
                d_st.n += 16
                P.emit("sync", lambda e, g=g, kv=kv, qo=qo: e.dma_start(
                    v_h.ap()[kv, g],
                    qo[:, 2560 + kv * 128 : 2560 + (kv + 1) * 128]).then_inc(hst, 16))

    P1_MM = e_p1.n
    P1_ST = d_st.n
    P1_EVA = e_evA.n
    P1_EVB = e_evB.n
    p1_stack.close()

    p2_stack = contextlib.ExitStack()
    wo_sb = p2_stack.enter_context(nc.sbuf_tensor("wo_sb", [128, NC16 * D], BF16))
    qt_ring = p2_stack.enter_context(nc.sbuf_tensor("qt_ring", [128, NH * TQ], BF16))
    kt_buf = p2_stack.enter_context(nc.sbuf_tensor("kt_buf", [128, 2 * T], BF16))
    v_buf = p2_stack.enter_context(nc.sbuf_tensor("v_buf", [128, 2 * T], BF16))
    pt_buf = p2_stack.enter_context(nc.sbuf_tensor("pt_buf", [128, 4 * 1024], BF16))
    ot_sb = p2_stack.enter_context(nc.sbuf_tensor("ot_sb", [128, NH * TQ], BF16))
    x_sb = p2_stack.enter_context(nc.sbuf_tensor("x_sb", [128, D], F32))
    x1_sb = p2_stack.enter_context(nc.sbuf_tensor("x1_sb", [128, D], F32))
    recip_sb = p2_stack.enter_context(nc.sbuf_tensor("recip_sb", [128, 2 * TQ], F32))
    acc2 = p2_stack.enter_context(nc.sbuf_tensor("acc2", [128, 4 * TQ], F32))
    thw_ring = p2_stack.enter_context(nc.sbuf_tensor("thw_ring", [128, 4 * D], BF16))
    thstg = p2_stack.enter_context(nc.sbuf_tensor("thstg", [128, D], F32))
    th_cols = p2_stack.enter_context(nc.sbuf_tensor("th_cols", [128, 96], F32))
    th_prm = p2_stack.enter_context(nc.sbuf_tensor("th_prm", [128, 7 * NC16], F32))
    th_d1bT = p2_stack.enter_context(nc.sbuf_tensor("th_d1bT", [128, 8], F32))
    th_pad = p2_stack.enter_context(nc.sbuf_tensor("th_pad", [128, 128], F32))
    th_sb128 = p2_stack.enter_context(nc.sbuf_tensor("th_sb128", [128, 128], F32))
    th_sdump = p2_stack.enter_context(nc.sbuf_tensor("th_sdump", [128, 16], F32))
    tb_sb = p2_stack.enter_context(nc.sbuf_tensor("tb_sb", [128, 128], F32))
    tn_sb = p2_stack.enter_context(nc.sbuf_tensor("tn_sb", [128, 128], F32))
    tnb_sb = p2_stack.enter_context(nc.sbuf_tensor("tnb_sb", [128, 128], BF16))
    hb_sb = p2_stack.enter_context(nc.sbuf_tensor("hb_sb", [128, 64], BF16))
    xraw_sb = p2_stack.enter_context(nc.sbuf_tensor("xraw_sb", [128, 128], F32))
    xnb_sb = p2_stack.enter_context(nc.sbuf_tensor("xnb_sb", [128, 128], BF16))
    arb_sb = p2_stack.enter_context(nc.sbuf_tensor("arb_sb", [128, 128], F32))
    bsel_sb = p2_stack.enter_context(nc.sbuf_tensor("bsel_sb", [128, 8], F32))

    sp0 = p2_stack.enter_context(nc.psum_tensor("sp0", [128, 1024], F32))
    sp1 = p2_stack.enter_context(nc.psum_tensor("sp1", [128, 1024], F32))
    o_ps = p2_stack.enter_context(nc.psum_tensor("o_ps", [128, TQ], F32))
    sm_ps = p2_stack.enter_context(nc.psum_tensor("sm_ps", [128, TQ], F32))
    op_ps = p2_stack.enter_context(nc.psum_tensor("op_ps", [128, TQ], F32))
    th_ps = p2_stack.enter_context(nc.psum_tensor("th_ps", [128, 384], F32))

    P.wait("sync", e_p1, P1_MM)
    P.wait("sync", e_evA, P1_EVA)
    P.wait("sync", e_evB, P1_EVB)
    ld(th_prm[:, 0:16], ingT_h.ap(), sem=d_csC)
    ld(th_prm[:, 16:32], inbT_h.ap(), sem=d_csC)
    ld(th_prm[:, 32:48], pngT_h.ap(), sem=d_csC)
    ld(th_prm[:, 48:64], pnbT_h.ap(), sem=d_csC)
    ld(th_prm[:, 64:80], aggbT_h.ap(), sem=d_csC)
    ld(th_prm[:, 80:96], d2bT_h.ap(), sem=d_csC)
    ld(th_prm[:, 96:112], bcbT_h.ap(), sem=d_csC)
    ld(th_d1bT[:], d1bT_h.ap(), sem=d_csC)
    ld(bsel_sb[:], bsel_h.ap().unsqueeze(0).partition_broadcast(128), sem=d_csC)
    TH_PRM_LDS = d_csC.n

    P.wait("gpsimd", e_p1, P1_MM)
    P.wait("gpsimd", e_evA, P1_EVA)
    P.wait("gpsimd", e_evB, P1_EVB)
    wo_src = wo_h.ap().rearrange("(c p) f -> p c f", p=128)
    wo_dst = wo_sb.ap().rearrange("p (c f) -> p c f", f=D)
    for c in range(NC16):
        d_wo.n += 16
        h = d_wo.h
        P.emit("gpsimd", lambda e, c=c, h=h: e.dma_start(
            wo_dst[:, c, :], wo_src[:, c, :]).then_inc(h, 16))

    if debug:
        dbg_ot = nc.dram_tensor("dbg_ot", [NH, 128, TQ], BF16, kind="ExternalOutput")
        dbg_sums = nc.dram_tensor("dbg_sums", [NH, 128, TQ], F32, kind="ExternalOutput")
        dbg_recip = nc.dram_tensor("dbg_recip", [NH, 128, TQ], F32, kind="ExternalOutput")
        dbg_r1 = nc.dram_tensor("dbg_r1", [NSUB, 128, D], F32, kind="ExternalOutput")
        dbg_lnv = nc.dram_tensor("dbg_lnv", [NSUB, 128, 32], F32, kind="ExternalOutput")
        d_dbg = S("d_dbg")

    tile_order = [0, NT - 1] + list(range(1, NT - 1))

    NB = 8
    C_SU, C_SQ2, C_RS2, C_NM2 = 0, 8, 16, 24
    R_MU, R_VAR, R_RS, R_NM, R_TMP = 40, 48, 56, 64, 72

    def th_step(eng, fn, waits=()):
        for sem, val in waits:
            P.wait(eng, sem, val)
        e_TH.n += 1
        h = e_TH.h
        if eng == "tensor":
            P.emit(eng, lambda e, fn=fn, h=h: fn(e).then_inc(h, 1))
        else:
            def run(e, fn=fn, h=h):
                fn(e)
                e.drain().then_inc(h, 1)
            P.emit(eng, run)

    def th_ln(src, dst_bf, g_off, b_off, eps_c):
        src3 = src.ap().rearrange("p (c b) -> p c b", b=NB)
        tn3 = tn_sb.ap().rearrange("p (c b) -> p c b", b=NB)
        for b in range(NB):
            th_step("vector", lambda e, b=b: e.reduce_sum(
                th_cols[:, C_SU + b : C_SU + b + 1], src3[:, :, b], axis=AX.X)
                and e.drain(),
                waits=([(e_TH, e_TH.n)] if b == 0 else ()))
        for b in range(NB):
            th_step("scalar", lambda e, b=b: e.activation(
                th_sdump[:], src3[:, :, b], AF.Square, bias=0.0, scale=1.0,
                accum_out=th_cols[:, C_SQ2 + b : C_SQ2 + b + 1]) and e.drain(),
                waits=([(e_TH, e_TH.n)] if b == 0 else ()))
        th_step("tensor", lambda e: e.matmul(
            th_ps.ap()[0:1, 256:272], onescol[:], th_cols[:, 0:16],
            start=True, stop=True), waits=[(e_TH, e_TH.n)])
        th_step("vector", lambda e: e.tensor_scalar(
            th_cols[0:1, R_MU : R_MU + 16], th_ps.ap()[0:1, 256:272],
            1.0 / D, None, ALU.mult), waits=[(e_TH, e_TH.n)])
        th_step("vector", lambda e: e.tensor_tensor(
            th_cols[0:1, R_TMP : R_TMP + 8], th_cols[0:1, R_MU : R_MU + 8],
            th_cols[0:1, R_MU : R_MU + 8], ALU.mult))
        th_step("vector", lambda e: e.tensor_tensor(
            th_cols[0:1, R_VAR : R_VAR + 8], th_cols[0:1, R_VAR : R_VAR + 8],
            th_cols[0:1, R_TMP : R_TMP + 8], ALU.subtract))
        th_step("scalar", lambda e, eps_c=eps_c: e.activation(
            th_cols[0:1, R_RS : R_RS + 8], th_cols[0:1, R_VAR : R_VAR + 8],
            AF.Ln, bias=eps_c, scale=1.0), waits=[(e_TH, e_TH.n)])
        th_step("scalar", lambda e: e.activation(
            th_cols[0:1, R_RS : R_RS + 8], th_cols[0:1, R_RS : R_RS + 8],
            AF.Exp, bias=0.0, scale=cNHr))
        th_step("vector", lambda e: e.scalar_tensor_tensor(
            th_cols[0:1, R_NM : R_NM + 8], th_cols[0:1, R_MU : R_MU + 8],
            C_M1, th_cols[0:1, R_RS : R_RS + 8], ALU.mult, ALU.mult),
            waits=[(e_TH, e_TH.n)])
        th_step("tensor", lambda e: e.matmul(
            th_ps.ap()[:, 272:288], onesrow[:], th_cols[0:1, R_RS : R_RS + 16],
            start=True, stop=True), waits=[(e_TH, e_TH.n)])
        th_step("vector", lambda e: e.tensor_copy(
            th_cols[:, C_RS2 : C_RS2 + 16], th_ps.ap()[:, 272:288]),
            waits=[(e_TH, e_TH.n)])
        for b in range(NB):
            th_step("scalar", lambda e, b=b: e.activation(
                tn3[:, :, b], src3[:, :, b], AF.Identity,
                bias=th_cols[:, C_NM2 + b : C_NM2 + b + 1],
                scale=th_cols[:, C_RS2 + b : C_RS2 + b + 1]),
                waits=([(e_TH, e_TH.n)] if b == 0 else ()))
        for ci in range(NC16):
            th_step("scalar", lambda e, ci=ci, dst_bf=dst_bf: e.activation(
                dst_bf[:, ci * 8 : ci * 8 + 8], tn_sb[:, ci * 8 : ci * 8 + 8],
                AF.Identity, bias=th_prm[:, b_off + ci : b_off + ci + 1],
                scale=th_prm[:, g_off + ci : g_off + ci + 1]),
                waits=([(e_TH, e_TH.n)] if ci == 0 else ()))

    slab_i = [0]
    slab_war = {0: 0, 1: 0, 2: 0, 3: 0}

    def th_slab(src_fn, mm_fn, width=D):
        slot = slab_i[0] % 4
        slab_i[0] += 1
        sem = d_thr[slot]
        P.wait("gpsimd", e_TH, slab_war[slot])
        P.wait("gpsimd", d_pre, PRE_CNT)
        sem.n += 16
        h = sem.h
        dst = thw_ring[:, slot * D : slot * D + width]
        P.emit("gpsimd", lambda e, dst=dst, src=src_fn(), h=h:
               e.dma_start(dst, src).then_inc(h, 16))
        P.wait("tensor", sem, sem.n)
        P.wait("tensor", e_TH, e_TH.n)
        mm_fn(dst)
        slab_war[slot] = e_TH.n

    def f_ag():
        for eng in ("vector", "scalar", "tensor"):
            P.wait(eng, d_csC, TH_PRM_LDS)
        P.wait("gpsimd", d_x1F, 16 * 4)
        P.wait("gpsimd", d_x1R, 16 * 4)
        d_arb.n += 16
        ha0 = d_arb.h
        P.emit("gpsimd", lambda e, ha0=ha0: e.dma_start(
            agin_h.ap(), x1_h.ap()[T - 1 : T, :]).then_inc(ha0, 16))
        P.wait("gpsimd", d_arb, d_arb.n)
        e_CC.n += 1
        hc = e_CC.h
        P.emit("gpsimd", lambda e, hc=hc: e.collective_compute(
            "AllGather", ALU.bypass, replica_groups=[list(range(8))],
            ins=[agin_h.ap().opt()],
            outs=[agout_h.ap().opt()]).then_inc(hc, 1))
        P.wait("gpsimd", e_CC, e_CC.n)
        d_arb.n += 16
        ha = d_arb.h
        P.emit("gpsimd", lambda e, ha=ha: e.dma_start(
            thstg[0:8, :], agout_h.ap()).then_inc(ha, 16))

    def f_tx():
        for ci in range(NC16):
            th_step("tensor", lambda e, ci=ci: e.transpose(
                th_ps.ap()[:, 128:256], thstg[:, ci * 128 : (ci + 1) * 128],
                ident_f32[:]), waits=[(d_arb, d_arb.n), (e_TH, e_TH.n)])
            th_step("vector", lambda e, ci=ci: e.tensor_copy(
                xraw_sb[:, ci * 8 : ci * 8 + 8], th_ps.ap()[:, 128:136]),
                waits=[(e_TH, e_TH.n)])

    def f_agg(ocs, w_h, mov_sb):
        for oc in ocs:
            def mm(slab, oc=oc):
                hTH = e_TH.h
                for ci in range(NC16):
                    def m(e, oc=oc, ci=ci, slab=slab,
                          last=(ci == NC16 - 1)):
                        ins = e.matmul(
                            th_ps.ap()[:, 128 + oc * 8 : 128 + oc * 8 + 8],
                            slab[:, ci * 128 : (ci + 1) * 128],
                            mov_sb[:, ci * 8 : ci * 8 + 8],
                            start=(ci == 0), stop=(ci == NC16 - 1))
                        if last:
                            ins.then_inc(hTH, 1)
                    P.emit("tensor", m)
                e_TH.n += 1
            th_slab(lambda oc=oc: w_h.ap()[oc], mm)

    def f_agg_ev():
        for ci in range(NC16):
            th_step("scalar", lambda e, ci=ci: e.activation(
                tb_sb[:, ci * 8 : ci * 8 + 8],
                th_ps.ap()[:, 128 + ci * 8 : 128 + ci * 8 + 8],
                AF.Identity, bias=th_prm[:, 64 + ci : 64 + ci + 1],
                scale=1.0), waits=([(e_TH, e_TH.n)] if ci == 0 else ()))

    def f_d1(js):
        for hjl in js:
            def mm(slab, hjl=hjl):
                hTH = e_TH.h
                for ci in range(NC16):
                    def m(e, ci=ci, hjl=hjl, slab=slab,
                          last=(ci == NC16 - 1)):
                        ins = e.matmul(
                            th_ps.ap()[:, hjl * 8 : hjl * 8 + 8],
                            slab[:, ci * 128 : (ci + 1) * 128],
                            tnb_sb[:, ci * 8 : ci * 8 + 8],
                            start=(ci == 0), stop=(ci == NC16 - 1))
                        if last:
                            ins.then_inc(hTH, 1)
                    P.emit("tensor", m)
                e_TH.n += 1
            th_slab(lambda hjl=hjl: d1b_h.ap()[hjl], mm)
            th_step("scalar", lambda e, hjl=hjl: e.activation(
                hb_sb[:, hjl * 8 : hjl * 8 + 8],
                th_ps.ap()[:, hjl * 8 : hjl * 8 + 8], AF.Gelu,
                bias=th_d1bT[:, hjl : hjl + 1], scale=1.0),
                waits=[(e_TH, e_TH.n)])

    def f_d2(ocs):
        for oc in ocs:
            def mm(slab, oc=oc):
                hTH = e_TH.h
                for hjl in range(8):
                    def m(e, oc=oc, hjl=hjl, slab=slab, last=(hjl == 7)):
                        ins = e.matmul(
                            th_ps.ap()[:, 128 + oc * 8 : 128 + oc * 8 + 8],
                            slab[:, hjl * 128 : (hjl + 1) * 128],
                            hb_sb[:, hjl * 8 : hjl * 8 + 8],
                            start=(hjl == 0), stop=(hjl == 7))
                        if last:
                            ins.then_inc(hTH, 1)
                    P.emit("tensor", m)
                e_TH.n += 1
            th_slab(lambda oc=oc: d2b_h.ap()[oc], mm, width=1024)

    def f_ar(k):
        th_step("vector", lambda e: e.tensor_copy(
            arb_sb[:], th_ps.ap()[:, 128:256]), waits=[(e_TH, e_TH.n)])
        P.wait("gpsimd", e_TH, e_TH.n)
        d_arb.n += 16
        ha = d_arb.h
        P.emit("gpsimd", lambda e, k=k, ha=ha: e.dma_start(
            arin_h[k].ap(), arb_sb[:]).then_inc(ha, 16))
        P.wait("gpsimd", d_arb, d_arb.n)
        e_CC.n += 1
        hc = e_CC.h
        P.emit("gpsimd", lambda e, k=k, hc=hc: e.collective_compute(
            "AllReduce", ALU.add, replica_groups=[list(range(8))],
            ins=[arin_h[k].ap().opt()],
            outs=[arout_h[k].ap().opt()]).then_inc(hc, 1))
        P.wait("gpsimd", e_CC, e_CC.n)
        d_arb.n += 16
        P.emit("gpsimd", lambda e, k=k, ha=ha: e.dma_start(
            arb_sb[:], arout_h[k].ap()).then_inc(ha, 16))

    def f_add():
        for ci in range(NC16):
            th_step("scalar", lambda e, ci=ci: e.activation(
                arb_sb[:, ci * 8 : ci * 8 + 8],
                arb_sb[:, ci * 8 : ci * 8 + 8],
                AF.Identity, bias=th_prm[:, 80 + ci : 80 + ci + 1],
                scale=1.0),
                waits=([(d_arb, d_arb.n), (e_TH, e_TH.n)] if ci == 0 else ()))
        th_step("vector", lambda e: e.tensor_tensor(
            tb_sb[:], tb_sb[:], arb_sb[:, 0:128], ALU.add),
            waits=[(e_TH, e_TH.n)])

    def f_thb():
        th_step("scalar", lambda e: e.copy(tnb_sb[:], tb_sb[:]),
                waits=[(e_TH, e_TH.n)])

    def f_bsel():
        arb3 = arb_sb.ap().rearrange("p (c b) -> p c b", b=NB)
        for ci in range(NC16):
            th_step("scalar", lambda e, ci=ci: e.activation(
                arb_sb[:, ci * 8 : ci * 8 + 8],
                th_ps.ap()[:, 128 + ci * 8 : 128 + ci * 8 + 8],
                AF.Identity, bias=th_prm[:, 96 + ci : 96 + ci + 1],
                scale=1.0), waits=([(e_TH, e_TH.n)] if ci == 0 else ()))
        th_step("vector", lambda e, arb3=arb3: e.tensor_tensor(
            arb3, arb3,
            bsel_sb.ap().unsqueeze(1).broadcast_to([128, NC16, NB]),
            ALU.mult), waits=[(e_TH, e_TH.n)])
        th_step("vector", lambda e, arb3=arb3: e.tensor_reduce(
            th_cols[:, 0:16], arb3, AX.X, ALU.add))
        th_step("vector", lambda e: e.memset(th_pad[:], 0.0))
        th_step("vector", lambda e: e.tensor_copy(
            th_pad[:, 0:16], th_cols[:, 0:16]))
        th_step("tensor", lambda e: e.transpose(
            th_ps.ap()[:, 128:256], th_pad[:], ident_f32[:]),
            waits=[(e_TH, e_TH.n)])
        th_step("vector", lambda e: e.tensor_copy(
            th_sb128[:], th_ps.ap()[:, 128:256]), waits=[(e_TH, e_TH.n)])
        P.wait("gpsimd", e_TH, e_TH.n)
        d_bcr.n += 16
        hbc = d_bcr.h
        P.emit("gpsimd", lambda e: e.dma_start(
            bcrow_h.ap().rearrange("(c f) -> c f", f=128),
            th_sb128[0:16, :]).then_inc(hbc, 16))

    fragments = []

    def piece(fn):
        fragments.append(fn)

    d_dmp = S("d_dmp")

    def th_dump(slot, src):
        def fr():
            P.wait("gpsimd", e_TH, e_TH.n)
            d_dmp.n += 16
            hd = d_dmp.h
            P.emit("gpsimd", lambda e, slot=slot, src=src, hd=hd: e.dma_start(
                dbg_th_h.ap()[slot], src).then_inc(hd, 16))
        return fr

    piece(f_ag)
    piece(f_tx)
    piece(lambda: th_ln(xraw_sb, xnb_sb, 0, 16, cEPS6r))
    for g_ in range(8):
        piece(lambda g_=g_: f_agg(range(2 * g_, 2 * g_ + 2), aggb_h, xnb_sb))
    piece(f_agg_ev)
    for _k in range(4):
        piece(lambda: th_ln(tb_sb, tnb_sb, 32, 48, cEPS6r))
        for g_ in range(4):
            piece(lambda g_=g_: f_d1(range(2 * g_, 2 * g_ + 2)))
        for g_ in range(4):
            piece(lambda g_=g_: f_d2(range(4 * g_, 4 * g_ + 4)))
        piece(lambda _k=_k: f_ar(_k))
        piece(f_add)
    piece(f_thb)
    for g_ in range(8):
        piece(lambda g_=g_: f_agg(range(2 * g_, 2 * g_ + 2), bcb_h, tnb_sb))
    piece(f_bsel)

    pair_i = [0]
    sp_last = {0: 0, 1: 0}

    def emit_qt_loads(ti, tile):
        tstart = tile * TQ
        P.wait("sync", d_st, 96 * 4 * (tile + 1))
        for qh in range(NH):
            if ti >= 1:
                P.wait("sync", e_QH, (ti - 1) * NH + qh + 1)
            d_qt[qh].n += 16
            h = d_qt[qh].h
            P.emit("sync", lambda e, qh=qh, tstart=tstart, h=h: e.dma_start(
                qt_ring[:, qh * TQ : (qh + 1) * TQ],
                q_h.ap()[tstart : tstart + TQ, qh * 128 : (qh + 1) * 128],
                transpose=True).then_inc(h, 16))

    def emit_kv_loads(ti, tile):
        tstart = tile * TQ
        keys = tstart + TQ
        nblk = keys // 128
        for kvh in range(NKV):
            slot = ti * NKV + kvh
            par = slot % 2
            if slot >= 2:
                P.wait("sync", e_KV, slot - 1)
            d_kv[par].n += 32
            h = d_kv[par].h
            P.emit("sync", lambda e, kvh=kvh, par=par, keys=keys, h=h: e.dma_start(
                kt_buf[:, par * T : par * T + keys],
                k_h.ap()[0:keys, kvh * 128 : (kvh + 1) * 128],
                transpose=True).then_inc(h, 16))
            P.emit("sync", lambda e, kvh=kvh, par=par, nblk=nblk, h=h: e.dma_start(
                v_buf.ap()[:, par * T : par * T + nblk * 128].rearrange(
                    "p (b f) -> p b f", f=128),
                v_h.ap()[kvh, 0:nblk].rearrange("b p f -> p b f")).then_inc(h, 16))

    emit_qt_loads(0, tile_order[0])
    emit_kv_loads(0, tile_order[0])

    for ti, tile in enumerate(tile_order):
        tstart = tile * TQ
        nch = (tstart + TQ) // 128
        if ti + 1 < NT:
            emit_qt_loads(ti + 1, tile_order[ti + 1])
            emit_kv_loads(ti + 1, tile_order[ti + 1])

        for kvh in range(NKV):
            kvslot = ti * NKV + kvh
            par_kv = kvslot % 2
            P.wait("tensor", d_kv[kvslot % 2], 32 * (kvslot // 2 + 1))
            for qp in range(2):
                qA = kvh * 4 + 2 * qp
                npairs = nch // 2
                HS = {}
                for h in (0, 1):
                    qh_ = qA + h
                    P.wait("tensor", d_qt[qh_], 16 * (ti + 1))
                    HS[h] = dict(qh=qh_, qidx=ti * NH + qh_,
                                 sp=(sp0 if h == 0 else sp1),
                                 ob=(o_ps if h == 0 else op_ps), pslot=2 * h,
                                 gate=[None] * npairs, ecnt=[0] * npairs,
                                 avcnt=[0] * npairs, accnt=[0] * npairs)

                def emit_S(h, p):
                    st = HS[h]
                    c0, c1 = 2 * p, 2 * p + 1
                    a0 = max(0, c0 * 128 - tstart)
                    sp = st["sp"]
                    P.wait("tensor", e_E,
                           st["ecnt"][p - 1] if p >= 1 else sp_last[h])
                    hS = e_S.h
                    for hh, c in ((0, c0), (1, c1)):
                        def mmS(e, sp=sp, hh=hh, c=c, a0=a0, par_kv=par_kv,
                                qh=st["qh"], last=(hh == 1)):
                            ins = e.matmul(
                                sp.ap()[:, hh * 512 + a0 : hh * 512 + 512],
                                kt_buf[:, par_kv * T + c * 128 : par_kv * T + (c + 1) * 128],
                                qt_ring[:, qh * TQ + a0 : (qh + 1) * TQ],
                                start=True, stop=True)
                            if last:
                                ins.then_inc(hS, 1)
                        P.emit("tensor", mmS)
                    e_S.n += 1
                    po = (st["pslot"] + p % 2) * 1024
                    ptb = pt_buf[:, po : po + 1024]
                    P.wait("scalar", e_S, e_S.n)
                    if p >= 2:
                        P.wait("scalar", e_AV, st["avcnt"][p - 2])
                        P.wait("scalar", e_ACv, st["accnt"][p - 2])
                    e_E.n += 1
                    hE = e_E.h
                    P.emit("scalar", lambda e, sp=sp, ptb=ptb, a0=a0:
                           e.activation(
                               ptb.rearrange("p (h f) -> p h f", f=512)[:, :, a0:512],
                               sp.ap().rearrange("p (h f) -> p h f", f=512)[:, :, a0:512],
                               AF.Exp, bias=cM40, scale=1.0).then_inc(hE, 1))
                    st["ecnt"][p] = e_E.n
                    diag = [(hh, c) for hh, c in ((0, c0), (1, c1))
                            if 0 <= c * 128 - tstart < TQ]
                    if diag:
                        P.wait("vector", e_E, e_E.n)
                        e_M.n += 1
                        hM = e_M.h
                        for idx, (hh, c) in enumerate(diag):
                            doff = c * 128 - tstart
                            def mk(e, hh=hh, doff=doff, ptb=ptb,
                                   last=(idx == len(diag) - 1)):
                                ins = e.tensor_tensor(
                                    ptb[:, hh * 512 + doff : hh * 512 + doff + 128],
                                    ptb[:, hh * 512 + doff : hh * 512 + doff + 128],
                                    trimask[:], ALU.mult)
                                if last:
                                    ins.then_inc(hM, 1)
                            P.emit("vector", mk)
                        st["gate"][p] = ("M", e_M.n)
                    else:
                        st["gate"][p] = ("E", e_E.n)

                def emit_AV(h, p):
                    st = HS[h]
                    c0, c1 = 2 * p, 2 * p + 1
                    a0 = max(0, c0 * 128 - tstart)
                    a1 = max(0, c1 * 128 - tstart)
                    po = (st["pslot"] + p % 2) * 1024
                    ptb = pt_buf[:, po : po + 1024]
                    gate = st["gate"][p]
                    P.wait("tensor", e_M if gate[0] == "M" else e_E, gate[1])
                    if p == 0:
                        P.wait("tensor", e_OE, st["qidx"] - 1)
                        if h == 1:
                            P.wait("tensor", e_OPE, 16 * ti)
                    e_AV.n += 1
                    hAV = e_AV.h
                    ilp = p == npairs - 1
                    for hh, c, ac in ((0, c0, a0), (1, c1, a1)):
                        def mmO(e, c=c, ac=ac, hh=hh, ptb=ptb, ob=st["ob"],
                                last=(hh == 1), ilp=ilp, h=h, qp=qp,
                                par_kv=par_kv, nch=nch):
                            ins = e.matmul(ob.ap()[:, ac:512],
                                           v_buf[:, par_kv * T + c * 128 : par_kv * T + (c + 1) * 128],
                                           ptb[:, hh * 512 + ac : hh * 512 + 512],
                                           start=(c == 0), stop=(c == nch - 1))
                            if last:
                                ins.then_inc(hAV, 1)
                                if ilp:
                                    e.nop().then_inc(e_QH.h, 1)
                                    if h == 1 and qp == 1:
                                        e.nop().then_inc(e_KV.h, 1)
                        P.emit("tensor", mmO)
                    st["avcnt"][p] = e_AV.n
                    if ilp:
                        e_QH.n += 1
                        if h == 1 and qp == 1:
                            e_KV.n += 1

                def emit_ACC(h, p):
                    st = HS[h]
                    c0, c1 = 2 * p, 2 * p + 1
                    a0 = max(0, c0 * 128 - tstart)
                    a1 = max(0, c1 * 128 - tstart)
                    po = (st["pslot"] + p % 2) * 1024
                    ptb = pt_buf[:, po : po + 1024]
                    ao = (st["qidx"] % 4) * 512
                    acc = acc2[:, ao : ao + 512]
                    gate = st["gate"][p]
                    P.wait("vector", e_M if gate[0] == "M" else e_E, gate[1])
                    if p == 0:
                        P.wait("vector", e_SM, st["qidx"] - 3)
                    e_ACv.n += 1
                    hA = e_ACv.h

                    def ops(e, first=(p == 0), a0=a0, a1=a1, acc=acc, ptb=ptb,
                            hA=hA):
                        if first:
                            e.tensor_copy(acc[:, a0:512], ptb[:, a0:512])
                        else:
                            e.tensor_tensor(acc[:, a0:512], acc[:, a0:512],
                                            ptb[:, a0:512], ALU.add)
                        e.tensor_tensor(acc[:, a1:512], acc[:, a1:512],
                                        ptb[:, 512 + a1 : 1024],
                                        ALU.add).then_inc(hA, 1)
                    P.emit("vector", ops)
                    st["accnt"][p] = e_ACv.n

                def finish(h):
                    st = HS[h]
                    P.wait("tensor", e_ACv, st["accnt"][npairs - 1])
                    P.wait("tensor", e_SE, st["qidx"])
                    e_SM.n += 1
                    hSM = e_SM.h
                    ao = (st["qidx"] % 4) * 512
                    P.emit("tensor", lambda e, ao=ao, hSM=hSM: e.matmul(
                        sm_ps.ap()[:], ones2df[:], acc2[:, ao : ao + 512],
                        start=True, stop=True).then_inc(hSM, 1))
                    sp_last[h] = st["ecnt"][npairs - 1]
                    ro = (st["qidx"] % 2) * TQ
                    P.wait("vector", e_SM, e_SM.n)
                    P.wait("vector", e_AV, st["avcnt"][npairs - 1])
                    if ti >= 1:
                        P.wait("vector", e_OP, ti * 16)
                    e_SE.n += 1
                    hSE = e_SE.h
                    P.emit("vector", lambda e, ro=ro: e.reciprocal(
                        recip_sb[:, ro : ro + TQ], sm_ps.ap()[:]).then_inc(hSE, 1))
                    e_OE.n += 1
                    hOE = e_OE.h
                    P.emit("vector", lambda e, qh=st["qh"], ob=st["ob"], ro=ro:
                           e.scalar_tensor_tensor(
                        ot_sb[:, qh * TQ : (qh + 1) * TQ], ob.ap()[:],
                        0.02209708691207961, recip_sb[:, ro : ro + TQ],
                        ALU.mult, ALU.mult).then_inc(hOE, 1))
                    if ti >= 2 and fragments:
                        fragments.pop(0)()
                    if ti >= 4 and fragments:
                        fragments.pop(0)()

                emit_S(0, 0)
                emit_S(1, 0)
                for p in range(npairs):
                    if p + 1 < npairs:
                        emit_S(0, p + 1)
                    emit_AV(0, p)
                    emit_ACC(0, p)
                    if p + 1 < npairs:
                        emit_S(1, p + 1)
                    emit_AV(1, p)
                    emit_ACC(1, p)
                finish(0)
                finish(1)

        P.wait("tensor", d_wo, 16 * NC16)
        _ = None
        for s in range(4):
            g = tile * 4 + s
            gi = ti * 4 + s
            par_x = gi % 2
            P.wait("scalar", e_OPE, 4 * gi)
            d_x.n += 16
            hx = d_x.h
            P.emit("scalar", lambda e, g=g, hx=hx: e.dma_start(
                x_sb[:],
                x_h.ap()[g * 128 : (g + 1) * 128, :]).then_inc(hx, 16))
            for oc in range(4):
                qtr = gi * 4 + oc
                P.wait("tensor", e_OE, (ti + 1) * NH)
                P.wait("tensor", e_OPE, qtr)
                for dc in range(NC16):
                    P.emit("tensor", lambda e, dc=dc, s=s, oc=oc:
                           e.matmul(op_ps.ap()[:],
                                    ot_sb[:, dc * TQ + s * 128 : dc * TQ + (s + 1) * 128],
                                    wo_sb[:, dc * D + oc * 512 : dc * D + (oc + 1) * 512],
                                    start=(dc == 0), stop=False))
                e_OP.n += 1
                hOP = e_OP.h
                P.emit("tensor", lambda e, oc=oc: e.matmul(
                    op_ps.ap()[:], ones1[:], bo_sb[:, oc * 512 : (oc + 1) * 512],
                    start=False, stop=True).then_inc(hOP, 1))
                P.wait("vector", e_OP, e_OP.n)
                P.wait("vector", d_x, 16 * (gi + 1))
                P.wait("vector", d_x1F, 16 * min(gi, 4))
                P.wait("vector", d_x1R, 16 * max(0, gi - 4))
                P.wait("vector", d_x1b, 16 * gi)
                e_OPE.n += 1
                hOPE = e_OPE.h
                P.emit("vector", lambda e, oc=oc: e.scalar_tensor_tensor(
                    x1_sb[:, oc * 512 : (oc + 1) * 512], op_ps.ap()[:], 1.0,
                    x_sb[:, oc * 512 : (oc + 1) * 512],
                    ALU.mult, ALU.add,
                    accum_out=lnv[:, L_R1 + oc : L_R1 + oc + 1]).then_inc(hOPE, 1))
            if debug:
                P.wait("scalar", e_OPE, e_OPE.n)
                d_dbg.n += 16
                hdbg2 = d_dbg.h
                P.emit("scalar", lambda e, g=g, hdbg2=hdbg2: e.dma_start(
                    dbg_r1.ap()[g], x1_sb[:]).then_inc(hdbg2, 16))
                P.wait("scalar", d_dbg, d_dbg.n)
            P.wait("scalar", e_OPE, e_OPE.n)
            e_LN.n += 1
            hLN = e_LN.h
            P.emit("scalar", lambda e: e.activation(
                x_sb[:], x1_sb[:], AF.Square, bias=0.0, scale=1.0,
                accum_out=lnv[:, L_SQ : L_SQ + 1]))
            P.emit("scalar", lambda e: e.drain().then_inc(hLN, 1))
            P.wait("vector", e_LN, e_LN.n)
            e_LN.n += 1
            def _ln1v(e):
                e.drain()
                e.tensor_reduce(lnv[:, L_SUM : L_SUM + 1],
                                lnv[:, L_R1 : L_R1 + 4], AX.X, ALU.add)
                e.drain()
                e.tensor_scalar(lnv[:, L_MU : L_MU + 1],
                                lnv[:, L_SUM : L_SUM + 1], 1.0 / D, None, ALU.mult)
                e.tensor_scalar(lnv[:, L_VAR : L_VAR + 1],
                                lnv[:, L_SQ : L_SQ + 1], 1.0 / D, None, ALU.mult)
                e.drain()
                e.tensor_tensor(lnv[:, L_TMP : L_TMP + 1],
                                lnv[:, L_MU : L_MU + 1],
                                lnv[:, L_MU : L_MU + 1], ALU.mult)
                e.drain()
                e.tensor_tensor(lnv[:, L_VAR : L_VAR + 1],
                                lnv[:, L_VAR : L_VAR + 1],
                                lnv[:, L_TMP : L_TMP + 1], ALU.subtract)
                e.drain().then_inc(hLN, 1)
            P.emit("vector", _ln1v)
            P.wait("scalar", e_LN, e_LN.n)
            e_LN.n += 1
            def _ln1s(e):
                e.activation(lnv[:, L_RSTD : L_RSTD + 1],
                             lnv[:, L_VAR : L_VAR + 1], AF.Ln, bias=cEPS5, scale=1.0)
                e.drain()
                e.activation(lnv[:, L_RSTD : L_RSTD + 1],
                             lnv[:, L_RSTD : L_RSTD + 1], AF.Exp, bias=0.0, scale=cNH)
                e.drain().then_inc(hLN, 1)
            P.emit("scalar", _ln1s)
            P.wait("vector", e_LN, e_LN.n)
            e_LN.n += 1
            def _ln1n(e):
                e.scalar_tensor_tensor(lnv[:, L_NMR : L_NMR + 1],
                                       lnv[:, L_MU : L_MU + 1], C_M1,
                                       lnv[:, L_RSTD : L_RSTD + 1], ALU.mult, ALU.mult)
                e.drain().then_inc(hLN, 1)
            P.emit("vector", _ln1n)
            P.wait("scalar", e_LN, e_LN.n)
            if debug:
                d_dbg.n += 16
                hdbg3 = d_dbg.h
                P.emit("scalar", lambda e, g=g, hdbg3=hdbg3: e.dma_start(
                    dbg_lnv.ap()[g], lnv[:]).then_inc(hdbg3, 16))
                P.wait("scalar", d_dbg, d_dbg.n)
            e_LN.n += 1
            P.emit("scalar", lambda e: e.activation(
                x1_sb[:], x1_sb[:], AF.Identity, bias=lnv[:, L_NMR : L_NMR + 1],
                scale=lnv[:, L_RSTD : L_RSTD + 1]).then_inc(hLN, 1))
            P.wait("scalar", e_LN, e_LN.n)
            sx1 = d_x1F if gi < 4 else d_x1R
            sx1.n += 16
            hx1 = sx1.h
            P.emit("scalar", lambda e, g=g, hx1=hx1: e.dma_start(
                x1_h.ap()[g * 128 : (g + 1) * 128, :], x1_sb[:]).then_inc(hx1, 16))
            P.wait("gpsimd", e_LN, e_LN.n)
            d_x1b.n += 16
            hx1b = d_x1b.h
            P.emit("gpsimd", lambda e, g=g: e.dma_start(
                x1b_h.ap()[g * 128 : (g + 1) * 128, :], x1_sb[:]).then_inc(hx1b, 16))

    while fragments:
        fragments.pop(0)()
    P.wait("gpsimd", d_dmp, d_dmp.n)

    P2_OP = e_OP.n
    P2_OPE = e_OPE.n
    P2_AV = e_AV.n
    P2_OE = e_OE.n
    P2_SE = e_SE.n
    P2_LN = e_LN.n
    TH_ALL = e_TH.n
    p2_stack.close()

    p3_stack = contextlib.ExitStack()
    gw_sb = p3_stack.enter_context(nc.sbuf_tensor("gw_sb", [128, NC16 * D], BF16))
    x1t_ring = p3_stack.enter_context(
        nc.sbuf_tensor("x1t_ring", [128, 2 * NC16 * TQ], BF16))
    bc_bc = p3_stack.enter_context(nc.sbuf_tensor("bc_bc", [128, D], F32))
    outg_bc = p3_stack.enter_context(nc.sbuf_tensor("outg_bc", [128, D], F32))
    outb_bc = p3_stack.enter_context(nc.sbuf_tensor("outb_bc", [128, D], F32))
    x1l_sb = p3_stack.enter_context(nc.sbuf_tensor("x1l_sb", [128, 2 * D], F32))
    g_sb2 = p3_stack.enter_context(nc.sbuf_tensor("g_sb2", [128, 2 * D], F32))
    o3_sb2 = p3_stack.enter_context(nc.sbuf_tensor("o3_sb2", [128, 2 * D], F32))
    st3 = p3_stack.enter_context(nc.sbuf_tensor("st3", [128, 64], F32))
    gps_l = [p3_stack.enter_context(nc.psum_tensor(f"g{i}_ps", [128, TQ], F32))
             for i in range(4)]
    gb_sb = p3_stack.enter_context(nc.sbuf_tensor("gb_sb", [1, D], BF16))

    for eng in ("sync", "gpsimd", "scalar"):
        P.wait(eng, e_OP, P2_OP)
        P.wait(eng, e_OPE, P2_OPE)
        P.wait(eng, e_AV, P2_AV)
        P.wait(eng, e_OE, P2_OE)
        P.wait(eng, e_SE, P2_SE)
        P.wait(eng, e_SM, e_SM.n)
        P.wait(eng, e_ACv, e_ACv.n)
        P.wait(eng, e_ACg, e_ACg.n)
        P.wait(eng, e_LN, P2_LN)
        P.wait(eng, e_TH, TH_ALL)
        for s_ in d_thr:
            P.wait(eng, s_, s_.n)
        for s_ in d_qt:
            P.wait(eng, s_, s_.n)
        P.wait(eng, d_kv[0], d_kv[0].n)
        P.wait(eng, d_kv[1], d_kv[1].n)
        P.wait(eng, d_x, d_x.n)
        P.wait(eng, d_x1F, 16 * min(NSUB, 4))
        P.wait(eng, d_x1R, 16 * max(0, NSUB - 4))
        P.wait(eng, d_x1b, 16 * NSUB)

    ld(outg_bc[:], outg_h.ap().unsqueeze(0).partition_broadcast(128), sem=d_csD)
    ld(outb_bc[:], outb_h.ap().unsqueeze(0).partition_broadcast(128), sem=d_csD)
    P3_CS = d_csD.n
    P.wait("sync", d_bcr, 16)
    d_bcb.n += 16
    hbc2 = d_bcb.h
    P.emit("sync", lambda e: e.dma_start(
        bc_bc[:], bcrow_h.ap().unsqueeze(0).partition_broadcast(128)
    ).then_inc(hbc2, 16))
    BC_READY = d_bcb.n

    ld(gb_sb[:], gb_h.ap(), sem=d_bias, q="gpsimd")
    GB_LD = d_bias.n
    P.wait("tensor", d_bias, GB_LD)
    gw_src = gw_h.ap().rearrange("(c p) f -> p c f", p=128)
    gw_dst = gw_sb.ap().rearrange("p (c f) -> p c f", f=D)
    for c in range(NC16):
        d_gw.n += 16
        h = d_gw.h
        P.emit("gpsimd", lambda e, c=c, h=h: e.dma_start(
            gw_dst[:, c, :], gw_src[:, c, :]).then_inc(h, 16))

    def x1t_slot(t, c):
        off = ((t % 2) * NC16 + c) * TQ
        return x1t_ring[:, off : off + TQ]

    F_PER_SUB = 9
    pending = []

    def flush_pending(k=None):
        n = len(pending) if k is None else min(k, len(pending))
        for _ in range(n):
            eng, waits, fns = pending.pop(0)
            for sem_, val_ in waits:
                P.wait(eng, sem_, val_)
            for fn in fns:
                P.emit(eng, fn)

    def build_finals(g):
        par = g % 2
        PB = 32 * par
        f_ = e_Fp[par]
        hF = f_.h
        base = f_.n
        gv = g_sb2[:, par * D : par * D + D]
        ov = o3_sb2[:, par * D : par * D + D]
        x1v = x1l_sb[:, par * D : par * D + D]
        bn = st3[:, PB : PB + 24]
        mu_c = st3[:, PB + 24 : PB + 25]
        var_c = st3[:, PB + 25 : PB + 26]
        mv_c = st3[:, PB + 24 : PB + 26]
        rstd_c = st3[:, PB + 26 : PB + 27]
        nmr_c = st3[:, PB + 27 : PB + 28]

        def bn_block(e, src):
            for q in range(4):
                e.bn_stats(bn[:, 6 * q : 6 * q + 6],
                           src[:, 512 * q : 512 * q + 512])
            e.drain()
            e.bn_aggr(mv_c, bn)

        def _b1(e):
            e.scalar_tensor_tensor(ov, gv, 1.0, x1v, ALU.mult, ALU.add)
            bn_block(e, ov)
            e.drain().then_inc(hF, 1)
        pending.append(("vector",
                        [(e_GE, e_GE.n), (d_csD, P3_CS),
                         (d_x1l[par], 16 * (g // 2 + 1)), (d_out, 16 * (g - 1))],
                        [_b1]))

        def _b2(e):
            e.activation(rstd_c, var_c, AF.Ln, bias=cEPS6, scale=1.0)
            e.drain()
            e.activation(rstd_c, rstd_c, AF.Exp, bias=0.0, scale=cNH)
            e.drain().then_inc(hF, 1)
        pending.append(("scalar", [(f_, base + 1)], [_b2]))

        def _b3(e):
            e.scalar_tensor_tensor(nmr_c, mu_c, C_M1, rstd_c, ALU.mult, ALU.mult)
            e.drain().then_inc(hF, 1)
        pending.append(("vector", [(f_, base + 2)], [_b3]))

        def _b4(e):
            e.activation(gv, ov, AF.Identity, bias=nmr_c,
                         scale=rstd_c).then_inc(hF, 1)
        pending.append(("scalar", [(f_, base + 3)], [_b4]))

        def _b5a(e):
            e.tensor_tensor(gv, gv, outg_bc[:], ALU.mult)
            e.tensor_tensor(gv, gv, outb_bc[:], ALU.add)
            e.drain().then_inc(hF, 1)
        pending.append(("vector", [(f_, base + 4), (d_csD, P3_CS)], [_b5a]))

        def _b5b(e):
            e.scalar_tensor_tensor(ov, gv, 1.0, x1v, ALU.mult, ALU.add)
            bn_block(e, ov)
            e.drain().then_inc(hF, 1)
        pending.append(("vector", [(f_, base + 5)], [_b5b]))

        def _b6(e):
            e.activation(rstd_c, var_c, AF.Ln, bias=cEPS5, scale=1.0)
            e.drain()
            e.activation(rstd_c, rstd_c, AF.Exp, bias=0.0, scale=cNH)
            e.drain().then_inc(hF, 1)
        pending.append(("scalar", [(f_, base + 6)], [_b6]))

        def _b7(e):
            e.scalar_tensor_tensor(nmr_c, mu_c, C_M1, rstd_c, ALU.mult, ALU.mult)
            e.drain().then_inc(hF, 1)
        pending.append(("vector", [(f_, base + 7)], [_b7]))

        def _b8(e):
            e.activation(ov, ov, AF.Identity, bias=nmr_c,
                         scale=rstd_c).then_inc(hF, 1)
        pending.append(("scalar", [(f_, base + 8)], [_b8]))

        d_out.n += 16
        hout = d_out.h

        def _b9(e):
            e.dma_start(out_h.ap()[g * 128 : (g + 1) * 128, :],
                        ov).then_inc(hout, 16)
        pending.append(("scalar", [(f_, base + 9)], [_b9]))
        f_.n += F_PER_SUB

    for t in range(NT):
        P.wait("sync", d_x1b, 16 * NSUB)
        if t >= 2:
            P.wait("sync", e_G, (t - 1) * 16)
        for c in range(NC16):
            d_x1t[t % 2].n += 16
            h = d_x1t[t % 2].h
            P.emit("sync", lambda e, t=t, c=c, h=h: e.dma_start(
                x1t_slot(t, c),
                x1b_h.ap()[t * TQ : (t + 1) * TQ, c * 128 : (c + 1) * 128],
                transpose=True).then_inc(h, 16))

        for s in range(4):
            g = t * 4 + s
            par_x = g % 2
            if g == 0:
                d_x1l[0].n += 16
                hxl = d_x1l[0].h
                P.emit("scalar", lambda e, hxl=hxl: e.dma_start(
                    x1l_sb[:, 0:D],
                    x1_h.ap()[0:128, :]).then_inc(hxl, 16))
            for oc in range(4):
                qtr = g * 4 + oc
                gps = gps_l[qtr % 4]
                P.wait("tensor", d_x1t[t % 2], 16 * NC16 * (t // 2 + 1))
                P.wait("tensor", d_gw, 16 * NC16)
                if qtr >= 4:
                    P.wait("tensor", e_GE, 2 * (qtr - 4) + 1)
                for dc in range(NC16):
                    P.emit("tensor", lambda e, dc=dc, s=s, oc=oc, t=t, gps=gps:
                           e.matmul(gps.ap()[:],
                                    x1t_slot(t, dc)[:, s * 128 : (s + 1) * 128],
                                    gw_sb[:, dc * D + oc * 512 : dc * D + (oc + 1) * 512],
                                    start=(dc == 0), stop=False))
                e_G.n += 1
                hG = e_G.h
                P.emit("tensor", lambda e, oc=oc, gps=gps: e.matmul(
                    gps.ap()[:], ones1[:], gb_sb[:, oc * 512 : (oc + 1) * 512],
                    start=False, stop=True).then_inc(hG, 1))
                P.wait("scalar", e_G, e_G.n)
                if g >= 2:
                    P.wait("scalar", e_Fp[g % 2], F_PER_SUB * (g // 2 - 1) + 6)
                gv = g_sb2[:, par_x * D : par_x * D + D]
                e_GE.n += 1
                hGE = e_GE.h
                P.emit("scalar", lambda e, oc=oc, gps=gps, gv=gv, hGE=hGE:
                       e.activation(
                    gv[:, oc * 512 : (oc + 1) * 512], gps.ap()[:],
                    AF.Sigmoid, bias=0.0, scale=1.0).then_inc(hGE, 1))
                P.wait("vector", d_bcb, BC_READY)
                P.wait("vector", e_GE, e_GE.n)
                e_GE.n += 1

                def _bcq(e, oc=oc, gv=gv, hGE=hGE):
                    e.tensor_tensor(gv[:, oc * 512 : (oc + 1) * 512],
                                    gv[:, oc * 512 : (oc + 1) * 512],
                                    bc_bc[:, oc * 512 : (oc + 1) * 512],
                                    ALU.mult)
                    e.drain().then_inc(hGE, 1)
                P.emit("vector", _bcq)
                flush_pending(3 if oc < 3 else 1)
            gl = g + 1
            if gl < NSUB:
                if gl >= 2:
                    P.wait("scalar", e_Fp[gl % 2], F_PER_SUB * (gl // 2 - 1) + 6)
                d_x1l[gl % 2].n += 16
                hxl = d_x1l[gl % 2].h
                P.emit("scalar", lambda e, gl=gl, par=gl % 2, hxl=hxl: e.dma_start(
                    x1l_sb[:, par * D : par * D + D],
                    x1_h.ap()[gl * 128 : (gl + 1) * 128, :]).then_inc(hxl, 16))
            build_finals(g)

    flush_pending()
    P.wait("scalar", d_out, 16 * NSUB)
    p3_stack.close()

    with nc.Block() as block:
        @block.tensor
        def _(e):
            for fn in P.q["tensor"]:
                fn(e)

        @block.vector
        def _(e):
            for fn in P.q["vector"]:
                fn(e)

        @block.scalar
        def _(e):
            for fn in P.q["scalar"]:
                fn(e)

        @block.sync
        def _(e):
            for fn in P.q["sync"]:
                fn(e)

        @block.gpsimd
        def _(e):
            for fn in P.q["gpsimd"]:
                fn(e)

    return nc


def _perm_for_heads(nheads):
    perm = []
    for h in range(nheads):
        base = h * HD
        perm.extend(base + 2 * np.arange(64))
        perm.extend(base + 2 * np.arange(64) + 1)
    return np.array(perm)


def _colform(v):
    return np.ascontiguousarray(np.asarray(v, np.float32).reshape(NC16, 128).T)


def pack_inputs(inputs, T):
    f32 = lambda a: np.ascontiguousarray(np.asarray(a, np.float32))
    x = f32(inputs["L0_attn_x"])
    B = x.shape[0]
    wq = f32(inputs["L0_attn_Wq"])
    bq = f32(inputs["L0_attn_bq"])
    wk = f32(inputs["L0_attn_Wk"])
    bk = f32(inputs["L0_attn_bk"])
    wv = f32(inputs["L0_attn_Wv"])
    bv = f32(inputs["L0_attn_bv"])
    qperm = _perm_for_heads(NH)
    kperm = _perm_for_heads(NKV)
    wqkv = np.ascontiguousarray(
        np.concatenate([wq[:, qperm], wk[:, kperm], wv], axis=1))
    bqkv = np.ascontiguousarray(
        np.concatenate([bq[qperm], bk[kperm], bv])[None, :])

    d1 = f32(inputs["L0_ffn_d1_w"])
    d1r = np.ascontiguousarray(
        d1.reshape(NC16, 128, FF // 128, 128).transpose(2, 1, 0, 3).reshape(
            FF // 128, 128, D))

    j = np.arange(0, D, 2, dtype=np.float32) / np.float32(D)
    freqs = (np.float32(10000.0) ** (-j))[:64]
    t = np.arange(T, dtype=np.float32)
    ang = np.outer(t, freqs).astype(np.float32)

    bf = ml_dtypes.bfloat16
    common = {
        "wqkv": wqkv,
        "bqkv": bqkv,
        "wo": f32(inputs["L0_attn_Wo"]),
        "bo": f32(inputs["L0_attn_bo"]).reshape(1, D),
        "gate_w": f32(inputs["L0_ffn_gate_w"]),
        "gate_b": f32(inputs["L0_ffn_gate_b"]).reshape(1, D),
        "agg_w": f32(inputs["L0_ffn_agg_w"]),
        "bc_w": f32(inputs["L0_ffn_bc_w"]),
        "agg_bT": _colform(inputs["L0_ffn_agg_b"]),
        "d2_bT": _colform(inputs["L0_ffn_d2_b"]),
        "bc_bT": _colform(inputs["L0_ffn_bc_b"]),
        "in_gT": _colform(inputs["L0_ffn_in_gamma"]),
        "in_bT": _colform(inputs["L0_ffn_in_beta"]),
        "pn_gT": _colform(inputs["L0_ffn_pn_gamma"]),
        "pn_bT": _colform(inputs["L0_ffn_pn_beta"]),
        "out_g": f32(inputs["L0_ffn_out_gamma"]),
        "out_b": f32(inputs["L0_ffn_out_beta"]),
        "cos_t": np.cos(ang).astype(np.float32),
        "sin_t": np.sin(ang).astype(np.float32),
        "trimask": np.triu(np.ones((128, 128), np.float32)).astype(bf),
        "ones1": np.ones((1, 128), np.float32).astype(bf),
        "ones2d": np.ones((128, 128), np.float32).astype(bf),
        "ones2df": np.ones((128, 128), np.float32),
        "onescol": np.ones((128, 1), np.float32),
        "onesrow": np.ones((1, 128), np.float32),
        "ident_bf": np.eye(128, dtype=np.float32).astype(bf),
        "ident_f32": np.eye(128, dtype=np.float32),
        "consts": np.array(REG_CONSTS, np.float32),
    }
    d2 = f32(inputs["L0_ffn_d2_w"])
    d1bT = np.ascontiguousarray(
        f32(inputs["L0_ffn_d1_b"]).reshape(FF // 128, 128).T)
    in_maps = []
    for b in range(B):
        m = dict(common)
        m["x"] = np.ascontiguousarray(x[b])
        m["d1_slice"] = np.ascontiguousarray(d1r[8 * b : 8 * b + 8])
        m["d2_slice"] = np.ascontiguousarray(d2[1024 * b : 1024 * (b + 1), :])
        m["d1_bT"] = np.ascontiguousarray(d1bT[:, 8 * b : 8 * b + 8])
        sel = np.zeros(8, np.float32)
        sel[b] = 1.0
        m["bsel"] = sel
        in_maps.append(m)
    return in_maps


def kernel(**inputs):
    x = np.asarray(inputs["L0_attn_x"])
    B, T, _ = x.shape
    in_maps = pack_inputs(inputs, T)
    nc = build(T)
    res = run_bass_kernel_spmd(nc, in_maps, core_ids=list(range(B)))
    out = np.stack([np.asarray(res.results[i]["out"], np.float32)
                    for i in range(B)], axis=0)
    return out
